# version 03
# baseline (speedup 1.0000x reference)
import numpy as np
import ml_dtypes
import concourse.bass as bass
import concourse.mybir as mybir
from concourse.bass import IndirectOffsetOnAxis
from concourse.bass_utils import run_bass_kernel_spmd
from contextlib import ExitStack

F32 = mybir.dt.float32
BF16 = mybir.dt.bfloat16
FP8 = mybir.dt.float8e4
U32 = mybir.dt.uint32
U8 = mybir.dt.uint8
AX = mybir.AxisListType.X
AF = mybir.ActivationFunctionType
ALU = mybir.AluOpType

BF = ml_dtypes.bfloat16
NEG = -1.0e30

D = 1024
S = 4096
TPC = 2048


def _ap_name(a):
    t = getattr(a, "tensor", None)
    if t is not None:
        return getattr(t, "name", None)
    return getattr(a, "name", None)


class EngProxy:

    _OUT1 = {"tensor_tensor", "tensor_scalar", "tensor_copy", "reduce_max",
             "reduce_sum", "tensor_reduce", "reciprocal", "copy", "select",
             "activation", "scalar_tensor_tensor", "iota", "memset",
             "affine_select", "transpose", "partition_broadcast"}

    def __init__(self, eng, dirty):
        self._e = eng
        self._d = dirty

    def __getattr__(self, name):
        fn = getattr(self._e, name)
        if name in ("wait_ge", "nop", "engine_nop", "dma_start",
                    "indirect_dma_start", "dma_start_transpose", "matmul",
                    "then_inc", "sem_inc", "collective_compute"):
            return fn
        if name == "drain":
            def drain():
                self._d[0].clear()
                self._d[1].clear()
                return self._e.drain()
            return drain
        if name not in self._OUT1:
            return fn

        def wrapped(*args, **kwargs):
            vals = list(args) + list(kwargs.values())
            outs = []
            if args:
                outs.append(args[0])
            if "out" in kwargs:
                outs.append(kwargs["out"])
            if "accum_out" in kwargs and kwargs["accum_out"] is not None:
                outs.append(kwargs["accum_out"])
            out_ids = {id(v) for v in outs}
            ins = [v for v in vals if v is not None and id(v) not in out_ids]
            names_in = {n for n in (_ap_name(v) for v in ins) if n}
            names_out = {n for n in (_ap_name(v) for v in outs) if n}
            wr, rd = self._d
            if (names_in & wr) or (names_out & wr) or (names_out & rd):
                self._e.drain()
                wr.clear()
                rd.clear()
            r = fn(*args, **kwargs)
            wr |= names_out
            rd |= names_in
            return r
        return wrapped


class Ctx:

    def __init__(self, nc, ctx):
        self.nc = nc
        self.ctx = ctx
        self.sems = {}
        self.cnt = {}
        self.dirty = {}

    def sem(self, name):
        if name not in self.sems:
            self.sems[name] = self.ctx.enter_context(self.nc.semaphore(name))
            self.cnt[name] = 0
        return self.sems[name]

    def inc(self, instr, name, amt=1):
        instr.then_inc(self.sem(name), amt)
        self.cnt[name] += amt
        return self.cnt[name]

    def n(self, name):
        return self.cnt.get(name, 0)

    def sb(self, name, shape, dtype=F32):
        return self.ctx.enter_context(self.nc.sbuf_tensor(name, shape, dtype))

    def ps(self, name, shape, dtype=F32):
        return self.ctx.enter_context(self.nc.psum_tensor(name, shape, dtype))

    def wrap(self, eng, kind):
        return EngProxy(eng, self.dirty.setdefault(kind, [set(), set()]))


def emit_ln_yT(C, block, x_dram, n_tiles, yT, ptps, y_tok=None, x_sb=None,
               prev_oT=None, name="ln"):
    ntok = n_tiles * 128
    xa = C.sb(name + "_xa", [128, D])
    xb2 = C.sb(name + "_xb", [128, D])
    mean = C.sb(name + "_m", [128, 1])
    var = C.sb(name + "_v", [128, 1])
    sdv = C.sb(name + "_sd", [128, 1])
    inv = C.sb(name + "_i", [128, 1])
    eps = C.sb(name + "_eps", [128, 1])
    sq = C.sb(name + "_sq", [128, 512])
    var2 = C.sb(name + "_v2", [128, 1])
    yb0 = C.sb(name + "_yb0", [128, D], BF16)
    yb1 = C.sb(name + "_yb1", [128, D], BF16)

    @block.sync
    def _(sync):
        for t in range(n_tiles):
            buf = xa if t % 2 == 0 else xb2
            if t >= 2:
                sync.wait_ge(C.sem(name + "_y"), t - 1)
            C.inc(sync.dma_start(out=buf[:], in_=x_dram[t * 128:(t + 1) * 128, :]),
                  name + "_ld%d" % (t % 2), 16)

    @block.vector
    def _(vector):
        vector = C.wrap(vector, 'v')
        C.inc(vector.memset(eps[:], 1.0e-5), name + "_eps", 1)
        for t in range(n_tiles):
            buf = xa if t % 2 == 0 else xb2
            yb = yb0 if t % 2 == 0 else yb1
            vector.wait_ge(C.sem(name + "_ld%d" % (t % 2)), 16 * (t // 2 + 1))
            if x_sb is not None:
                vector.tensor_copy(x_sb[:, t * D:(t + 1) * D], buf[:])
            vector.reduce_sum(mean[:], buf[:], axis=AX)
            vector.drain()
            vector.tensor_scalar(mean[:], mean[:], 1.0 / D, None, ALU.mult)
            C.inc(vector.tensor_scalar(buf[:], buf[:], mean[:], None, ALU.subtract),
                  name + "_xc", 1)
            vector.wait_ge(C.sem(name + "_sv"), t + 1)
            vector.reciprocal(inv[:], sdv[:])
            vector.drain()
            if t >= 2:
                vector.wait_ge(C.sem(name + "_tp"), 2 * (t - 1))
            C.inc(vector.tensor_scalar(yb[:], buf[:], inv[:], None, ALU.mult),
                  name + "_y", 1)
            if y_tok is not None:
                vector.tensor_copy(y_tok[:, t * D:(t + 1) * D], yb[:])

    def _copy_groups(scalar, t):
        for g in range(2):
            gi = 2 * t + g
            scalar.wait_ge(C.sem(name + "_tp"), gi + 1)
            ptp = ptps[gi % 2]
            for f4 in range(4):
                f = g * 4 + f4
                scalar.copy(
                    yT[:, f * ntok + t * 128: f * ntok + (t + 1) * 128],
                    ptp[:, f4 * 128: f4 * 128 + 128])
            C.inc(scalar.drain(), name + "_tpc", 1)

    @block.scalar
    def _(scalar):
        scalar = C.wrap(scalar, 's')
        scalar.wait_ge(C.sem(name + "_eps"), 1)
        for t in range(n_tiles):
            scalar.wait_ge(C.sem(name + "_xc"), t + 1)
            buf = xa if t % 2 == 0 else xb2
            scalar.activation(sq[:], buf[:, 0:512], AF.Square, accum_out=var[:])
            scalar.activation(sq[:], buf[:, 512:1024], AF.Square,
                              accum_out=var2[:])
            scalar.drain()
            scalar.activation(var[:], var[:], AF.Relu, bias=var2[:])
            C.inc(scalar.activation(sdv[:], var[:], AF.Sqrt, scale=1.0 / D,
                                    bias=eps[:]), name + "_sv", 1)
            if t >= 1:
                _copy_groups(scalar, t - 1)
        _copy_groups(scalar, n_tiles - 1)

    @block.tensor
    def _(tensor):
        for t in range(n_tiles):
            tensor.wait_ge(C.sem(name + "_y"), t + 1)
            yb = yb0 if t % 2 == 0 else yb1
            for g in range(2):
                gi = 2 * t + g
                if gi >= 2:
                    tensor.wait_ge(C.sem(name + "_tpc"), gi - 1)
                ptp = ptps[gi % 2]
                for f4 in range(4):
                    f = g * 4 + f4
                    tensor.transpose(
                        ptp[:, f4 * 128: f4 * 128 + 128],
                        yb[:, f * 128:(f + 1) * 128], C.ident[:])
                C.inc(tensor.drain(), name + "_tp", 1)




def build_A0(npairs=4, nheads=2, ln_tiles=32, do_w=True):
    nc = bass.Bass("TRN2", target_bir_lowering=False)
    xb = nc.dram_tensor("xb", [S, D], F32, kind="ExternalInput").ap()
    wk = nc.dram_tensor("wk", [D, 512], F32, kind="ExternalInput").ap()
    wq = nc.dram_tensor("wq", [D, 512], F32, kind="ExternalInput").ap()
    wv = nc.dram_tensor("wv", [D, 512], F32, kind="ExternalInput").ap()
    mh = nc.dram_tensor("mh", [2, 4], F32, kind="ExternalInput").ap()
    oT = nc.dram_tensor("oT", [512, S], F32, kind="ExternalOutput").ap()

    ident_c = nc.inline_tensor(np.eye(128, dtype=BF), "ident_c")
    iota_c = nc.inline_tensor(
        np.broadcast_to(np.arange(512, dtype=np.float32), (2, 512)).copy(),
        "iota_c")
    mvs = []
    for v in range(4):
        r = np.arange(128)[:, None]
        f = np.arange(512)[None, :]
        mvs.append(np.where(f < 128 * v + r, NEG, 0.0).astype(np.float32))
    mask_c = nc.inline_tensor(np.concatenate(mvs, axis=1).astype(BF), "mask_c")
    ones64_c = nc.inline_tensor(np.ones((1, 64), dtype=np.float32), "ones64_c")
    bones_c = nc.inline_tensor(
        np.concatenate([np.concatenate([np.ones((64, 1)), np.zeros((64, 1))], 1),
                        np.concatenate([np.zeros((64, 1)), np.ones((64, 1))], 1)],
                       0).astype(np.float32), "bones_c")

    NQC = S // 512

    with ExitStack() as ctx:
        C = Ctx(nc, ctx)
        C.ident = C.sb("ident", [128, 128], BF16)
        rowsb = C.sb("rowsb", [128, S], BF16)
        rowsb2 = C.sb("rowsb2", [32, S], BF16)
        iota512 = C.sb("iota512", [2, 512])
        ach = C.sb("ach", [2, 512])
        bch = C.sb("bch", [2, 512])
        cch = C.sb("cch", [2, 512])
        masks = C.sb("masks", [128, 2048], BF16)
        ones64 = C.sb("ones64", [1, 64])
        bones = C.sb("bones", [128, 2])
        smh = C.sb("smh", [2, 4])
        negma = C.sb("negma", [2, 4])
        yT = C.sb("yT", [128, 8 * S], BF16)

        block = ctx.enter_context(nc.Block())

        @block.sync
        def _(sync):
            C.inc(sync.dma_start(out=C.ident[:], in_=ident_c.ap()), "cld", 16)
            C.inc(sync.dma_start(out=iota512[:], in_=iota_c.ap()), "cld", 16)
            C.inc(sync.dma_start(out=masks[:], in_=mask_c.ap()), "cld", 16)
            C.inc(sync.dma_start(out=ones64[:], in_=ones64_c.ap()), "cld", 16)
            C.inc(sync.dma_start(out=bones[:], in_=bones_c.ap()), "cld", 16)
            C.inc(sync.dma_start(out=smh[:], in_=mh), "cld", 16)

        @block.tensor
        def _(tensor):
            tensor.wait_ge(C.sem("cld"), C.cnt["cld"])

        @block.vector
        def _(vector):
            vector = C.wrap(vector, 'v')
            vector.wait_ge(C.sem("cld"), C.cnt["cld"])

        @block.scalar
        def _(scalar):
            scalar = C.wrap(scalar, 's')
            scalar.wait_ge(C.sem("cld"), C.cnt["cld"])
            C.inc(scalar.activation(negma[:], smh[:], AF.Relu, scale=-1.0),
                  "negm", 1)

        ptps = [C.ps("ptpA", [128, 512], BF16), C.ps("ptpB", [128, 512], BF16)]
        emit_ln_yT(C, block, xb, ln_tiles, yT, ptps, name="ln")

        wkb = C.sb("wkb", [128, 8 * 512], BF16)
        wqb = C.sb("wqb", [128, 8 * 512], BF16)
        wvb = C.sb("wvb", [128, 8 * 512], BF16)

        @block.gpsimd
        def _(gpsimd):
            gpsimd = C.wrap(gpsimd, 'g')
            if not do_w:
                return
            for (w_d, w_s) in ((wk, wkb), (wq, wqb), (wv, wvb)):
                for c in range(8):
                    C.inc(gpsimd.dma_start(out=w_s[:, c * 512:(c + 1) * 512],
                                           in_=w_d[c * 128:(c + 1) * 128, :]),
                          "wld", 16)
        WLD = C.n("wld")

        kaug = [C.sb("kaug0", [69, S], BF16), C.sb("kaug1", [69, S], BF16)]
        qaug = [C.sb("qaug0", [69, S], BF16), C.sb("qaug1", [69, S], BF16)]
        vtok = [C.sb("vtok0", [128, 32 * 65], BF16),
                C.sb("vtok1", [128, 32 * 65], BF16)]
        rhib1 = rowsb[0:2, :]
        rhib2 = rowsb[32:34, :]
        mkhi = rowsb[64:66, :]
        mkmid = rowsb[96:98, :]
        mklo = rowsb2[0:2, :]
        negm = C.sb("negm", [2, 1])
        kmax = C.sb("kmax", [2, 1])
        sqv = C.sb("sqv", [128, 512])
        pT = [C.sb("pT0", [128, 512], BF16), C.sb("pT1", [128, 512], BF16)]
        stmp = [C.sb("stmp0", [128, 512]), C.sb("stmp1", [128, 512])]
        obuf = C.sb("obuf", [64, 512])
        pbcs = C.sb("pbcs", [64, 512])
        rrow = cch[0:1, :]

        pproj = [C.ps("pproj0", [128, 512]), C.ps("pproj1", [128, 512])]
        pshared = C.ps("pshared", [64, 512])
        pnrm = pshared[0:2, :]
        pbc = pshared[0:64, :]
        ps2 = [C.ps("ps0", [128, 512]), C.ps("ps1", [128, 512])]
        ppv = C.ps("ppv", [65, 512])

        for hp in range(npairs):
            PJ0 = C.n("pj"); PJC0 = C.n("pjc"); SQ0 = C.n("sq")
            NRM0 = C.n("nrm")

            @block.tensor
            def _(tensor, hp=hp, PJC0=PJC0, SQ0=SQ0, ON_PREV=C.n("on")):
                tensor.wait_ge(C.sem("wld"), WLD)
                tensor.wait_ge(C.sem("ln_tpc"), 64)
                tensor.wait_ge(C.sem("on"), ON_PREV)

                def nrm_mm(jn):
                    tensor.wait_ge(C.sem("sq"), SQ0 + jn + 1)
                    if jn >= 1:
                        tensor.wait_ge(C.sem("pjc"), PJC0 + jn)
                    C.inc(tensor.matmul(pnrm, bones[:], sqv[:], start=True,
                                        stop=True), "nrm", 1)
                j = 0
                for which, wsb in (("k", wkb), ("q", wqb)):
                    for tc in range(NQC):
                        if j >= 2:
                            tensor.wait_ge(C.sem("pjc"), PJC0 + j - 1)
                            tensor.wait_ge(C.sem("sq"), SQ0 + j - 1)
                        pp = pproj[j % 2]
                        for ch in range(8):
                            ins = tensor.matmul(
                                pp[:],
                                wsb[:, ch * 512 + hp * 128: ch * 512 + hp * 128 + 128],
                                yT[:, ch * S + tc * 512: ch * S + (tc + 1) * 512],
                                start=(ch == 0), stop=(ch == 7))
                        C.inc(ins, "pj", 1)
                        j += 1
                        if j >= 2 and j - 2 < 2 * NQC:
                            nrm_mm(j - 2)
                for tt in range(32):
                    if j >= 2:
                        tensor.wait_ge(C.sem("pjc"), PJC0 + j - 1)
                        tensor.wait_ge(C.sem("sq"), SQ0 + min(j - 1, 2 * NQC))
                    pp = pproj[j % 2]
                    for ch in range(8):
                        ins = tensor.matmul(
                            pp[:, 0:128],
                            yT[:, ch * S + tt * 128: ch * S + (tt + 1) * 128],
                            wvb[:, ch * 512 + hp * 128: ch * 512 + hp * 128 + 128],
                            start=(ch == 0), stop=(ch == 7))
                    C.inc(ins, "pj", 1)
                    j += 1
                    if j - 2 < 2 * NQC:
                        nrm_mm(j - 2)

            UC0 = C.n("uc"); US0 = C.n("us")

            @block.scalar
            def _(scalar, PJ0=PJ0, NRM0=NRM0, UC0=UC0):
                scalar = C.wrap(scalar, 's')
                def square(j):
                    scalar.wait_ge(C.sem("pj"), PJ0 + j + 1)
                    if j >= 1:
                        scalar.wait_ge(C.sem("nrm"), NRM0 + j)
                    pp = pproj[j % 2]
                    C.inc(scalar.activation(sqv[:], pp[:], AF.Square), "sq", 1)
                for j in range(NQC + 1):
                    square(j)
                for tc in range(NQC):
                    scalar.wait_ge(C.sem("uc"), UC0 + tc + 1)
                    C.inc(scalar.activation(ach[:], ach[:], AF.Sqrt,
                                            scale=1.0 / 1024.0), "us", 1)
                    if NQC + 1 + tc < 2 * NQC:
                        square(NQC + 1 + tc)


            @block.vector
            def _(vector, hp=hp, PJ0=PJ0, NRM0=NRM0, US0=US0):
                vector = C.wrap(vector, 'v')
                mh2 = smh[:, hp:hp + 1]
                nm2 = negma[:, hp:hp + 1]
                vector.wait_ge(C.sem("negm"), 1)
                vector.memset(kmax[:], 0.0)
                j = 0
                for wi, which in enumerate(("k", "q")):
                    dst = kaug if which == "k" else qaug
                    for tc in range(NQC):
                        vector.wait_ge(C.sem("pj"), PJ0 + j + 1)
                        pp = pproj[j % 2]
                        if which == "q":
                            vector.tensor_scalar(
                                dst[0][0:64, tc * 512:(tc + 1) * 512],
                                pp[0:64, :], 1.0 / 32.0, None, ALU.mult)
                            vector.tensor_scalar(
                                dst[1][0:64, tc * 512:(tc + 1) * 512],
                                pp[64:128, :], 1.0 / 32.0, None, ALU.mult)
                        else:
                            vector.tensor_copy(dst[0][0:64, tc * 512:(tc + 1) * 512],
                                               pp[0:64, :])
                            vector.tensor_copy(dst[1][0:64, tc * 512:(tc + 1) * 512],
                                               pp[64:128, :])
                        vector.wait_ge(C.sem("nrm"), NRM0 + j + 1)
                        if which == "k":
                            vector.reduce_max(bch[:, 0:1], pnrm, axis=AX)
                            vector.drain()
                            ins = vector.tensor_tensor(kmax[:], kmax[:],
                                                       bch[:, 0:1], ALU.max)
                        else:
                            sl = slice(tc * 512, (tc + 1) * 512)
                            vector.tensor_copy(ach[:], pnrm)
                            C.inc(vector.tensor_scalar(ach[:], ach[:], kmax[:],
                                                       None, ALU.mult), "uc", 1)
                            vector.tensor_scalar(bch[:], iota512[:],
                                                 float(512 * tc), None, ALU.add)
                            vector.wait_ge(C.sem("us"), US0 + tc + 1)
                            vector.tensor_scalar(cch[:], bch[:], nm2, None,
                                                 ALU.mult)
                            vector.tensor_tensor(ach[:], ach[:], cch[:], ALU.add)
                            vector.tensor_scalar(cch[:], bch[:], mh2, None,
                                                 ALU.mult)
                            vector.tensor_tensor(ach[:], ach[:], cch[:], ALU.add)
                            vector.tensor_scalar(ach[:], ach[:], -1.0, None,
                                                 ALU.mult)
                            vector.tensor_copy(rhib1[:, sl], ach[:])
                            vector.tensor_copy(cch[:], rhib1[:, sl])
                            vector.tensor_tensor(ach[:], ach[:], cch[:],
                                                 ALU.subtract)
                            vector.tensor_copy(rhib2[:, sl], ach[:])
                            vector.tensor_scalar(cch[:], bch[:], mh2, None,
                                                 ALU.mult)
                            vector.tensor_copy(mkhi[:, sl], cch[:])
                            vector.tensor_copy(ach[:], mkhi[:, sl])
                            vector.tensor_tensor(cch[:], cch[:], ach[:],
                                                 ALU.subtract)
                            vector.tensor_copy(mkmid[:, sl], cch[:])
                            vector.tensor_copy(ach[:], mkmid[:, sl])
                            vector.tensor_tensor(cch[:], cch[:], ach[:],
                                                 ALU.subtract)
                            ins = vector.tensor_copy(mklo[:, sl], cch[:])
                        C.inc(ins, "pjc", 1)
                        j += 1
                for tt in range(32):
                    vector.wait_ge(C.sem("pj"), PJ0 + j + 1)
                    pp = pproj[j % 2]
                    vector.tensor_copy(vtok[0][:, tt * 65: tt * 65 + 64],
                                       pp[:, 0:64])
                    vector.tensor_copy(vtok[1][:, tt * 65: tt * 65 + 64],
                                       pp[:, 64:128])
                    C.inc(vector.drain(), "pjc", 1)
                    j += 1
                vector.memset(vtok[0][:, 64::65], 1.0)
                vector.memset(vtok[1][:, 64::65], 1.0)
                C.inc(vector.drain(), "vdone", 1)

            @block.vector
            def _(vector):
                vector = C.wrap(vector, 'v')
                vector.memset(qaug[0][64:69, :], 1.0)
                vector.memset(qaug[1][64:69, :], 1.0)
                vector.memset(kaug[0][64:69, :], 1.0)
                vector.memset(kaug[1][64:69, :], 1.0)
                C.inc(vector.drain(), "aug", 1)

            @block.gpsimd
            def _(gpsimd, AUG_V=None, PJC_V=None):
                gpsimd = C.wrap(gpsimd, 'g')
                AUG_V = C.n("aug")
                PJC_V = PJC0 + 2 * NQC
                gpsimd.wait_ge(C.sem("aug"), AUG_V)
                gpsimd.wait_ge(C.sem("pjc"), PJC_V)
                for i in range(2):
                    C.inc(gpsimd.dma_start(out=kaug[i][64:65, :],
                                           in_=rowsb[64 + i:65 + i, :]), "augd", 16)
                    C.inc(gpsimd.dma_start(out=kaug[i][65:66, :],
                                           in_=rowsb[96 + i:97 + i, :]), "augd", 16)
                    C.inc(gpsimd.dma_start(out=kaug[i][66:67, :],
                                           in_=rowsb2[0 + i:1 + i, :]), "augd", 16)
                    C.inc(gpsimd.dma_start(out=qaug[i][67:68, :],
                                           in_=rowsb[0 + i:1 + i, :]), "augd", 16)
                    C.inc(gpsimd.dma_start(out=qaug[i][68:69, :],
                                           in_=rowsb[32 + i:33 + i, :]), "augd", 16)

            AUG_T = C.n("augd")
            VD_T = C.n("vdone")

            for hi in range(nheads):
                ka, qa, vt = kaug[hi], qaug[hi], vtok[hi]
                h_abs = hp * 2 + hi
                S0 = C.n("s"); E0 = C.n("exp"); M0 = C.n("msk")
                P0 = C.n("pvd"); R0 = C.n("rec"); B0 = C.n("bc")
                O0 = C.n("on")
                OSp = [C.n("osc0"), C.n("osc1")]

                def g_of(qc, tk):
                    return sum(4 * q + 4 for q in range(qc)) + tk

                @block.tensor
                def _(tensor, ka=ka, qa=qa, vt=vt, g_of=g_of, AUG_T=AUG_T,
                      VD_T=VD_T, E0=E0, R0=R0, O0=O0):
                    tensor.wait_ge(C.sem("augd"), AUG_T)
                    tensor.wait_ge(C.sem("vdone"), VD_T)
                    tensor.wait_ge(C.sem("exp"), E0)
                    tensor.wait_ge(C.sem("on"), O0)
                    for qc in range(NQC):
                        ntk = 4 * qc + 4
                        if qc > 0:
                            tensor.wait_ge(C.sem("on"), O0 + qc)
                        for tk in range(ntk):
                            gi = g_of(qc, tk)
                            if gi >= 2:
                                tensor.wait_ge(C.sem("exp"), E0 + gi - 1)
                            C.inc(tensor.matmul(ps2[gi % 2][:],
                                          ka[:, tk * 128:(tk + 1) * 128],
                                          qa[:, qc * 512:(qc + 1) * 512],
                                          start=True, stop=True), "s", 1)
                            if tk >= 1:
                                tensor.wait_ge(C.sem("exp"), E0 + gi)
                                tensor.matmul(ppv[:],
                                              vt[:, (tk - 1) * 65:(tk - 1) * 65 + 65],
                                              pT[(gi - 1) % 2][:],
                                              start=(tk == 1), stop=False)
                        gl = g_of(qc, ntk - 1)
                        tensor.wait_ge(C.sem("exp"), E0 + gl + 1)
                        C.inc(tensor.matmul(
                            ppv[:], vt[:, (ntk - 1) * 65:(ntk - 1) * 65 + 65],
                            pT[gl % 2][:], start=False, stop=True), "pvd", 1)
                        tensor.wait_ge(C.sem("rec"), R0 + qc + 1)
                        C.inc(tensor.matmul(pbc, ones64[:], rrow, start=True,
                                            stop=True), "bc", 1)

                @block.scalar
                def _(scalar, g_of=g_of, S0=S0, M0=M0):
                    scalar = C.wrap(scalar, 's')
                    for qc in range(NQC):
                        ntk = 4 * qc + 4
                        for tk in range(ntk):
                            gi = g_of(qc, tk)
                            scalar.wait_ge(C.sem("s"), S0 + gi + 1)
                            if tk >= 4 * qc:
                                v = tk - 4 * qc
                                scalar.wait_ge(C.sem("msk"), M0 + 4 * qc + v + 1)
                                C.inc(scalar.activation(pT[gi % 2][:],
                                                        stmp[v % 2][:],
                                                        AF.Exp), "exp", 1)
                            else:
                                C.inc(scalar.activation(pT[gi % 2][:],
                                                        ps2[gi % 2][:], AF.Exp),
                                      "exp", 1)

                @block.vector
                def _(vector, g_of=g_of, S0=S0, E0=E0, P0=P0, B0=B0, OSp=OSp):
                    vector = C.wrap(vector, 'v')
                    vector.wait_ge(C.sem("exp"), E0)
                    vector.wait_ge(C.sem("osc0"), OSp[0])
                    vector.wait_ge(C.sem("osc1"), OSp[1])
                    for qc in range(NQC):
                        ntk = 4 * qc + 4
                        for tk in range(4 * qc, ntk):
                            gi = g_of(qc, tk)
                            v = tk - 4 * qc
                            vector.wait_ge(C.sem("s"), S0 + gi + 1)
                            if v >= 2:
                                vector.wait_ge(C.sem("exp"),
                                               E0 + g_of(qc, 4 * qc + v - 2) + 1)
                            elif qc > 0:
                                vector.wait_ge(C.sem("exp"),
                                               E0 + g_of(qc - 1,
                                                         4 * (qc - 1) + v + 2) + 1)
                            C.inc(vector.tensor_tensor(
                                stmp[v % 2][:], ps2[gi % 2][:],
                                masks[:, v * 512:(v + 1) * 512], ALU.add),
                                "msk", 1)
                        vector.wait_ge(C.sem("pvd"), P0 + qc + 1)
                        vector.reciprocal(rrow, ppv[64:65, :])
                        C.inc(vector.drain(), "rec", 1)
                        vector.wait_ge(C.sem("bc"), B0 + qc + 1)
                        vector.tensor_copy(pbcs[:], pbc)
                        if qc > 0:
                            pq = (qc - 1) % 2
                            vector.wait_ge(C.sem("osc%d" % pq),
                                           OSp[pq] + 16 * ((qc - 1) // 2 + 1))
                        C.inc(vector.tensor_tensor(obuf[:], ppv[0:64, :],
                                                    pbcs[:], ALU.mult), "on", 1)

                @block.sync
                def _(sync, h_abs=h_abs, O0=O0):
                    for qc in range(NQC):
                        sync.wait_ge(C.sem("on"), O0 + qc + 1)
                        C.inc(sync.dma_start(
                            out=oT[h_abs * 64:(h_abs + 1) * 64,
                                   qc * 512:(qc + 1) * 512],
                            in_=obuf[:]), "osc%d" % (qc % 2), 16)

        @block.sync
        def _(sync):
            sync.wait_ge(C.sem("osc0"), C.n("osc0"))
            sync.wait_ge(C.sem("osc1"), C.n("osc1"))

    return nc



_A0_CACHE = {}


def run_A0(x, W_kqv, m):
    if "nc" not in _A0_CACHE:
        _A0_CACHE["nc"] = build_A0()
    nc = _A0_CACHE["nc"]
    in_maps = []
    for c in range(8):
        b, p = c // 2, c % 2
        h0 = p * 8
        in_maps.append(dict(
            xb=np.ascontiguousarray(x[b]),
            wk=np.ascontiguousarray(W_kqv[:, h0 * 64:(h0 + 8) * 64]),
            wq=np.ascontiguousarray(W_kqv[:, 1024 + h0 * 64: 1024 + (h0 + 8) * 64]),
            wv=np.ascontiguousarray(W_kqv[:, 2048 + h0 * 64: 2048 + (h0 + 8) * 64]),
            mh=np.ascontiguousarray(m.reshape(16)[h0:h0 + 8].reshape(4, 2).T),
        ))
    res = run_bass_kernel_spmd(nc, in_maps, core_ids=list(range(8)))
    out = np.zeros((4, S, D), np.float32)
    for c in range(8):
        b, p = c // 2, c % 2
        oT = np.asarray(res.results[c]["oT"])
        out[b][:, p * 512:(p + 1) * 512] = oT.T
    return out






def emit_consts_attn(C, nc, block, extra=()):
    ident_c = nc.inline_tensor(np.eye(128, dtype=BF), "ident_c")
    mvs, m01 = [], []
    for v in range(4):
        r = np.arange(128)[:, None]
        f = np.arange(512)[None, :]
        bad = f < 128 * v + r
        mvs.append(np.where(bad, NEG, 0.0).astype(np.float32))
        m01.append(np.where(bad, 0.0, 1.0).astype(np.float32))
    mask_c = nc.inline_tensor(np.concatenate(mvs, axis=1).astype(BF), "mask_c")
    mask01_c = nc.inline_tensor(np.concatenate(m01, axis=1).astype(BF), "mask01_c")
    ones64_c = nc.inline_tensor(np.ones((1, 64), dtype=np.float32), "ones64_c")
    C.ident = C.sb("ident", [128, 128], BF16)
    masks = C.sb("masks", [128, 2048], BF16)
    masks01 = C.sb("masks01", [128, 2048], BF16)
    ones64 = C.sb("ones64", [1, 64])

    @block.sync
    def _(sync):
        C.inc(sync.dma_start(out=C.ident[:], in_=ident_c.ap()), "cld", 16)
        C.inc(sync.dma_start(out=masks[:], in_=mask_c.ap()), "cld", 16)
        C.inc(sync.dma_start(out=masks01[:], in_=mask01_c.ap()), "cld", 16)
        C.inc(sync.dma_start(out=ones64[:], in_=ones64_c.ap()), "cld", 16)

    @block.tensor
    def _(tensor):
        tensor.wait_ge(C.sem("cld"), C.cnt["cld"])

    @block.vector
    def _(vector):
        vector.wait_ge(C.sem("cld"), C.cnt["cld"])

    @block.scalar
    def _(scalar):
        scalar.wait_ge(C.sem("cld"), C.cnt["cld"])
    return dict(masks=masks, masks01=masks01, ones64=ones64)


def emit_wcast(C, block, pairs, name="wld"):
    @block.gpsimd
    def _(gpsimd):
        gpsimd = C.wrap(gpsimd, 'g')
        for (w_d, w_s, rows, cols) in pairs:
            for c in range(rows // 128):
                C.inc(gpsimd.dma_start(
                    out=w_s[:, c * cols:(c + 1) * cols],
                    in_=w_d[c * 128:(c + 1) * 128, :]), name, 16)
    return C.n(name)


def emit_proj_pair(C, block, hp, yT, wsb_list, dsts, vtok, ntok, pproj,
                   scale_q=None, ctr_chunks=8, src_w=512):
    NQ = ntok // 512
    PJ0 = C.n("pj")
    PJC0 = C.n("pjc")
    wk_sb, wq_sb, wv_sb = wsb_list
    k0, k1, q0, q1 = dsts

    @block.tensor
    def _(tensor):
        j = 0
        for wsb in (wk_sb, wq_sb):
            for tc in range(NQ):
                if j >= 2:
                    tensor.wait_ge(C.sem("pjc"), PJC0 + j - 1)
                pp = pproj[j % 2]
                for ch in range(ctr_chunks):
                    ins = tensor.matmul(
                        pp[:],
                        wsb[:, ch * src_w + hp * 128: ch * src_w + hp * 128 + 128],
                        yT[:, ch * ntok + tc * 512: ch * ntok + (tc + 1) * 512],
                        start=(ch == 0), stop=(ch == ctr_chunks - 1))
                C.inc(ins, "pj", 1)
                j += 1
        for tt in range(ntok // 128):
            if j >= 2:
                tensor.wait_ge(C.sem("pjc"), PJC0 + j - 1)
            pp = pproj[j % 2]
            for ch in range(ctr_chunks):
                ins = tensor.matmul(
                    pp[:, 0:128],
                    yT[:, ch * ntok + tt * 128: ch * ntok + (tt + 1) * 128],
                    wv_sb[:, ch * src_w + hp * 128: ch * src_w + hp * 128 + 128],
                    start=(ch == 0), stop=(ch == ctr_chunks - 1))
            C.inc(ins, "pj", 1)
            j += 1

    vst = 65 if vtok[0] is not None and C.vones else 64

    @block.vector
    def _(vector):
        vector = C.wrap(vector, 'v')
        j = 0
        for wi, dst in enumerate(((k0, k1), (q0, q1))):
            for tc in range(NQ):
                vector.wait_ge(C.sem("pj"), PJ0 + j + 1)
                pp = pproj[j % 2]
                sl = slice(tc * 512, (tc + 1) * 512)
                if wi == 1 and scale_q is not None:
                    vector.tensor_scalar(dst[0][0:64, sl], pp[0:64, :],
                                         scale_q, None, ALU.mult)
                    vector.tensor_scalar(dst[1][0:64, sl], pp[64:128, :],
                                         scale_q, None, ALU.mult)
                else:
                    vector.tensor_copy(dst[0][0:64, sl], pp[0:64, :])
                    vector.tensor_copy(dst[1][0:64, sl], pp[64:128, :])
                C.inc(vector.drain(), "pjc", 1)
                j += 1
        for tt in range(ntok // 128):
            vector.wait_ge(C.sem("pj"), PJ0 + j + 1)
            pp = pproj[j % 2]
            vector.tensor_copy(vtok[0][:, tt * vst: tt * vst + 64], pp[:, 0:64])
            vector.tensor_copy(vtok[1][:, tt * vst: tt * vst + 64], pp[:, 64:128])
            C.inc(vector.drain(), "pjc", 1)
            j += 1
        if vst == 65:
            vector.memset(vtok[0][:, 64::65], 1.0)
            vector.memset(vtok[1][:, 64::65], 1.0)
        C.inc(vector.drain(), "vdone", 1)
    return C.n("vdone")


def causal_softmax_head(C, block, consts, kp, qp, vt, oT, h_abs, ps2, ppv,
                        pbc, stmp, pT, obuf, pbcs, rrow, softmax=True):
    NQC = S // 512
    masks = consts["masks"]
    masks01 = consts["masks01"]
    ones64 = consts["ones64"]
    S0 = C.n("s"); E0 = C.n("exp"); P0 = C.n("pvd"); R0 = C.n("rec")
    B0 = C.n("bc"); O0 = C.n("on"); M0 = C.n("msk")
    OSp = [C.n("osc0"), C.n("osc1")]
    vst = 65 if softmax else 64

    def g_of(qc, tk):
        return sum(4 * q + 4 for q in range(qc)) + tk

    @block.tensor
    def _(tensor):
        tensor.wait_ge(C.sem("vdone"), C.n("vdone"))
        tensor.wait_ge(C.sem("exp"), E0)
        tensor.wait_ge(C.sem("on"), O0)
        for qc in range(NQC):
            ntk = 4 * qc + 4
            if qc > 0:
                tensor.wait_ge(C.sem("on"), O0 + qc)
            for tk in range(ntk):
                gi = g_of(qc, tk)
                if gi >= 2:
                    tensor.wait_ge(C.sem("exp"), E0 + gi - 1)
                C.inc(tensor.matmul(ps2[gi % 2][:],
                                    kp[:, tk * 128:(tk + 1) * 128],
                                    qp[:, qc * 512:(qc + 1) * 512],
                                    start=True, stop=True), "s", 1)
                if tk >= 1:
                    tensor.wait_ge(C.sem("exp"), E0 + gi)
                    tensor.matmul(ppv[0:vst, :],
                                  vt[:, (tk - 1) * vst:(tk - 1) * vst + vst],
                                  pT[(gi - 1) % 2][:],
                                  start=(tk == 1), stop=False)
            gl = g_of(qc, ntk - 1)
            tensor.wait_ge(C.sem("exp"), E0 + gl + 1)
            C.inc(tensor.matmul(ppv[0:vst, :],
                                vt[:, (ntk - 1) * vst:(ntk - 1) * vst + vst],
                                pT[gl % 2][:], start=False, stop=True), "pvd", 1)
            if softmax:
                tensor.wait_ge(C.sem("rec"), R0 + qc + 1)
                C.inc(tensor.matmul(pbc, ones64[:], rrow, start=True,
                                    stop=True), "bc", 1)

    if softmax:
        @block.scalar
        def _(scalar):
            scalar = C.wrap(scalar, 's')
            for qc in range(NQC):
                ntk = 4 * qc + 4
                for tk in range(ntk):
                    gi = g_of(qc, tk)
                    scalar.wait_ge(C.sem("s"), S0 + gi + 1)
                    if tk >= 4 * qc:
                        v = tk - 4 * qc
                        scalar.wait_ge(C.sem("msk"), M0 + 4 * qc + v + 1)
                        C.inc(scalar.activation(pT[gi % 2][:], stmp[v % 2][:],
                                                AF.Exp), "exp", 1)
                    else:
                        C.inc(scalar.activation(pT[gi % 2][:],
                                                ps2[gi % 2][:], AF.Exp), "exp", 1)

    @block.vector
    def _(vector):
        vector = C.wrap(vector, 'v')
        vector.wait_ge(C.sem("exp"), E0)
        vector.wait_ge(C.sem("osc0"), OSp[0])
        vector.wait_ge(C.sem("osc1"), OSp[1])
        for qc in range(NQC):
            ntk = 4 * qc + 4
            for tk in range((0 if not softmax else 4 * qc), ntk):
                gi = g_of(qc, tk)
                vector.wait_ge(C.sem("s"), S0 + gi + 1)
                if softmax:
                    v = tk - 4 * qc
                    if v >= 2:
                        vector.wait_ge(C.sem("exp"), E0 + g_of(qc, 4 * qc + v - 2) + 1)
                    elif qc > 0:
                        vector.wait_ge(C.sem("exp"),
                                       E0 + g_of(qc - 1, 4 * (qc - 1) + v + 2) + 1)
                    C.inc(vector.tensor_tensor(
                        stmp[v % 2][:], ps2[gi % 2][:],
                        masks[:, v * 512:(v + 1) * 512], ALU.add), "msk", 1)
                else:
                    if gi >= 2:
                        vector.wait_ge(C.sem("exp"), E0 + gi - 1)
                    if tk >= 4 * qc:
                        v = tk - 4 * qc
                        C.inc(vector.tensor_tensor(
                            pT[gi % 2][:], ps2[gi % 2][:],
                            masks01[:, v * 512:(v + 1) * 512], ALU.mult),
                            "exp", 1)
                    else:
                        C.inc(vector.tensor_copy(pT[gi % 2][:],
                                                 ps2[gi % 2][:]), "exp", 1)
            vector.wait_ge(C.sem("pvd"), P0 + qc + 1)
            if softmax:
                vector.reciprocal(rrow, ppv[64:65, :])
                C.inc(vector.drain(), "rec", 1)
                vector.wait_ge(C.sem("bc"), B0 + qc + 1)
                vector.tensor_copy(pbcs[:], pbc)
            if qc > 0:
                pq = (qc - 1) % 2
                vector.wait_ge(C.sem("osc%d" % pq),
                               OSp[pq] + 16 * ((qc - 1) // 2 + 1))
            if softmax:
                C.inc(vector.tensor_tensor(obuf[:], ppv[0:64, :], pbcs[:],
                                           ALU.mult), "on", 1)
            else:
                C.inc(vector.tensor_copy(obuf[:], ppv[0:64, :]), "on", 1)

    O0n = O0

    @block.sync
    def _(sync):
        for qc in range(NQC):
            sync.wait_ge(C.sem("on"), O0n + qc + 1)
            C.inc(sync.dma_start(
                out=oT[h_abs * 64:(h_abs + 1) * 64, qc * 512:(qc + 1) * 512],
                in_=obuf[:]), "osc%d" % (qc % 2), 16)


def build_A2():
    nc = bass.Bass("TRN2", target_bir_lowering=False)
    xb = nc.dram_tensor("xb", [S, D], F32, kind="ExternalInput").ap()
    wq = nc.dram_tensor("wq", [D, 512], F32, kind="ExternalInput").ap()
    wkvd = nc.dram_tensor("wkvd", [D, 256], F32, kind="ExternalInput").ap()
    wku = nc.dram_tensor("wku", [256, 512], F32, kind="ExternalInput").ap()
    wvu = nc.dram_tensor("wvu", [256, 512], F32, kind="ExternalInput").ap()
    oT = nc.dram_tensor("oT", [512, S], F32, kind="ExternalOutput").ap()

    with ExitStack() as ctx:
        C = Ctx(nc, ctx)
        C.vones = True
        block = ctx.enter_context(nc.Block())
        consts = emit_consts_attn(C, nc, block)
        yT = C.sb("yT", [128, 8 * S], BF16)
        ptps = [C.ps("ptpA", [128, 512], BF16), C.ps("ptpB", [128, 512], BF16)]
        emit_ln_yT(C, block, xb, 32, yT, ptps, name="ln")
        wqb = C.sb("wqb", [128, 8 * 512], BF16)
        wkvdb = C.sb("wkvdb", [128, 8 * 256], BF16)
        wkub = C.sb("wkub", [128, 2 * 512], BF16)
        wvub = C.sb("wvub", [128, 2 * 512], BF16)
        WLD = emit_wcast(C, block, [(wq, wqb, 1024, 512), (wkvd, wkvdb, 1024, 256),
                                    (wku, wkub, 256, 512), (wvu, wvub, 256, 512)])
        latT = C.sb("latT", [128, 2 * S], BF16)
        pproj = [C.ps("pproj0", [128, 512]), C.ps("pproj1", [128, 512])]

        LT0 = C.n("lt"); LTC0 = C.n("ltc")

        @block.tensor
        def _(tensor):
            tensor.wait_ge(C.sem("wld"), WLD)
            tensor.wait_ge(C.sem("ln_tpc"), 64)
            j = 0
            for lt in range(2):
                for tc in range(8):
                    if j >= 2:
                        tensor.wait_ge(C.sem("ltc"), LTC0 + j - 1)
                    pp = pproj[j % 2]
                    for ch in range(8):
                        ins = tensor.matmul(
                            pp[:],
                            wkvdb[:, ch * 256 + lt * 128: ch * 256 + lt * 128 + 128],
                            yT[:, ch * S + tc * 512: ch * S + (tc + 1) * 512],
                            start=(ch == 0), stop=(ch == 7))
                    C.inc(ins, "lt", 1)
                    j += 1

        @block.vector
        def _(vector):
            vector = C.wrap(vector, 'v')
            for j in range(16):
                vector.wait_ge(C.sem("lt"), LT0 + j + 1)
                lt, tc = j // 8, j % 8
                vector.tensor_copy(
                    latT[:, lt * S + tc * 512: lt * S + (tc + 1) * 512],
                    pproj[j % 2][:])
                C.inc(vector.drain(), "ltc", 1)

        kp = [C.sb("kp0", [64, S], BF16), C.sb("kp1", [64, S], BF16)]
        qp = [C.sb("qp0", [64, S], BF16), C.sb("qp1", [64, S], BF16)]
        vtok = [C.sb("vtok0", [128, 32 * 65], BF16),
                C.sb("vtok1", [128, 32 * 65], BF16)]
        ps2 = [C.ps("ps0", [128, 512]), C.ps("ps1", [128, 512])]
        ppv = C.ps("ppv", [65, 512])
        pbc = C.ps("pbc", [64, 512])[0:64, :]
        stmp = [C.sb("stmp0", [128, 512]), C.sb("stmp1", [128, 512])]
        pT = [C.sb("pT0", [128, 512], BF16), C.sb("pT1", [128, 512], BF16)]
        obuf = C.sb("obuf", [64, 512])
        pbcs = C.sb("pbcs", [64, 512])
        rrowt = C.sb("rrowt", [1, 512])
        rrow = rrowt[0:1, :]

        for hp in range(4):
            PJ0 = C.n("pj"); PJC0 = C.n("pjc")

            @block.tensor
            def _(tensor, hp=hp, PJC0=PJC0, LTC_ALL=C.n("ltc"),
                  ON_PREV=C.n("on")):
                tensor.wait_ge(C.sem("ltc"), LTC_ALL)
                tensor.wait_ge(C.sem("on"), ON_PREV)
                j = 0
                for tc in range(8):
                    if j >= 2:
                        tensor.wait_ge(C.sem("pjc"), PJC0 + j - 1)
                    pp = pproj[j % 2]
                    for ch in range(8):
                        ins = tensor.matmul(
                            pp[:],
                            wqb[:, ch * 512 + hp * 128: ch * 512 + hp * 128 + 128],
                            yT[:, ch * S + tc * 512: ch * S + (tc + 1) * 512],
                            start=(ch == 0), stop=(ch == 7))
                    C.inc(ins, "pj", 1)
                    j += 1
                for tc in range(8):
                    if j >= 2:
                        tensor.wait_ge(C.sem("pjc"), PJC0 + j - 1)
                    pp = pproj[j % 2]
                    for ch in range(2):
                        ins = tensor.matmul(
                            pp[:],
                            wkub[:, ch * 512 + hp * 128: ch * 512 + hp * 128 + 128],
                            latT[:, ch * S + tc * 512: ch * S + (tc + 1) * 512],
                            start=(ch == 0), stop=(ch == 1))
                    C.inc(ins, "pj", 1)
                    j += 1
                for tt in range(32):
                    if j >= 2:
                        tensor.wait_ge(C.sem("pjc"), PJC0 + j - 1)
                    pp = pproj[j % 2]
                    for ch in range(2):
                        ins = tensor.matmul(
                            pp[:, 0:128],
                            latT[:, ch * S + tt * 128: ch * S + (tt + 1) * 128],
                            wvub[:, ch * 512 + hp * 128: ch * 512 + hp * 128 + 128],
                            start=(ch == 0), stop=(ch == 1))
                    C.inc(ins, "pj", 1)
                    j += 1

            @block.vector
            def _(vector, hp=hp, PJ0=PJ0):
                vector = C.wrap(vector, 'v')
                j = 0
                for wi in range(2):
                    dst = qp if wi == 0 else kp
                    for tc in range(8):
                        vector.wait_ge(C.sem("pj"), PJ0 + j + 1)
                        pp = pproj[j % 2]
                        sl = slice(tc * 512, (tc + 1) * 512)
                        if wi == 0:
                            vector.tensor_scalar(dst[0][0:64, sl], pp[0:64, :],
                                                 0.125, None, ALU.mult)
                            vector.tensor_scalar(dst[1][0:64, sl], pp[64:128, :],
                                                 0.125, None, ALU.mult)
                        else:
                            vector.tensor_copy(dst[0][0:64, sl], pp[0:64, :])
                            vector.tensor_copy(dst[1][0:64, sl], pp[64:128, :])
                        C.inc(vector.drain(), "pjc", 1)
                        j += 1
                for tt in range(32):
                    vector.wait_ge(C.sem("pj"), PJ0 + j + 1)
                    pp = pproj[j % 2]
                    vector.tensor_copy(vtok[0][:, tt * 65: tt * 65 + 64],
                                       pp[:, 0:64])
                    vector.tensor_copy(vtok[1][:, tt * 65: tt * 65 + 64],
                                       pp[:, 64:128])
                    C.inc(vector.drain(), "pjc", 1)
                    j += 1
                vector.memset(vtok[0][:, 64::65], 1.0)
                vector.memset(vtok[1][:, 64::65], 1.0)
                C.inc(vector.drain(), "vdone", 1)

            for hi in range(2):
                causal_softmax_head(C, block, consts, kp[hi], qp[hi], vtok[hi],
                                    oT, hp * 2 + hi, ps2, ppv, pbc, stmp, pT,
                                    obuf, pbcs, rrow, softmax=True)

        @block.sync
        def _(sync):
            sync.wait_ge(C.sem("osc0"), C.n("osc0"))
            sync.wait_ge(C.sem("osc1"), C.n("osc1"))
    return nc


def build_A3():
    nc = bass.Bass("TRN2", target_bir_lowering=False)
    xb = nc.dram_tensor("xb", [S, D], F32, kind="ExternalInput").ap()
    wqd = nc.dram_tensor("wq", [D, 512], F32, kind="ExternalInput").ap()
    wkd = nc.dram_tensor("wk", [D, 512], F32, kind="ExternalInput").ap()
    wvd = nc.dram_tensor("wv", [D, 512], F32, kind="ExternalInput").ap()
    oT = nc.dram_tensor("oT", [512, S], F32, kind="ExternalOutput").ap()

    with ExitStack() as ctx:
        C = Ctx(nc, ctx)
        C.vones = False
        block = ctx.enter_context(nc.Block())
        consts = emit_consts_attn(C, nc, block)
        yT = C.sb("yT", [128, 8 * S], BF16)
        ptps = [C.ps("ptpA", [128, 512], BF16), C.ps("ptpB", [128, 512], BF16)]
        emit_ln_yT(C, block, xb, 32, yT, ptps, name="ln")
        wqb = C.sb("wqb", [128, 8 * 512], BF16)
        wkb = C.sb("wkb", [128, 8 * 512], BF16)
        wvb = C.sb("wvb", [128, 8 * 512], BF16)
        WLD = emit_wcast(C, block, [(wqd, wqb, 1024, 512), (wkd, wkb, 1024, 512),
                                    (wvd, wvb, 1024, 512)])
        kp = [C.sb("kp0", [64, S], BF16), C.sb("kp1", [64, S], BF16)]
        qp = [C.sb("qp0", [64, S], BF16), C.sb("qp1", [64, S], BF16)]
        vtok = [C.sb("vtok0", [128, 32 * 64], BF16),
                C.sb("vtok1", [128, 32 * 64], BF16)]
        pproj = [C.ps("pproj0", [128, 512]), C.ps("pproj1", [128, 512])]
        ps2 = [C.ps("ps0", [128, 512]), C.ps("ps1", [128, 512])]
        ppv = C.ps("ppv", [65, 512])
        pbc = None
        stmp = None
        pT = [C.sb("pT0", [128, 512], BF16), C.sb("pT1", [128, 512], BF16)]
        obuf = C.sb("obuf", [64, 512])
        pbcs = None
        rrow = None

        for hp in range(4):
            @block.tensor
            def _(tensor, WLD=WLD, ON_PREV=C.n("on")):
                tensor.wait_ge(C.sem("wld"), WLD)
                tensor.wait_ge(C.sem("ln_tpc"), 64)
                tensor.wait_ge(C.sem("on"), ON_PREV)
            emit_proj_pair(C, block, hp, yT, (wkb, wqb, wvb),
                           (kp[0], kp[1], qp[0], qp[1]), vtok, S, pproj)
            for hi in range(2):
                causal_softmax_head(C, block, consts, kp[hi], qp[hi], vtok[hi],
                                    oT, hp * 2 + hi, ps2, ppv, pbc, stmp, pT,
                                    obuf, pbcs, rrow, softmax=False)

        @block.sync
        def _(sync):
            sync.wait_ge(C.sem("osc0"), C.n("osc0"))
            sync.wait_ge(C.sem("osc1"), C.n("osc1"))
    return nc


def build_A1():
    nc = bass.Bass("TRN2", target_bir_lowering=False)
    xb = nc.dram_tensor("xb", [S, D], F32, kind="ExternalInput").ap()
    wqd = nc.dram_tensor("wq", [D, 512], F32, kind="ExternalInput").ap()
    wkd = nc.dram_tensor("wk", [D, 512], F32, kind="ExternalInput").ap()
    wvd = nc.dram_tensor("wv", [D, 512], F32, kind="ExternalInput").ap()
    oT = nc.dram_tensor("oT", [512, S], F32, kind="ExternalOutput").ap()

    with ExitStack() as ctx:
        C = Ctx(nc, ctx)
        C.vones = True
        block = ctx.enter_context(nc.Block())
        consts = emit_consts_attn(C, nc, block)
        ones64 = consts["ones64"]
        yT = C.sb("yT", [128, 8 * S], BF16)
        ptps = [C.ps("ptpA", [128, 512], BF16), C.ps("ptpB", [128, 512], BF16)]
        emit_ln_yT(C, block, xb, 32, yT, ptps, name="ln")
        wqb = C.sb("wqb", [128, 8 * 512], BF16)
        wkb = C.sb("wkb", [128, 8 * 512], BF16)
        wvb = C.sb("wvb", [128, 8 * 512], BF16)
        WLD = emit_wcast(C, block, [(wqd, wqb, 1024, 512), (wkd, wkb, 1024, 512),
                                    (wvd, wvb, 1024, 512)])
        kp = [C.sb("kp0", [64, S], BF16), C.sb("kp1", [64, S], BF16)]
        qp = [C.sb("qp0", [64, S], BF16), C.sb("qp1", [64, S], BF16)]
        vtok = [C.sb("vtok0", [128, 32 * 65], BF16),
                C.sb("vtok1", [128, 32 * 65], BF16)]
        vg = [C.sb("vg0", [128, 4 * 8 * 65], BF16),
              C.sb("vg1", [128, 4 * 8 * 65], BF16)]
        pproj = [C.ps("pproj0", [128, 512]), C.ps("pproj1", [128, 512])]
        ps2 = [C.ps("ps0", [128, 512]), C.ps("ps1", [128, 512])]
        ppv = C.ps("ppv", [65, 512])
        pbch = C.ps("pbc", [64, 512])
        pbc = pbch[0:64, :]
        pT = [C.sb("pT0", [128, 512], BF16), C.sb("pT1", [128, 512], BF16)]
        obufs = [C.sb("obufA", [64, 512]), C.sb("obufB", [64, 512])]
        pbcs = C.sb("pbcs", [64, 512])
        rrowt = C.sb("rrowt", [1, 512])
        rrow = rrowt[0:1, :]
        oacc = C.sb("oacc", [64, S])
        NQC = S // 512

        for hp in range(4):
            PJ0 = C.n("pj"); PJC0 = C.n("pjc")

            @block.tensor
            def _(tensor, WLD=WLD, ON_PREV=C.n("oend"), PJGC_ALL=C.n("pjgc")):
                tensor.wait_ge(C.sem("wld"), WLD)
                tensor.wait_ge(C.sem("ln_tpc"), 64)
                tensor.wait_ge(C.sem("oend"), ON_PREV)
                tensor.wait_ge(C.sem("pjgc"), PJGC_ALL)
            emit_proj_pair(C, block, hp, yT, (wkb, wqb, wvb),
                           (kp[0], kp[1], qp[0], qp[1]), vtok, S, pproj,
                           scale_q=0.125)

            PJG0 = C.n("pjg"); PJGC0 = C.n("pjgc")

            @block.tensor
            def _(tensor, hp=hp, PJGC0=PJGC0, PJC_ALL=C.n("pjc")):
                tensor.wait_ge(C.sem("pjc"), PJC_ALL)
                j = 0
                for g in range(4):
                    for tt in range(8):
                        if j >= 2:
                            tensor.wait_ge(C.sem("pjgc"), PJGC0 + j - 1)
                        pp = pproj[j % 2]
                        for ch in range(8):
                            base = ch * S + g
                            ins = tensor.matmul(
                                pp[:, 0:128],
                                yT[:, base + tt * 512: base + tt * 512 + 509: 4],
                                wvb[:, ch * 512 + hp * 128: ch * 512 + hp * 128 + 128],
                                start=(ch == 0), stop=(ch == 7))
                        C.inc(ins, "pjg", 1)
                        j += 1

            @block.vector
            def _(vector, PJG0=PJG0):
                vector = C.wrap(vector, 'v')
                j = 0
                for g in range(4):
                    for tt in range(8):
                        vector.wait_ge(C.sem("pjg"), PJG0 + j + 1)
                        pp = pproj[j % 2]
                        o0 = (g * 8 + tt) * 65
                        vector.tensor_copy(vg[0][:, o0:o0 + 64], pp[:, 0:64])
                        vector.tensor_copy(vg[1][:, o0:o0 + 64], pp[:, 64:128])
                        C.inc(vector.drain(), "pjgc", 1)
                        j += 1
                vector.memset(vg[0][:, 64::65], 1.0)
                vector.memset(vg[1][:, 64::65], 1.0)
                C.inc(vector.drain(), "vgdone", 1)

            for hi in range(2):
                kph, qph, vth, vgh = kp[hi], qp[hi], vtok[hi], vg[hi]
                h_abs = hp * 2 + hi
                S0 = C.n("s"); E0 = C.n("exp"); P0 = C.n("pvd")
                R0 = C.n("rec"); B0 = C.n("bc"); O0 = C.n("on")
                VD = C.n("vdone"); VGD = C.n("vgdone")

                @block.tensor
                def _(tensor, kph=kph, qph=qph, vth=vth, E0=E0, R0=R0, O0=O0,
                      VD=VD, VGD=VGD):
                    tensor.wait_ge(C.sem("vdone"), VD)
                    tensor.wait_ge(C.sem("vgdone"), VGD)
                    tensor.wait_ge(C.sem("exp"), E0)
                    tensor.wait_ge(C.sem("on"), O0)
                    gi = 0
                    for qc in range(NQC):
                        if qc > 0:
                            tensor.wait_ge(C.sem("on"), O0 + qc)
                        for tk in range(32):
                            if gi >= 2:
                                tensor.wait_ge(C.sem("exp"), E0 + gi - 1)
                            C.inc(tensor.matmul(ps2[gi % 2][:],
                                                kph[:, tk * 128:(tk + 1) * 128],
                                                qph[:, qc * 512:(qc + 1) * 512],
                                                start=True, stop=True), "s", 1)
                            if tk >= 1:
                                tensor.wait_ge(C.sem("exp"), E0 + gi)
                                tensor.matmul(ppv[:],
                                              vth[:, (tk - 1) * 65:(tk - 1) * 65 + 65],
                                              pT[(gi - 1) % 2][:],
                                              start=(tk == 1), stop=False)
                            gi += 1
                        tensor.wait_ge(C.sem("exp"), E0 + gi)
                        C.inc(tensor.matmul(ppv[:], vth[:, 31 * 65: 31 * 65 + 65],
                                            pT[(gi - 1) % 2][:], start=False,
                                            stop=True), "pvd", 1)
                        tensor.wait_ge(C.sem("rec"), R0 + qc + 1)
                        C.inc(tensor.matmul(pbc, ones64[:], rrow, start=True,
                                            stop=True), "bc", 1)

                @block.scalar
                def _(scalar, S0=S0):
                    scalar = C.wrap(scalar, 's')
                    for gi in range(NQC * 32):
                        scalar.wait_ge(C.sem("s"), S0 + gi + 1)
                        C.inc(scalar.activation(pT[gi % 2][:], ps2[gi % 2][:],
                                                AF.Exp), "exp", 1)

                @block.vector
                def _(vector, E0=E0, P0=P0, B0=B0):
                    vector = C.wrap(vector, 'v')
                    vector.wait_ge(C.sem("exp"), E0)
                    for qc in range(NQC):
                        vector.wait_ge(C.sem("pvd"), P0 + qc + 1)
                        vector.reciprocal(rrow, ppv[64:65, :])
                        C.inc(vector.drain(), "rec", 1)
                        vector.wait_ge(C.sem("bc"), B0 + qc + 1)
                        vector.tensor_copy(pbcs[:], pbc)
                        C.inc(vector.tensor_tensor(
                            oacc[:, qc * 512:(qc + 1) * 512], ppv[0:64, :],
                            pbcs[:], ALU.mult), "on", 1)

                SL0 = C.n("sl"); EL0 = C.n("expl"); PL0 = C.n("pvl")
                RL0 = C.n("recl"); BL0 = C.n("bcl"); OL0 = C.n("onl")

                @block.tensor
                def _(tensor, kph=kph, qph=qph, vth=vth, EL0=EL0, RL0=RL0,
                      OL0=OL0, O_ALL=C.n("on")):
                    tensor.wait_ge(C.sem("on"), O_ALL)
                    for w in range(32):
                        if w >= 2:
                            tensor.wait_ge(C.sem("expl"), EL0 + w - 1)
                        sl = slice(w * 128, (w + 1) * 128)
                        C.inc(tensor.matmul(ps2[w % 2][:, 0:128], kph[:, sl],
                                            qph[:, sl], start=True, stop=True),
                              "sl", 1)
                        tensor.wait_ge(C.sem("expl"), EL0 + w + 1)
                        if w > 0:
                            tensor.wait_ge(C.sem("onl"), OL0 + w)
                        C.inc(tensor.matmul(ppv[:, 0:128],
                                            vth[:, w * 65: w * 65 + 65],
                                            pT[w % 2][:, 0:128],
                                            start=True, stop=True), "pvl", 1)
                        tensor.wait_ge(C.sem("recl"), RL0 + w + 1)
                        C.inc(tensor.matmul(pbc[:, 0:128], ones64[:],
                                            rrowt[0:1, 0:128],
                                            start=True, stop=True), "bcl", 1)

                @block.scalar
                def _(scalar, SL0=SL0):
                    scalar = C.wrap(scalar, 's')
                    for w in range(32):
                        scalar.wait_ge(C.sem("sl"), SL0 + w + 1)
                        C.inc(scalar.activation(pT[w % 2][:, 0:128],
                                                ps2[w % 2][:, 0:128], AF.Exp),
                              "expl", 1)

                @block.vector
                def _(vector, PL0=PL0, BL0=BL0):
                    vector = C.wrap(vector, 'v')
                    for w in range(32):
                        vector.wait_ge(C.sem("pvl"), PL0 + w + 1)
                        vector.reciprocal(rrowt[0:1, 0:128], ppv[64:65, 0:128])
                        C.inc(vector.drain(), "recl", 1)
                        vector.wait_ge(C.sem("bcl"), BL0 + w + 1)
                        vector.tensor_copy(pbcs[:, 0:128], pbc[:, 0:128])
                        sl = slice(w * 128, (w + 1) * 128)
                        vector.tensor_tensor(pbcs[:, 0:128], ppv[0:64, 0:128],
                                             pbcs[:, 0:128], ALU.mult)
                        C.inc(vector.tensor_tensor(oacc[:, sl], oacc[:, sl],
                                                   pbcs[:, 0:128], ALU.add),
                              "onl", 1)

                SD0 = C.n("sd"); ED0 = C.n("expd"); PD0 = C.n("pvdd")
                RD0 = C.n("recd"); BD0 = C.n("bcd"); OD0 = C.n("ond")
                OSCp = [C.n("osc0"), C.n("osc1")]

                @block.tensor
                def _(tensor, kph=kph, qph=qph, vgh=vgh, ED0=ED0, RD0=RD0,
                      OD0=OD0, OL_ALL=C.n("onl")):
                    tensor.wait_ge(C.sem("onl"), OL_ALL)
                    gi = 0
                    nq = 0
                    for g in range(4):
                        for qc in range(2):
                            if nq > 0:
                                tensor.wait_ge(C.sem("ond"), OD0 + nq)
                            for tk in range(8):
                                if gi >= 2:
                                    tensor.wait_ge(C.sem("expd"), ED0 + gi - 1)
                                kgs = kph[0:64, g + tk * 512: g + tk * 512 + 509: 4]
                                qgs = qph[0:64, g + qc * 2048: g + qc * 2048 + 2045: 4]
                                C.inc(tensor.matmul(ps2[gi % 2][:], kgs, qgs,
                                                    start=True, stop=True), "sd", 1)
                                if tk >= 1:
                                    tensor.wait_ge(C.sem("expd"), ED0 + gi)
                                    o0 = (g * 8 + (tk - 1)) * 65
                                    tensor.matmul(ppv[:], vgh[:, o0:o0 + 65],
                                                  pT[(gi - 1) % 2][:],
                                                  start=(tk == 1), stop=False)
                                gi += 1
                            tensor.wait_ge(C.sem("expd"), ED0 + gi)
                            o0 = (g * 8 + 7) * 65
                            C.inc(tensor.matmul(ppv[:], vgh[:, o0:o0 + 65],
                                                pT[(gi - 1) % 2][:], start=False,
                                                stop=True), "pvdd", 1)
                            nq += 1
                            tensor.wait_ge(C.sem("recd"), RD0 + nq)
                            C.inc(tensor.matmul(pbc, ones64[:], rrow, start=True,
                                                stop=True), "bcd", 1)

                @block.scalar
                def _(scalar, SD0=SD0):
                    scalar = C.wrap(scalar, 's')
                    for gi in range(4 * 2 * 8):
                        scalar.wait_ge(C.sem("sd"), SD0 + gi + 1)
                        C.inc(scalar.activation(pT[gi % 2][:], ps2[gi % 2][:],
                                                AF.Exp), "expd", 1)

                @block.vector
                def _(vector, PD0=PD0, BD0=BD0, h_abs=h_abs):
                    vector = C.wrap(vector, 'v')
                    nq = 0
                    for g in range(4):
                        for qc in range(2):
                            vector.wait_ge(C.sem("pvdd"), PD0 + nq + 1)
                            vector.reciprocal(rrow, ppv[64:65, :])
                            C.inc(vector.drain(), "recd", 1)
                            vector.wait_ge(C.sem("bcd"), BD0 + nq + 1)
                            vector.tensor_copy(pbcs[:], pbc)
                            vector.tensor_tensor(pbcs[:], ppv[0:64, :], pbcs[:],
                                                 ALU.mult)
                            osl = oacc[0:64, g + qc * 2048: g + qc * 2048 + 2045: 4]
                            C.inc(vector.tensor_tensor(osl, osl, pbcs[:],
                                                       ALU.add), "ond", 1)
                            nq += 1
                    for qc in range(NQC):
                        pq = qc % 2
                        if qc >= 2:
                            vector.wait_ge(C.sem("osc%d" % pq),
                                           OSCp[pq] + 16 * ((qc - 2) // 2 + 1))
                        else:
                            vector.wait_ge(C.sem("osc%d" % pq), OSCp[pq])
                        C.inc(vector.tensor_scalar(
                            obufs[pq][:], oacc[:, qc * 512:(qc + 1) * 512],
                            1.0 / 3.0, None, ALU.mult), "oconv", 1)

                OC0 = C.n("oconv") - NQC

                @block.sync
                def _(sync, h_abs=h_abs, OC0=OC0):
                    for qc in range(NQC):
                        sync.wait_ge(C.sem("oconv"), OC0 + qc + 1)
                        C.inc(sync.dma_start(
                            out=oT[h_abs * 64:(h_abs + 1) * 64,
                                   qc * 512:(qc + 1) * 512],
                            in_=obufs[qc % 2][:]), "osc%d" % (qc % 2), 16)

                @block.tensor
                def _(tensor, OE=C.n("oconv")):
                    tensor.wait_ge(C.sem("oconv"), OE)
                    C.inc(tensor.drain(), "oend", 1)

        @block.sync
        def _(sync):
            sync.wait_ge(C.sem("osc0"), C.n("osc0"))
            sync.wait_ge(C.sem("osc1"), C.n("osc1"))
    return nc


def emit_resid_ln(C, block, x_dram, x1_out, n_tiles, yT, ptps, pproj,
                  o_dram=None, w_dram=None, otok_dram=None, y_out=None,
                  x_sb=None, fp8=None, name="rl"):
    ntok = n_tiles * 128
    xa = C.sb(name + "_xa", [128, D])
    xb2 = C.sb(name + "_xb", [128, D])
    if otok_dram is not None:
        ot0 = C.sb(name + "_ot0", [128, D])
        ot1 = C.sb(name + "_ot1", [128, D])
    else:
        ot0 = ot1 = None
    xp0 = C.sb(name + "_xp0", [128, D])
    xp1 = C.sb(name + "_xp1", [128, D])
    mean = C.sb(name + "_m", [128, 1])
    var = C.sb(name + "_v", [128, 1])
    var2 = C.sb(name + "_v2", [128, 1])
    sdv = C.sb(name + "_sd", [128, 1])
    inv = C.sb(name + "_i", [128, 1])
    eps = C.sb(name + "_eps", [128, 1])
    sq = C.sb(name + "_sq", [128, 512])
    yb0 = C.sb(name + "_yb0", [128, D], BF16)
    yb1 = C.sb(name + "_yb1", [128, D], BF16)
    C.rl_bufs = (xa, xb2, xp0, xp1)
    if fp8 is not None:
        absy = C.sb(name + "_ab", [128, D])
        fsc = C.sb(name + "_fs", [128, 1])
        ysc0 = C.sb(name + "_ys0", [128, D], BF16)
        ysc1 = C.sb(name + "_ys1", [128, D], BF16)
        q8t = C.sb(name + "_q8", [128, 128], FP8)

    use_wout = o_dram is not None
    if use_wout:
        otb = [C.sb(name + "_otbA", [128, 8 * 128], BF16),
               C.sb(name + "_otbB", [128, 8 * 128], BF16)]
        wob = C.sb(name + "_wob", [128, 8 * D], BF16)
        emit_wcast(C, block, [(w_dram, wob, 1024, D)], name + "_wl")
        WL = C.n(name + "_wl")



    @block.sync
    def _(sync):
        for t in range(n_tiles):
            buf = xa if t % 2 == 0 else xb2
            ot = ot0 if t % 2 == 0 else ot1
            if t >= 2:
                sync.wait_ge(C.sem(name + "_y"), t - 1)
            C.inc(sync.dma_start(out=buf[:], in_=x_dram[t * 128:(t + 1) * 128, :]),
                  name + "_ld%d" % (t % 2), 16)
            if otok_dram is not None:
                C.inc(sync.dma_start(out=ot[:],
                                     in_=otok_dram[t * 128:(t + 1) * 128, :]),
                      name + "_ld%d" % (t % 2), 16)

    @block.vector
    def _(vector):
        vector = C.wrap(vector, 'v')
        C.inc(vector.memset(eps[:], 1.0e-5), name + "_eps", 1)
        nld = 2 if otok_dram is not None else 1
        for t in range(n_tiles):
            buf = xa if t % 2 == 0 else xb2
            ot = ot0 if t % 2 == 0 else ot1
            xp = xp0 if t % 2 == 0 else xp1
            yb = yb0 if t % 2 == 0 else yb1
            vector.wait_ge(C.sem(name + "_ld%d" % (t % 2)),
                           16 * nld * (t // 2 + 1))
            if t >= 2:
                nst = 2 if y_out is not None else 1
                vector.wait_ge(C.sem(name + "_st%d" % (t % 2)),
                               16 * nst * (t // 2))
            if use_wout:
                vector.wait_ge(C.sem(name + "_pw"), t + 1)
                vector.tensor_tensor(xp[:, 0:512], buf[:, 0:512], pproj[0][:],
                                     ALU.add)
                vector.tensor_tensor(xp[:, 512:1024], buf[:, 512:1024],
                                     pproj[1][:], ALU.add)
            elif otok_dram is not None:
                vector.tensor_tensor(xp[:], buf[:], ot[:], ALU.add)
            else:
                vector.tensor_copy(xp[:], buf[:])
            if x_sb is not None:
                vector.tensor_copy(x_sb[:, t * D:(t + 1) * D], xp[:])
            C.inc(vector.drain(), name + "_xpv", 1)
            vector.reduce_sum(mean[:], xp[:], axis=AX)
            vector.drain()
            vector.tensor_scalar(mean[:], mean[:], 1.0 / D, None, ALU.mult)
            C.inc(vector.tensor_scalar(buf[:], xp[:], mean[:], None,
                                       ALU.subtract), name + "_xc", 1)
            vector.wait_ge(C.sem(name + "_sv"), t + 1)
            vector.reciprocal(inv[:], sdv[:])
            vector.drain()
            if t >= 2:
                vector.wait_ge(C.sem(name + "_tp"), 2 * (t - 1))
            C.inc(vector.tensor_scalar(yb[:], buf[:], inv[:], None, ALU.mult),
                  name + "_y", 1)
            if fp8 is not None:
                ysc = ysc0 if t % 2 == 0 else ysc1
                vector.wait_ge(C.sem(name + "_aby"), t + 1)
                vector.reduce_max(fsc[:], absy[:], axis=AX)
                vector.drain()
                vector.tensor_copy(fp8["amaxs"][:, t:t + 1], fsc[:])
                vector.reciprocal(fsc[:], fsc[:])
                vector.drain()
                vector.tensor_scalar(fsc[:], fsc[:], 224.0, None, ALU.mult)
                C.inc(vector.tensor_scalar(ysc[:], yb[:], fsc[:], None,
                                           ALU.mult), name + "_yq", 1)

    @block.gpsimd
    def _(gpsimd):
        gpsimd = C.wrap(gpsimd, 'g')

        def store(t):
            xp = xp0 if t % 2 == 0 else xp1
            gpsimd.wait_ge(C.sem(name + "_xpv"), t + 1)
            C.inc(gpsimd.dma_start(out=x1_out[t * 128:(t + 1) * 128, :],
                                   in_=xp[:]), name + "_st%d" % (t % 2), 16)
            if y_out is not None:
                yb = yb0 if t % 2 == 0 else yb1
                gpsimd.wait_ge(C.sem(name + "_y"), t + 1)
                C.inc(gpsimd.dma_start(out=y_out[t * 128:(t + 1) * 128, :],
                                       in_=yb[:]), name + "_st%d" % (t % 2), 16)
        for t in range(n_tiles):
            if use_wout:
                if t >= 2:
                    gpsimd.wait_ge(C.sem(name + "_pw"), t - 1)
                for ch in range(8):
                    C.inc(gpsimd.dma_start(
                        out=otb[t % 2][:, ch * 128:(ch + 1) * 128],
                        in_=o_dram[(ch * n_tiles + t) * 128:
                                   (ch * n_tiles + t + 1) * 128, :]),
                        name + "_ol%d" % (t % 2), 16)
            if t >= 1:
                store(t - 1)
        store(n_tiles - 1)

    def _copy_groups(scalar, t):
        for g in range(2):
            gi = 2 * t + g
            scalar.wait_ge(C.sem(name + "_tp"), gi + 1)
            ptp = ptps[gi % 2]
            for f4 in range(4):
                f = g * 4 + f4
                dst = yT[:, f * ntok + t * 128: f * ntok + (t + 1) * 128]
                if fp8 is not None:
                    scalar.copy(q8t[:], ptp[:, f4 * 128: f4 * 128 + 128])
                    scalar.drain()
                    scalar.copy(dst, q8t[:])
                else:
                    scalar.copy(dst, ptp[:, f4 * 128: f4 * 128 + 128])
            C.inc(scalar.drain(), name + "_tpc", 1)

    @block.scalar
    def _(scalar):
        scalar = C.wrap(scalar, 's')
        scalar.wait_ge(C.sem(name + "_eps"), 1)
        for t in range(n_tiles):
            xp = xp0 if t % 2 == 0 else xp1
            yb = yb0 if t % 2 == 0 else yb1
            buf = xa if t % 2 == 0 else xb2
            scalar.wait_ge(C.sem(name + "_xc"), t + 1)
            scalar.activation(sq[:], buf[:, 0:512], AF.Square, accum_out=var[:])
            scalar.activation(sq[:], buf[:, 512:1024], AF.Square,
                              accum_out=var2[:])
            scalar.drain()
            scalar.activation(var[:], var[:], AF.Relu, bias=var2[:])
            C.inc(scalar.activation(sdv[:], var[:], AF.Sqrt, scale=1.0 / D,
                                    bias=eps[:]), name + "_sv", 1)
            if fp8 is not None:
                scalar.wait_ge(C.sem(name + "_y"), t + 1)
                C.inc(scalar.activation(absy[:], yb[:], AF.Abs), name + "_aby", 1)
            if t >= 1:
                _copy_groups(scalar, t - 1)
        _copy_groups(scalar, n_tiles - 1)

    @block.tensor
    def _(tensor):
        if use_wout:
            tensor.wait_ge(C.sem(name + "_wl"), WL)
        ysem = name + ("_yq" if fp8 is not None else "_y")

        def tp_tile(t):
            tensor.wait_ge(C.sem(ysem), t + 1)
            if fp8 is not None:
                src = ysc0 if t % 2 == 0 else ysc1
            else:
                src = yb0 if t % 2 == 0 else yb1
            for g in range(2):
                gi = 2 * t + g
                if gi >= 2:
                    tensor.wait_ge(C.sem(name + "_tpc"), gi - 1)
                ptp = ptps[gi % 2]
                for f4 in range(4):
                    f = g * 4 + f4
                    tensor.transpose(ptp[:, f4 * 128: f4 * 128 + 128],
                                     src[:, f * 128:(f + 1) * 128], C.ident[:])
                C.inc(tensor.drain(), name + "_tp", 1)
        for t in range(n_tiles):
            if use_wout:
                if t >= 1:
                    tensor.wait_ge(C.sem(name + "_xpv"), t)
                tensor.wait_ge(C.sem(name + "_ol%d" % (t % 2)),
                               16 * 8 * (t // 2 + 1))
                for half in range(2):
                    pp = pproj[half]
                    for ch in range(8):
                        tensor.matmul(
                            pp[:],
                            otb[t % 2][:, ch * 128:(ch + 1) * 128],
                            wob[:, ch * D + half * 512: ch * D + half * 512 + 512],
                            start=(ch == 0), stop=(ch == 7))
                C.inc(tensor.drain(), name + "_pw", 1)
            if t >= 1:
                tp_tile(t - 1)
        tp_tile(n_tiles - 1)
    return


def build_F1():
    nc = bass.Bass("TRN2", target_bir_lowering=False)
    NT = TPC // 128
    xd = nc.dram_tensor("xd", [TPC, D], F32, kind="ExternalInput").ap()
    oTd = nc.dram_tensor("oTd", [8 * NT * 128, 128], F32,
                         kind="ExternalInput").ap()
    wod = nc.dram_tensor("wod", [D, D], F32, kind="ExternalInput").ap()
    w8d = nc.dram_tensor("w8d", [D, D], U8, kind="ExternalInput").ap()
    wsd = nc.dram_tensor("wsd", [128, 1], F32, kind="ExternalInput").ap()
    bd = nc.dram_tensor("bd", [1, D], F32, kind="ExternalInput").ap()
    x1o = nc.dram_tensor("x1o", [TPC, D], F32, kind="ExternalOutput").ap()
    x2o = nc.dram_tensor("x2o", [TPC, D], F32, kind="ExternalOutput").ap()
    ident_c = nc.inline_tensor(np.eye(128, dtype=BF), "ident_c")
    ones1_c = nc.inline_tensor(np.ones((1, 128), dtype=BF), "ones1_c")

    with ExitStack() as ctx:
        C = Ctx(nc, ctx)
        block = ctx.enter_context(nc.Block())
        C.ident = C.sb("ident", [128, 128], BF16)
        ones1 = C.sb("ones1", [1, 128], BF16)
        ws128 = C.sb("ws128", [128, 1])
        brow = C.sb("brow", [1, D])
        biasb = C.sb("biasb", [128, D])
        amaxs = C.sb("amaxs", [128, NT])
        fsct = C.sb("fsct", [128, NT])
        w8s = C.sb("w8s", [128, 8 * D], U8)
        wdec = C.sb("wdec", [128, 8 * D], BF16)
        yT = C.sb("yT", [128, 8 * TPC], BF16)
        xr = [C.sb("xrA", [128, D]), C.sb("xrB", [128, D])]

        @block.sync
        def _(sync):
            C.inc(sync.dma_start(out=C.ident[:], in_=ident_c.ap()), "cld", 16)
            C.inc(sync.dma_start(out=ones1[:], in_=ones1_c.ap()), "cld", 16)
            C.inc(sync.dma_start(out=ws128[:], in_=wsd), "cld", 16)
            C.inc(sync.dma_start(out=brow[:], in_=bd), "cld", 16)
            for c in range(8):
                C.inc(sync.dma_start(out=w8s[:, c * D:(c + 1) * D],
                                     in_=w8d[c * 128:(c + 1) * 128, :]), "cld", 16)

        @block.tensor
        def _(tensor):
            tensor.wait_ge(C.sem("cld"), C.cnt["cld"])

        @block.vector
        def _(vector):
            vector.wait_ge(C.sem("cld"), C.cnt["cld"])

        @block.scalar
        def _(scalar):
            scalar.wait_ge(C.sem("cld"), C.cnt["cld"])

        dtmp = C.sb("dtmp", [128, D])
        d_n = C.sb("d_n", [128, D])
        d_s = C.sb("d_s", [128, D])
        d_e = C.sb("d_e", [128, D])
        d_m = C.sb("d_m", [128, D])
        d_p = C.sb("d_p", [128, D])

        @block.vector
        def _(vector):
            vector = C.wrap(vector, 'v')
            for c in range(8):
                sl = slice(c * D, (c + 1) * D)
                vector.tensor_copy(d_n[:], w8s[:, sl])
                vector.tensor_scalar(d_s[:], d_n[:], 127.5, None, ALU.is_gt)
                vector.tensor_scalar(dtmp[:], d_s[:], -128.0, None, ALU.mult)
                vector.tensor_tensor(d_n[:], d_n[:], dtmp[:], ALU.add)
                vector.memset(d_e[:], 0.0)
                vector.tensor_copy(d_m[:], d_n[:])
                for k in (64.0, 32.0, 16.0, 8.0):
                    vector.tensor_scalar(dtmp[:], d_m[:], k - 0.5, None,
                                         ALU.is_ge)
                    vector.tensor_scalar(d_p[:], dtmp[:], -k, None, ALU.mult)
                    vector.tensor_tensor(d_m[:], d_m[:], d_p[:], ALU.add)
                    vector.tensor_scalar(d_p[:], dtmp[:], k / 8.0, None,
                                         ALU.mult)
                    vector.tensor_tensor(d_e[:], d_e[:], d_p[:], ALU.add)
                vector.tensor_scalar(dtmp[:], d_e[:], 0.5, None, ALU.is_ge)
                vector.tensor_scalar(dtmp[:], dtmp[:], 8.0, None, ALU.mult)
                vector.tensor_tensor(d_m[:], d_m[:], dtmp[:], ALU.add)
                vector.tensor_scalar(d_e[:], d_e[:], 1.0, -1.0, ALU.max, ALU.add)
                vector.memset(d_p[:], 1.0)
                for k, fac in ((8, 255.0), (4, 15.0), (2, 3.0), (1, 1.0)):
                    vector.tensor_scalar(dtmp[:], d_e[:], float(k) - 0.5, None,
                                         ALU.is_ge)
                    vector.tensor_scalar(d_n[:], dtmp[:], float(-k), None,
                                         ALU.mult)
                    vector.tensor_tensor(d_e[:], d_e[:], d_n[:], ALU.add)
                    vector.tensor_scalar(dtmp[:], dtmp[:], fac, 1.0, ALU.mult,
                                         ALU.add)
                    vector.tensor_tensor(d_p[:], d_p[:], dtmp[:], ALU.mult)
                vector.tensor_scalar(d_s[:], d_s[:], -2.0, 1.0, ALU.mult, ALU.add)
                vector.tensor_tensor(d_p[:], d_p[:], d_m[:], ALU.mult)
                vector.tensor_tensor(d_p[:], d_p[:], d_s[:], ALU.mult)
                vector.tensor_scalar(wdec[:, sl], d_p[:], float(2.0 ** -9),
                                     None, ALU.mult)
            C.inc(vector.drain(), "dec", 1)

        ptps = [C.ps("ptpA", [128, 512], BF16), C.ps("ptpB", [128, 512], BF16)]
        pproj = [C.ps("pproj0", [128, 512]), C.ps("pproj1", [128, 512])]
        emit_resid_ln(C, block, xd, x1o, NT, yT, ptps, pproj,
                      o_dram=oTd, w_dram=wod,
                      fp8=dict(amaxs=amaxs), name="rl")

        browb = C.sb("browb", [1, D], BF16)
        pb2 = [C.ps("pb0", [128, 512]), C.ps("pb1", [128, 512])]

        @block.scalar
        def _(scalar):
            scalar = C.wrap(scalar, 's')
            C.inc(scalar.copy(browb[:], brow[:]), "bb", 1)

        @block.tensor
        def _(tensor, DEC=C.n("dec"), YQ=C.n("rl_yq")):
            tensor.wait_ge(C.sem("bb"), 1)
            tensor.matmul(pb2[0][:], ones1[:], browb[0:1, 0:512], start=True,
                          stop=True)
            C.inc(tensor.matmul(pb2[1][:], ones1[:], browb[0:1, 512:1024],
                                start=True, stop=True), "bbt", 1)

        @block.vector
        def _(vector):
            vector = C.wrap(vector, 'v')
            vector.wait_ge(C.sem("bbt"), 1)
            vector.tensor_copy(biasb[:, 0:512], pb2[0][:])
            vector.tensor_copy(biasb[:, 512:1024], pb2[1][:])
            vector.wait_ge(C.sem("rl_yq"), NT)
            vector.tensor_scalar(fsct[:], amaxs[:], 2.0 / 448.0, None, ALU.mult)
            C.inc(vector.tensor_tensor(fsct[:], fsct[:],
                                       ws128[:].broadcast_to([128, NT]),
                                       ALU.mult), "fsc", 1)

        MM0 = C.n("mm")

        @block.tensor
        def _(tensor, DEC=C.n("dec")):
            tensor.wait_ge(C.sem("dec"), DEC)
            tensor.wait_ge(C.sem("rl_tpc"), 2 * NT)
            tensor.wait_ge(C.sem("fsc"), 1)
            for t in range(NT):
                if t >= 1:
                    tensor.wait_ge(C.sem("mmc"), t)
                for half in range(2):
                    pp = pb2[half]
                    for ch in range(8):
                        ins = tensor.matmul(
                            pp[:],
                            yT[:, ch * TPC + t * 128: ch * TPC + (t + 1) * 128],
                            wdec[:, ch * D + half * 512: ch * D + half * 512 + 512],
                            start=(ch == 0), stop=(ch == 7))
                C.inc(tensor.drain(), "mm", 1)

        ocmb = [C.sb("ocmbA", [128, D]), C.sb("ocmbB", [128, D])]

        @block.gpsimd
        def _(gpsimd):
            gpsimd = C.wrap(gpsimd, 'g')
            for t in range(NT):
                gpsimd.wait_ge(C.sem("rl_st%d" % (t % 2)), 16 * (t // 2 + 1))
                if t >= 2:
                    gpsimd.wait_ge(C.sem("fin%d" % (t % 2)), t // 2)
                C.inc(gpsimd.dma_start(out=xr[t % 2][:],
                                       in_=x1o[t * 128:(t + 1) * 128, :]),
                      "xl%d" % (t % 2), 16)

        @block.vector
        def _(vector, MM0=MM0):
            vector = C.wrap(vector, 'v')
            for t in range(NT):
                ob = ocmb[t % 2]
                vector.wait_ge(C.sem("xl%d" % (t % 2)), 16 * (t // 2 + 1))
                vector.wait_ge(C.sem("mm"), MM0 + t + 1)
                if t >= 2:
                    vector.wait_ge(C.sem("fst%d" % (t % 2)), 16 * (t // 2))
                fcol = fsct[:, t:t + 1]
                vector.tensor_scalar(ob[:, 0:512], pb2[0][:], fcol, None,
                                     ALU.mult)
                vector.tensor_scalar(ob[:, 512:1024], pb2[1][:], fcol, None,
                                     ALU.mult)
                C.inc(vector.drain(), "mmc", 1)
                vector.tensor_tensor(ob[:], ob[:], biasb[:], ALU.add)
                C.inc(vector.tensor_tensor(ob[:], ob[:], xr[t % 2][:], ALU.add),
                      "fin%d" % (t % 2), 1)

        @block.sync
        def _(sync):
            for t in range(NT):
                sync.wait_ge(C.sem("fin%d" % (t % 2)), t // 2 + 1)
                C.inc(sync.dma_start(out=x2o[t * 128:(t + 1) * 128, :],
                                     in_=ocmb[t % 2][:]),
                      "fst%d" % (t % 2), 16)
            sync.wait_ge(C.sem("fst0"), C.n("fst0"))
            sync.wait_ge(C.sem("fst1"), C.n("fst1"))
            sync.wait_ge(C.sem("rl_st0"), C.n("rl_st0"))
            sync.wait_ge(C.sem("rl_st1"), C.n("rl_st1"))
    return nc


def build_F3():
    nc = bass.Bass("TRN2", target_bir_lowering=False)
    NT = TPC // 128
    xd = nc.dram_tensor("xd", [TPC, D], F32, kind="ExternalInput").ap()
    oTd = nc.dram_tensor("oTd", [8 * NT * 128, 128], F32,
                         kind="ExternalInput").ap()
    wod = nc.dram_tensor("wod", [D, D], F32, kind="ExternalInput").ap()
    wrd = nc.dram_tensor("wrd", [D, 8], F32, kind="ExternalInput").ap()
    brd = nc.dram_tensor("brd", [1, 8], F32, kind="ExternalInput").ap()
    w1d = nc.dram_tensor("w1d", [8 * 1024, 2048], F32, kind="ExternalInput").ap()
    b1d = nc.dram_tensor("b1d", [128, 8 * 16], F32, kind="ExternalInput").ap()
    w2d = nc.dram_tensor("w2d", [8 * 2048, 1024], F32, kind="ExternalInput").ap()
    b2d = nc.dram_tensor("b2d", [8, D], F32, kind="ExternalInput").ap()
    x1o = nc.dram_tensor("x1o", [TPC, D], F32, kind="ExternalOutput").ap()
    x2o = nc.dram_tensor("x2o", [TPC, D], F32, kind="ExternalOutput").ap()
    ident_c = nc.inline_tensor(np.eye(128, dtype=BF), "ident_c")
    ones1_c = nc.inline_tensor(np.ones((1, 128), dtype=BF), "ones1_c")

    with ExitStack() as ctx:
        C = Ctx(nc, ctx)
        block = ctx.enter_context(nc.Block())
        C.ident = C.sb("ident", [128, 128], BF16)
        ones1 = C.sb("ones1", [1, 128], BF16)
        yT = C.sb("yT", [128, 8 * TPC], BF16)
        wrb = C.sb("wrb", [128, 8 * 8], BF16)
        brow = C.sb("brow", [1, 8], BF16)
        b2b = C.sb("b2b", [8, D], BF16)
        b1s = C.sb("b1s", [128, 8 * 16])
        acc = C.sb("acc", [128, NT * D], BF16)
        cw = C.sb("cw", [128, NT * 8])
        cwT = C.sb("cwT", [8, TPC], BF16)
        probs = C.sb("probs", [128, 8])
        pm = C.sb("pm", [128, 8])
        m1 = C.sb("m1", [128, 1])
        esum = C.sb("esum", [128, 1])
        w1e = C.sb("w1e", [128, 8 * 2048], BF16)
        w2e = C.sb("w2e", [128, 16 * 1024], BF16)
        hTs = [C.sb("hTsA", [128, 16 * 128], BF16),
               C.sb("hTsB", [128, 16 * 128], BF16)]
        etmp = C.sb("etmp", [128, 512])

        @block.sync
        def _(sync):
            C.inc(sync.dma_start(out=C.ident[:], in_=ident_c.ap()), "cld", 16)
            C.inc(sync.dma_start(out=ones1[:], in_=ones1_c.ap()), "cld", 16)
            C.inc(sync.dma_start(out=b1s[:], in_=b1d), "cld", 16)

        @block.gpsimd
        def _(gpsimd):
            gpsimd = C.wrap(gpsimd, 'g')
            C.inc(gpsimd.dma_start(out=brow[:], in_=brd), "cldg", 16)
            for c in range(8):
                C.inc(gpsimd.dma_start(out=wrb[:, c * 8:(c + 1) * 8],
                                       in_=wrd[c * 128:(c + 1) * 128, :]),
                      "cldg", 16)
            C.inc(gpsimd.dma_start(out=b2b[:], in_=b2d), "cldg", 16)
        CLD = C.cnt["cld"]
        CLDG = C.cnt["cldg"]

        @block.tensor
        def _(tensor):
            tensor.wait_ge(C.sem("cld"), CLD)
            tensor.wait_ge(C.sem("cldg"), CLDG)

        @block.vector
        def _(vector):
            vector.wait_ge(C.sem("cld"), CLD)
            vector.wait_ge(C.sem("cldg"), CLDG)

        @block.scalar
        def _(scalar):
            scalar.wait_ge(C.sem("cld"), CLD)
            scalar.wait_ge(C.sem("cldg"), CLDG)

        ptps = [C.ps("ptpA", [128, 512], BF16), C.ps("ptpB", [128, 512], BF16)]
        pproj = [C.ps("pproj0", [128, 512]), C.ps("pproj1", [128, 512])]
        emit_resid_ln(C, block, xd, x1o, NT, yT, ptps, pproj, o_dram=oTd,
                      w_dram=wod, name="rl")
        xr = [C.rl_bufs[0], C.rl_bufs[1]]
        ocmb = [C.rl_bufs[2], C.rl_bufs[3]]

        po2 = [C.ps("po0", [128, 512]), C.ps("po1", [128, 512])]
        prt = po2[0][:, 0:8]
        pbr = po2[1][:, 0:8]

        @block.tensor
        def _(tensor):
            tensor.wait_ge(C.sem("rl_tpc"), 2 * NT)
            C.inc(tensor.matmul(pbr, ones1[:], brow[0:1, :], start=True,
                                stop=True), "brt", 1)
            for t in range(NT):
                if t >= 1:
                    tensor.wait_ge(C.sem("rtc"), t)
                for ch in range(8):
                    ins = tensor.matmul(
                        prt,
                        yT[:, ch * TPC + t * 128: ch * TPC + (t + 1) * 128],
                        wrb[:, ch * 8:(ch + 1) * 8],
                        start=(ch == 0), stop=(ch == 7))
                C.inc(ins, "rt", 1)

        brs = C.sb("brs", [128, 8])

        @block.vector
        def _(vector, RT0=C.n("rt") - NT):
            vector = C.wrap(vector, 'v')
            vector.wait_ge(C.sem("brt"), 1)
            vector.tensor_copy(brs[:], pbr)
            for t in range(NT):
                vector.wait_ge(C.sem("rt"), RT0 + t + 1)
                vector.tensor_tensor(probs[:], prt, brs[:], ALU.add)
                C.inc(vector.drain(), "rtc", 1)
                vector.wait_ge(C.sem("rtx"), t + 1)
                vector.reciprocal(esum[:], esum[:])
                vector.drain()
                vector.tensor_scalar(probs[:], probs[:], esum[:], None, ALU.mult)
                vector.drain()
                vector.reduce_max(m1[:], probs[:], axis=AX)
                vector.drain()
                vector.tensor_scalar(pm[:], probs[:], m1[:], 0.0, ALU.subtract,
                                     ALU.is_ge)
                vector.tensor_scalar(pm[:], pm[:], -1.0, 1.0, ALU.mult, ALU.add)
                vector.tensor_tensor(pm[:], probs[:], pm[:], ALU.mult)
                vector.reduce_max(m1[:], pm[:], axis=AX)
                vector.drain()
                vector.tensor_scalar(pm[:], probs[:], m1[:], 0.0, ALU.subtract,
                                     ALU.is_ge)
                C.inc(vector.tensor_tensor(cw[:, t * 8:(t + 1) * 8], probs[:],
                                           pm[:], ALU.mult), "cwd", 1)

        @block.scalar
        def _(scalar, RTC0=C.n("rtc") - NT):
            scalar = C.wrap(scalar, 's')
            for t in range(NT):
                scalar.wait_ge(C.sem("rtc"), RTC0 + t + 1)
                C.inc(scalar.activation(probs[:], probs[:], AF.Exp,
                                        accum_out=esum[:]), "rtx", 1)

        pcw = None
        cwb = C.sb("cwb", [128, 8], BF16)

        @block.vector
        def _(vector, CWD0=C.n("cwd") - NT):
            vector = C.wrap(vector, 'v')
            vector.memset(acc[:], 0.0)
            C.inc(vector.drain(), "acz", 1)
            for t in range(NT):
                vector.wait_ge(C.sem("cwd"), CWD0 + t + 1)
                if t >= 1:
                    vector.wait_ge(C.sem("cwt"), t)
                C.inc(vector.tensor_copy(cwb[:], cw[:, t * 8:(t + 1) * 8]),
                      "cwb", 1)
                vector.wait_ge(C.sem("cwt2"), t)
                C.inc(vector.engine_nop(), "cwgo", 1)

        @block.tensor
        def _(tensor, CWB0=C.n("cwb") - NT, CWT20=C.n("cwt2")):
            for t in range(NT):
                tensor.wait_ge(C.sem("cwb"), CWB0 + t + 1)
                if t >= 1:
                    tensor.wait_ge(C.sem("cwt2"), CWT20 + t)
                tensor.transpose(ptps[0][0:8, 0:128], cwb[:], C.ident[:])
                C.inc(tensor.drain(), "cwt", 1)

        @block.scalar
        def _(scalar, CWT0=C.n("cwt") - NT):
            scalar = C.wrap(scalar, 's')
            for t in range(NT):
                scalar.wait_ge(C.sem("cwt"), CWT0 + t + 1)
                scalar.copy(cwT[0:8, t * 128:(t + 1) * 128], ptps[0][0:8, 0:128])
                C.inc(scalar.drain(), "cwt2", 1)

        pe = [pproj[0][:, 0:128], pproj[1][:, 0:128], po2[0][:, :], po2[1][:, :]]

        for e in range(8):
            EWL = C.n("ewl")

            @block.gpsimd
            def _(gpsimd, e=e, EDONE=C.n("eoc")):
                gpsimd = C.wrap(gpsimd, 'g')
                gpsimd.wait_ge(C.sem("eoc"), EDONE)
                for ch in range(8):
                    C.inc(gpsimd.dma_start(
                        out=w1e[:, ch * 2048:(ch + 1) * 2048],
                        in_=w1d[e * 1024 + ch * 128: e * 1024 + (ch + 1) * 128, :]),
                        "ewl", 16)
                for ch in range(16):
                    C.inc(gpsimd.dma_start(
                        out=w2e[:, ch * 1024:(ch + 1) * 1024],
                        in_=w2d[e * 2048 + ch * 128: e * 2048 + (ch + 1) * 128, :]),
                        "ewl", 16)
            EWL_T = C.n("ewl")

            @block.tensor
            def _(tensor, e=e, EWL_T=EWL_T, ACZ=C.n("acz"), CWGO=C.n("cwgo"),
                  HC0=C.n("hc"), EAC0=C.n("eac")):
                tensor.wait_ge(C.sem("ewl"), EWL_T)
                tensor.wait_ge(C.sem("acz"), ACZ)
                tensor.wait_ge(C.sem("cwgo"), CWGO)
                nh = 0
                for t in range(NT):
                    hb = hTs[t % 2]
                    for ot in range(16):
                        pp = pe[ot % 2]
                        if nh >= 2:
                            tensor.wait_ge(C.sem("hc"), HC0 + nh - 1)
                        for ch in range(8):
                            ins = tensor.matmul(
                                pp,
                                w1e[:, ch * 2048 + ot * 128: ch * 2048 + (ot + 1) * 128],
                                yT[:, ch * TPC + t * 128: ch * TPC + (t + 1) * 128],
                                start=(ch == 0), stop=(ch == 7))
                        C.inc(ins, "eh", 1)
                        nh += 1
                    tensor.wait_ge(C.sem("hc"), HC0 + nh)
                    po = pe[2 + (t % 2)]
                    if t >= 2:
                        tensor.wait_ge(C.sem("eac"), EAC0 + 2 * (t - 1))
                    for hf in range(2):
                        for ch in range(16):
                            ins = tensor.matmul(
                                po,
                                hb[:, ch * 128:(ch + 1) * 128],
                                w2e[:, ch * 1024 + hf * 512: ch * 1024 + hf * 512 + 512],
                                start=(ch == 0), stop=(ch == 15))
                        C.inc(ins, "eo", 1)
                        if hf == 0:
                            tensor.wait_ge(C.sem("eac"), EAC0 + 2 * t + 1)
                        po = pe[2 + (t % 2)]

            @block.scalar
            def _(scalar, e=e, EH0=C.n("eh") - 16 * NT, EO0e=C.n("eo") - 2 * NT):
                scalar = C.wrap(scalar, 's')
                nh = 0
                for t in range(NT):
                    hb = hTs[t % 2]
                    if t >= 2:
                        scalar.wait_ge(C.sem("eo"), EO0e + 2 * (t - 1))
                    for ot in range(16):
                        scalar.wait_ge(C.sem("eh"), EH0 + nh + 1)
                        scalar.activation(hb[:, ot * 128:(ot + 1) * 128],
                                          pe[ot % 2], AF.Relu,
                                          bias=b1s[:, e * 16 + ot: e * 16 + ot + 1])
                        C.inc(scalar.drain(), "hc", 1)
                        nh += 1

            @block.vector
            def _(vector, e=e, EO0=C.n("eo") - 2 * NT):
                vector = C.wrap(vector, 'v')
                n = 0
                for t in range(NT):
                    for hf in range(2):
                        vector.wait_ge(C.sem("eo"), EO0 + n + 1)
                        po = pe[2 + (t % 2)]
                        cwcol = cw[:, t * 8 + e: t * 8 + e + 1]
                        sl = slice(t * D + hf * 512, t * D + hf * 512 + 512)
                        vector.tensor_scalar(etmp[:], po, cwcol, None,
                                             ALU.mult)
                        vector.tensor_tensor(acc[:, sl], acc[:, sl], etmp[:],
                                             ALU.add)
                        C.inc(vector.drain(), "eac", 1)
                        n += 1
                C.inc(vector.drain(), "eoc", 1)

        pb2t = [po2[0], po2[1]]

        @block.tensor
        def _(tensor, EOC=C.n("eoc"), FB0=C.n("fb")):
            tensor.wait_ge(C.sem("eoc"), EOC)
            for t in range(NT):
                if t >= 2:
                    tensor.wait_ge(C.sem("fin%d" % (t % 2)), t // 2 - 1 + 1)
                pp = pb2t[t % 2]
                tensor.matmul(pp[:], cwT[0:8, t * 128:(t + 1) * 128],
                              b2b[:, 0:512], start=True, stop=True)
                C.inc(tensor.matmul(pproj[t % 2][:],
                                    cwT[0:8, t * 128:(t + 1) * 128],
                                    b2b[:, 512:1024], start=True, stop=True),
                      "fb", 1)

        @block.gpsimd
        def _(gpsimd, EOC=C.n("eoc"), ST0=C.n("rl_st0"), ST1=C.n("rl_st1")):
            gpsimd = C.wrap(gpsimd, 'g')
            gpsimd.wait_ge(C.sem("eoc"), EOC)
            gpsimd.wait_ge(C.sem("rl_st0"), ST0)
            gpsimd.wait_ge(C.sem("rl_st1"), ST1)
            gpsimd.wait_ge(C.sem("rl_tpc"), 2 * NT)
            for t in range(NT):
                if t >= 2:
                    gpsimd.wait_ge(C.sem("fin%d" % (t % 2)), t // 2)
                C.inc(gpsimd.dma_start(out=xr[t % 2][:],
                                       in_=x1o[t * 128:(t + 1) * 128, :]),
                      "xl%d" % (t % 2), 16)

        @block.vector
        def _(vector, FB0=C.n("fb") - NT, ST0=C.n("rl_st0"), ST1=C.n("rl_st1")):
            vector = C.wrap(vector, 'v')
            vector.wait_ge(C.sem("rl_st0"), ST0)
            vector.wait_ge(C.sem("rl_st1"), ST1)
            vector.wait_ge(C.sem("rl_tpc"), 2 * NT)
            for t in range(NT):
                ob = ocmb[t % 2]
                vector.wait_ge(C.sem("fb"), FB0 + t + 1)
                vector.wait_ge(C.sem("xl%d" % (t % 2)), 16 * (t // 2 + 1))
                if t >= 2:
                    vector.wait_ge(C.sem("fst%d" % (t % 2)), 16 * (t // 2))
                vector.tensor_tensor(ob[:, 0:512], xr[t % 2][:, 0:512],
                                     pb2t[t % 2][:], ALU.add)
                vector.tensor_tensor(ob[:, 512:1024], xr[t % 2][:, 512:1024],
                                     pproj[t % 2][:], ALU.add)
                C.inc(vector.tensor_tensor(ob[:], ob[:],
                                           acc[:, t * D:(t + 1) * D], ALU.add),
                      "fin%d" % (t % 2), 1)

        @block.sync
        def _(sync):
            nf = [0, 0]
            for t in range(NT):
                nf[t % 2] += 1
                sync.wait_ge(C.sem("fin%d" % (t % 2)), nf[t % 2])
                C.inc(sync.dma_start(out=x2o[t * 128:(t + 1) * 128, :],
                                     in_=ocmb[t % 2][:]), "fst%d" % (t % 2), 16)
            sync.wait_ge(C.sem("fst0"), C.n("fst0"))
            sync.wait_ge(C.sem("fst1"), C.n("fst1"))
            sync.wait_ge(C.sem("rl_st0"), C.n("rl_st0"))
            sync.wait_ge(C.sem("rl_st1"), C.n("rl_st1"))
    return nc


def build_F2():
    nc = bass.Bass("TRN2", target_bir_lowering=False)
    NT = TPC // 128
    HB = 512
    xd = nc.dram_tensor("xd", [TPC, D], F32, kind="ExternalInput").ap()
    oTd = nc.dram_tensor("oTd", [8 * NT * 128, 128], F32, kind="ExternalInput").ap()
    wod = nc.dram_tensor("wod", [D, D], F32, kind="ExternalInput").ap()
    w1bk = nc.dram_tensor("w1bk", [8 * 32 * 128, 128], F32, kind="ExternalInput").ap()
    w2bk = nc.dram_tensor("w2bk", [8 * 32 * 128, 128], F32, kind="ExternalInput").ap()
    h1bk = nc.dram_tensor("h1bk", [32 * 32 * 128, 128], F32, kind="ExternalInput").ap()
    h2bk = nc.dram_tensor("h2bk", [32 * 32 * 128, 128], F32, kind="ExternalInput").ap()
    w3bk = nc.dram_tensor("w3bk", [32 * 128, 1024], F32, kind="ExternalInput").ap()
    s1d = nc.dram_tensor("s1d", [128, 32], F32, kind="ExternalInput").ap()
    s2d = nc.dram_tensor("s2d", [128, 32], F32, kind="ExternalInput").ap()
    sh1d = nc.dram_tensor("sh1d", [128, 32], F32, kind="ExternalInput").ap()
    sh2d = nc.dram_tensor("sh2d", [128, 32], F32, kind="ExternalInput").ap()
    s3d = nc.dram_tensor("s3d", [1, 1024], F32, kind="ExternalInput").ap()
    x1o = nc.dram_tensor("x1o", [TPC, D], F32, kind="ExternalOutput").ap()
    x2o = nc.dram_tensor("x2o", [TPC, D], F32, kind="ExternalOutput").ap()
    ident_c = nc.inline_tensor(np.eye(128, dtype=BF), "ident_c")
    ones1_c = nc.inline_tensor(np.ones((1, 128), dtype=BF), "ones1_c")
    SQ2 = float(1.0 / np.sqrt(2.0))

    with ExitStack() as ctx:
        C = Ctx(nc, ctx)
        block = ctx.enter_context(nc.Block())
        C.ident = C.sb("ident", [128, 128], BF16)
        ones1 = C.sb("ones1", [1, 128], BF16)
        yT = C.sb("yT", [128, 8 * TPC], BF16)
        s1s = C.sb("s1s", [128, 32])
        s2s = C.sb("s2s", [128, 32])
        sh1s = C.sb("sh1s", [128, 32])
        sh2s = C.sb("sh2s", [128, 32])
        s3row = C.sb("s3row", [1, 1024], BF16)
        s3b = C.sb("s3b", [128, 1024])
        hT = C.sb("hT", [128, 32 * 512], BF16)
        gT = C.sb("gT", [128, 32 * 512], BF16)
        NHB = HB // 128
        xout = C.sb("xout", [128, NHB * D])
        wst = [C.sb("wstA", [128, 16 * 128], BF16),
               C.sb("wstB", [128, 16 * 128], BF16)]
        w2st = [C.sb("w2stA", [128, 32 * 128], BF16),
                C.sb("w2stB", [128, 32 * 128], BF16)]
        w3st = [C.sb("w3stA", [128, 1024], BF16), C.sb("w3stB", [128, 1024], BF16)]
        ev1 = C.sb("ev1", [128, HB])
        ev2 = C.sb("ev2", [128, HB])
        ev3 = C.sb("ev3", [128, HB])
        assert HB == 512

        @block.sync
        def _(sync):
            C.inc(sync.dma_start(out=C.ident[:], in_=ident_c.ap()), "cld", 16)
            C.inc(sync.dma_start(out=ones1[:], in_=ones1_c.ap()), "cld", 16)
            C.inc(sync.dma_start(out=s1s[:], in_=s1d), "cld", 16)
            C.inc(sync.dma_start(out=s2s[:], in_=s2d), "cld", 16)
            C.inc(sync.dma_start(out=sh1s[:], in_=sh1d), "cld", 16)
            C.inc(sync.dma_start(out=sh2s[:], in_=sh2d), "cld", 16)

        @block.gpsimd
        def _(gpsimd):
            gpsimd = C.wrap(gpsimd, 'g')
            C.inc(gpsimd.dma_start(out=s3row[:], in_=s3d), "cldg", 16)
        CLD = C.cnt["cld"]
        CLDG = C.cnt["cldg"]

        @block.tensor
        def _(tensor):
            tensor.wait_ge(C.sem("cld"), CLD)
            tensor.wait_ge(C.sem("cldg"), CLDG)

        @block.vector
        def _(vector):
            vector.wait_ge(C.sem("cld"), CLD)
            vector.wait_ge(C.sem("cldg"), CLDG)

        @block.scalar
        def _(scalar):
            scalar.wait_ge(C.sem("cld"), CLD)
            scalar.wait_ge(C.sem("cldg"), CLDG)

        ptps = [C.ps("ptpA", [128, 512], BF16), C.ps("ptpB", [128, 512], BF16)]
        pproj = [C.ps("pproj0", [128, 512]), C.ps("pproj1", [128, 512])]
        emit_resid_ln(C, block, xd, x1o, NT, yT, ptps, pproj, o_dram=oTd,
                      w_dram=wod, name="rl")
        xr = [C.rl_bufs[0], C.rl_bufs[1]]
        ocmb = [C.rl_bufs[2], C.rl_bufs[3]]

        pa = [C.ps("pa0", [128, 512]), C.ps("pa1", [128, 512])]
        pb = [C.ps("pb0", [128, 512]), C.ps("pb1", [128, 512])]

        @block.tensor
        def _(tensor, RLT=2 * NT):
            tensor.wait_ge(C.sem("rl_tpc"), RLT)
            tensor.matmul(pa[0][:], ones1[:], s3row[0:1, 0:512], start=True,
                          stop=True)
            C.inc(tensor.matmul(pa[1][:], ones1[:], s3row[0:1, 512:1024],
                                start=True, stop=True), "s3t", 1)

        @block.vector
        def _(vector):
            vector = C.wrap(vector, 'v')
            vector.wait_ge(C.sem("s3t"), 1)
            vector.tensor_copy(s3b[:, 0:512], pa[0][:])
            C.inc(vector.tensor_copy(s3b[:, 512:1024], pa[1][:]), "s3c", 1)

        for half in range(TPC // HB):
            tb = half * (HB // 512)
            WSH = [C.n("ws0"), C.n("ws1")]


            @block.gpsimd
            def _(gpsimd, half=half, EV0=C.n("ev")):
                gpsimd = C.wrap(gpsimd, 'g')
                for ot in range(32):
                    if ot >= 2:
                        gpsimd.wait_ge(C.sem("ev"), EV0 + ot - 1)
                    dst = wst[ot % 2]
                    for ch in range(8):
                        C.inc(gpsimd.dma_start(
                            out=dst[:, ch * 128:(ch + 1) * 128],
                            in_=w1bk[(ch * 32 + ot) * 128:(ch * 32 + ot + 1) * 128, :]),
                            "ws%d" % (ot % 2), 16)
                        C.inc(gpsimd.dma_start(
                            out=dst[:, (8 + ch) * 128:(9 + ch) * 128],
                            in_=w2bk[(ch * 32 + ot) * 128:(ch * 32 + ot + 1) * 128, :]),
                            "ws%d" % (ot % 2), 16)

            @block.tensor
            def _(tensor, half=half, tb=tb, WS0=WSH[0], WS1=WSH[1],
                  EV0=C.n("ev"), RLT=2 * NT,
                  S3C=C.n("s3c"), S4P=C.n("s4x")):
                tensor.wait_ge(C.sem("rl_tpc"), RLT)
                tensor.wait_ge(C.sem("s3c"), S3C)
                tensor.wait_ge(C.sem("s4x"), S4P)
                WSp = [WS0, WS1]
                for ot in range(32):
                    tensor.wait_ge(C.sem("ws%d" % (ot % 2)),
                                   WSp[ot % 2] + 256 * (ot // 2 + 1))
                    if ot >= 1:
                        tensor.wait_ge(C.sem("ev"), EV0 + ot)
                    src = wst[ot % 2]
                    for tc in range(HB // 512):
                        for ch in range(8):
                            tensor.matmul(
                                pa[tc][:], src[:, ch * 128:(ch + 1) * 128],
                                yT[:, ch * TPC + (tb + tc) * 512:
                                   ch * TPC + (tb + tc + 1) * 512],
                                start=(ch == 0), stop=(ch == 7))
                        for ch in range(8):
                            tensor.matmul(
                                pb[tc][:], src[:, (8 + ch) * 128:(9 + ch) * 128],
                                yT[:, ch * TPC + (tb + tc) * 512:
                                   ch * TPC + (tb + tc + 1) * 512],
                                start=(ch == 0), stop=(ch == 7))
                    C.inc(tensor.drain(), "st1", 1)

            @block.vector
            def _(vector, half=half, ST10=C.n("st1") - 32, SIL0=C.n("sil")):
                vector = C.wrap(vector, 'v')
                for ot in range(32):
                    vector.wait_ge(C.sem("st1"), ST10 + ot + 1)
                    s1c = s1s[:, ot:ot + 1]
                    s2c = s2s[:, ot:ot + 1]
                    vector.tensor_scalar(ev1[:], pa[0][:], s1c, None, ALU.mult)
                    C.inc(vector.drain(), "evs", 1)
                    vector.tensor_scalar(ev2[:], pb[0][:], s2c, None, ALU.mult)
                    vector.wait_ge(C.sem("sil"), SIL0 + ot + 1)
                    C.inc(vector.tensor_tensor(hT[:, ot * HB:(ot + 1) * HB],
                                               ev3[:], ev2[:], ALU.mult),
                          "ev", 1)

            @block.scalar
            def _(scalar, EVS0=C.n("evs") - 32):
                scalar = C.wrap(scalar, 's')
                for ot in range(32):
                    scalar.wait_ge(C.sem("evs"), EVS0 + ot + 1)
                    C.inc(scalar.activation(ev3[:], ev1[:], AF.Silu), "sil", 1)

            for stg, (wbk, scol, src_t, dst_t) in enumerate(
                    ((h1bk, sh1s, hT, gT), (h2bk, sh2s, gT, hT))):
                W20p = [C.n("w2s0"), C.n("w2s1")]

                @block.gpsimd
                def _(gpsimd, wbk=wbk, EV0=C.n("ge"), ST2A=C.n("st2")):
                    gpsimd = C.wrap(gpsimd, 'g')
                    gpsimd.wait_ge(C.sem("st2"), ST2A)
                    for ot in range(32):
                        if ot >= 2:
                            gpsimd.wait_ge(C.sem("ge"), EV0 + ot - 1)
                        dst = w2st[ot % 2]
                        for ch in range(32):
                            C.inc(gpsimd.dma_start(
                                out=dst[:, ch * 128:(ch + 1) * 128],
                                in_=wbk[(ch * 32 + ot) * 128:
                                        (ch * 32 + ot + 1) * 128, :]),
                                "w2s%d" % (ot % 2), 16)

                @block.tensor
                def _(tensor, src_t=src_t, W20p=W20p, EV0=C.n("ge"),
                      EVA=C.n("ev")):
                    tensor.wait_ge(C.sem("ev"), EVA)
                    tensor.wait_ge(C.sem("ge"), EV0)
                    for ot in range(32):
                        tensor.wait_ge(C.sem("w2s%d" % (ot % 2)),
                                       W20p[ot % 2] + 512 * (ot // 2 + 1))
                        if ot >= 1:
                            tensor.wait_ge(C.sem("ge"), EV0 + ot)
                        src = w2st[ot % 2]
                        for tc in range(HB // 512):
                            for ch in range(32):
                                tensor.matmul(
                                    pa[tc][:], src[:, ch * 128:(ch + 1) * 128],
                                    src_t[:, ch * HB + tc * 512:
                                          ch * HB + (tc + 1) * 512],
                                    start=(ch == 0), stop=(ch == 31))
                        C.inc(tensor.drain(), "st2", 1)

                @block.vector
                def _(vector, scol=scol, ST20=C.n("st2") - 32,
                      GE0=C.n("ge")):
                    vector = C.wrap(vector, 'v')
                    for ot in range(32):
                        vector.wait_ge(C.sem("st2"), ST20 + ot + 1)
                        sc = scol[:, ot:ot + 1]
                        if ot >= 1:
                            vector.wait_ge(C.sem("ge"), GE0 + ot)
                        vector.tensor_scalar(ev1[:], pa[0][:], sc, None,
                                             ALU.mult)
                        C.inc(vector.drain(), "evg", 1)

                @block.scalar
                def _(scalar, dst_t=dst_t, EVG0=C.n("evg") - 32):
                    scalar = C.wrap(scalar, 's')
                    for ot in range(32):
                        scalar.wait_ge(C.sem("evg"), EVG0 + ot + 1)
                        C.inc(scalar.activation(dst_t[:, ot * HB:(ot + 1) * HB],
                                                ev1[:], AF.Gelu), "ge", 1)

            @block.vector
            def _(vector):
                vector = C.wrap(vector, 'v')
                C.inc(vector.memset(xout[:], 0.0), "xz", 1)

            W30p = [C.n("w3s0"), C.n("w3s1")]

            @block.gpsimd
            def _(gpsimd, EV0=C.n("s4")):
                gpsimd = C.wrap(gpsimd, 'g')
                for ot in range(32):
                    if ot >= 2:
                        gpsimd.wait_ge(C.sem("s4"), EV0 + NHB * (ot - 1))
                    dst = w3st[ot % 2]
                    C.inc(gpsimd.dma_start(
                        out=dst[:],
                        in_=w3bk[ot * 128:(ot + 1) * 128, :]),
                        "w3s%d" % (ot % 2), 16)

            @block.tensor
            def _(tensor, W30p=W30p, GEA=C.n("ge"), XZ=C.n("xz"),
                  S40=C.n("s4x")):
                tensor.wait_ge(C.sem("ge"), GEA)
                tensor.wait_ge(C.sem("xz"), XZ)
                n = 0
                for ot in range(32):
                    tensor.wait_ge(C.sem("w3s%d" % (ot % 2)),
                                   W30p[ot % 2] + 16 * (ot // 2 + 1))
                    src = w3st[ot % 2]
                    for t in range(NHB):
                        if n >= 2:
                            tensor.wait_ge(C.sem("s4x"), S40 + n - 1)
                        pp = pb[n % 2]
                        tensor.matmul(pp[:],
                                      hT[:, ot * HB + t * 128: ot * HB + (t + 1) * 128],
                                      src[:, 0:512], start=True, stop=True)
                        C.inc(tensor.matmul(pa[n % 2][:],
                                            hT[:, ot * HB + t * 128:
                                               ot * HB + (t + 1) * 128],
                                            src[:, 512:1024], start=True,
                                            stop=True), "s4", 1)
                        n += 1

            @block.vector
            def _(vector, S40=C.n("s4") - 32 * (HB // 128)):
                vector = C.wrap(vector, 'v')
                n = 0
                for ot in range(32):
                    for t in range(NHB):
                        vector.wait_ge(C.sem("s4"), S40 + n + 1)
                        sl = slice(t * D, t * D + 512)
                        sl2 = slice(t * D + 512, t * D + 1024)
                        vector.tensor_tensor(xout[:, sl], xout[:, sl],
                                             pb[n % 2][:], ALU.add)
                        C.inc(vector.tensor_tensor(xout[:, sl2], xout[:, sl2],
                                                   pa[n % 2][:], ALU.add),
                              "s4x", 1)
                        n += 1

            XLH = [C.n("xl0"), C.n("xl1")]

            @block.gpsimd
            def _(gpsimd, half=half, S4X=C.n("s4x"), ST0=C.n("rl_st0"),
                  ST1=C.n("rl_st1")):
                gpsimd = C.wrap(gpsimd, 'g')
                gpsimd.wait_ge(C.sem("rl_st0"), ST0)
                gpsimd.wait_ge(C.sem("rl_st1"), ST1)
                gpsimd.wait_ge(C.sem("rl_tpc"), 2 * NT)
                gpsimd.wait_ge(C.sem("s4x"), S4X)
                FINb = [C.n("fin0"), C.n("fin1")]
                for t in range(NHB):
                    tg = half * NHB + t
                    if t >= 2:
                        gpsimd.wait_ge(C.sem("fin%d" % (t % 2)),
                                       FINb[t % 2] + t // 2)
                    C.inc(gpsimd.dma_start(out=xr[t % 2][:],
                                           in_=x1o[tg * 128:(tg + 1) * 128, :]),
                          "xl%d" % (t % 2), 16)

            @block.vector
            def _(vector, half=half, XL0=XLH[0], XL1=XLH[1],
                  FS0=C.n("fst0"), FS1=C.n("fst1")):
                vector = C.wrap(vector, 'v')
                XL = [XL0, XL1]
                FS = [FS0, FS1]
                for t in range(NHB):
                    pq = t % 2
                    vector.wait_ge(C.sem("xl%d" % pq), XL[pq] + 16 * (t // 2 + 1))
                    if t >= 2:
                        vector.wait_ge(C.sem("fst%d" % pq), FS[pq] + 16 * (t // 2))
                    else:
                        vector.wait_ge(C.sem("fst%d" % pq), FS[pq])
                    ob = ocmb[pq]
                    vector.tensor_tensor(ob[:], xout[:, t * D:(t + 1) * D],
                                         s3b[:], ALU.mult)
                    C.inc(vector.tensor_tensor(ob[:], ob[:], xr[pq][:], ALU.add),
                          "fin%d" % pq, 1)

            @block.sync
            def _(sync, half=half, FIN0=C.n("fin0"), FIN1=C.n("fin1")):
                FINb = [FIN0 - (NHB + 1) // 2, FIN1 - NHB // 2]
                for t in range(NHB):
                    pq = t % 2
                    tg = half * NHB + t
                    sync.wait_ge(C.sem("fin%d" % pq), FINb[pq] + t // 2 + 1)
                    C.inc(sync.dma_start(out=x2o[tg * 128:(tg + 1) * 128, :],
                                         in_=ocmb[pq][:]), "fst%d" % pq, 16)

        @block.sync
        def _(sync):
            sync.wait_ge(C.sem("fst0"), C.n("fst0"))
            sync.wait_ge(C.sem("fst1"), C.n("fst1"))
            sync.wait_ge(C.sem("rl_st0"), C.n("rl_st0"))
            sync.wait_ge(C.sem("rl_st1"), C.n("rl_st1"))
    return nc


def build_F0a():
    nc = bass.Bass("TRN2", target_bir_lowering=False)
    NT = TPC // 128
    xd = nc.dram_tensor("xd", [TPC, D], F32, kind="ExternalInput").ap()
    otd = nc.dram_tensor("otd", [TPC, D], F32, kind="ExternalInput").ap()
    wrd = nc.dram_tensor("wrd", [D, 32], F32, kind="ExternalInput").ap()
    x1o = nc.dram_tensor("x1o", [TPC, D], F32, kind="ExternalOutput").ap()
    yo = nc.dram_tensor("yo", [TPC, D], BF16, kind="ExternalOutput").ap()
    lgo = nc.dram_tensor("lgo", [TPC, 32], F32, kind="ExternalOutput").ap()
    cto = nc.dram_tensor("cto", [32, 1], F32, kind="ExternalOutput").ap()
    ident_c = nc.inline_tensor(np.eye(128, dtype=BF), "ident_c")
    ones_c = nc.inline_tensor(np.ones((128, 1), dtype=BF), "ones_c")

    with ExitStack() as ctx:
        C = Ctx(nc, ctx)
        block = ctx.enter_context(nc.Block())
        C.ident = C.sb("ident", [128, 128], BF16)
        onescol = C.sb("onescol", [128, 1], BF16)
        yT = C.sb("yT", [128, 8 * TPC], BF16)
        wrb = C.sb("wrb", [128, 8 * 32], BF16)
        lg2 = [C.sb("lgA", [128, 32]), C.sb("lgB", [128, 32])]
        m1 = C.sb("m1", [128, 1])
        ohb = C.sb("ohb", [128, 32], BF16)
        ctr = C.sb("ctr", [32, 1])

        @block.sync
        def _(sync):
            C.inc(sync.dma_start(out=C.ident[:], in_=ident_c.ap()), "cld", 16)
            C.inc(sync.dma_start(out=onescol[:], in_=ones_c.ap()), "cld", 16)

        @block.gpsimd
        def _(gpsimd):
            gpsimd = C.wrap(gpsimd, 'g')
            for c in range(8):
                C.inc(gpsimd.dma_start(out=wrb[:, c * 32:(c + 1) * 32],
                                       in_=wrd[c * 128:(c + 1) * 128, :]),
                      "cldg", 16)
        CLD = C.cnt["cld"]
        CLDG = C.cnt["cldg"]

        @block.tensor
        def _(tensor):
            tensor.wait_ge(C.sem("cld"), CLD)
            tensor.wait_ge(C.sem("cldg"), CLDG)

        @block.vector
        def _(vector):
            vector.wait_ge(C.sem("cld"), CLD)
            vector.wait_ge(C.sem("cldg"), CLDG)

        @block.scalar
        def _(scalar):
            scalar.wait_ge(C.sem("cld"), CLD)
            scalar.wait_ge(C.sem("cldg"), CLDG)

        ptps = [C.ps("ptpA", [128, 512], BF16), C.ps("ptpB", [128, 512], BF16)]
        pproj = [C.ps("pproj0", [128, 512]), C.ps("pproj1", [128, 512])]
        emit_resid_ln(C, block, xd, x1o, NT, yT, ptps, pproj, otok_dram=otd,
                      y_out=yo, name="rl")

        prt = pproj[0][:, 0:32]
        pct = pproj[1][0:32, 0:1]

        @block.tensor
        def _(tensor):
            tensor.wait_ge(C.sem("rl_tpc"), 2 * NT)
            for t in range(NT):
                if t >= 1:
                    tensor.wait_ge(C.sem("rtc"), t)
                for ch in range(8):
                    ins = tensor.matmul(
                        prt,
                        yT[:, ch * TPC + t * 128: ch * TPC + (t + 1) * 128],
                        wrb[:, ch * 32:(ch + 1) * 32],
                        start=(ch == 0), stop=(ch == 7))
                C.inc(ins, "rt", 1)
                tensor.wait_ge(C.sem("ohd"), t + 1)
                C.inc(tensor.matmul(pct, ohb[:], onescol[:],
                                    start=(t == 0), stop=(t == NT - 1)), "ct", 1)

        @block.vector
        def _(vector, RT0=C.n("rt") - NT, CT0=C.n("ct") - NT):
            vector = C.wrap(vector, 'v')
            for t in range(NT):
                vector.wait_ge(C.sem("rt"), RT0 + t + 1)
                lg = lg2[t % 2]
                if t >= 2:
                    vector.wait_ge(C.sem("lgs%d" % (t % 2)), 16 * (t // 2))
                vector.tensor_copy(lg[:], prt)
                C.inc(vector.drain(), "rtc", 1)
                vector.reduce_max(m1[:], lg[:], axis=AX)
                vector.drain()
                if t >= 1:
                    vector.wait_ge(C.sem("ct"), CT0 + t)
                C.inc(vector.tensor_scalar(ohb[:], lg[:], m1[:], 0.0,
                                           ALU.subtract, ALU.is_ge), "ohd", 1)

        @block.gpsimd
        def _(gpsimd, RTC0=C.n("rtc") - NT, OHD0=C.n("ohd") - NT):
            gpsimd = C.wrap(gpsimd, 'g')
            for t in range(NT):
                gpsimd.wait_ge(C.sem("ohd"), OHD0 + t + 1)
                C.inc(gpsimd.dma_start(out=lgo[t * 128:(t + 1) * 128, :],
                                       in_=lg2[t % 2][:]), "lgs%d" % (t % 2), 16)

        @block.vector
        def _(vector, CT=C.n("ct")):
            vector = C.wrap(vector, 'v')
            vector.wait_ge(C.sem("ct"), CT)
            C.inc(vector.tensor_copy(ctr[:], pct), "ctc", 1)

        @block.sync
        def _(sync):
            sync.wait_ge(C.sem("ctc"), 1)
            C.inc(sync.dma_start(out=cto, in_=ctr[:]), "cts", 16)
            sync.wait_ge(C.sem("cts"), 16)
            sync.wait_ge(C.sem("lgs0"), C.n("lgs0"))
            sync.wait_ge(C.sem("lgs1"), C.n("lgs1"))
            sync.wait_ge(C.sem("rl_st0"), C.n("rl_st0"))
            sync.wait_ge(C.sem("rl_st1"), C.n("rl_st1"))
    return nc


def build_F0b():
    nc = bass.Bass("TRN2", target_bir_lowering=False)
    NT = TPC // 128
    E, CL = 32, 256
    x1d = nc.dram_tensor("x1d", [TPC, D], F32, kind="ExternalInput").ap()
    yd = nc.dram_tensor("yd", [TPC, D], BF16, kind="ExternalInput").ap()
    lgd = nc.dram_tensor("lgd", [TPC, 32], F32, kind="ExternalInput").ap()
    cta = nc.dram_tensor("cta", [8, 32], F32, kind="ExternalInput").ap()
    mkc = nc.dram_tensor("mkc", [8, 1], F32, kind="ExternalInput").ap()
    wgd = nc.dram_tensor("wgd", [E * 1024, 1024], F32, kind="ExternalInput").ap()
    wud = nc.dram_tensor("wud", [E * 1024, 1024], F32, kind="ExternalInput").ap()
    wdd = nc.dram_tensor("wdd", [E * 1024, 1024], F32, kind="ExternalInput").ap()
    x2o = nc.dram_tensor("x2o", [TPC, D], F32, kind="ExternalOutput").ap()
    bufin = nc.dram_tensor("bufin", [E * CL + 128, D], BF16)
    bufout = nc.dram_tensor("bufout", [E * CL + 128, D], F32)
    ident_c = nc.inline_tensor(np.eye(128, dtype=BF), "ident_c")
    identf_c = nc.inline_tensor(np.eye(128, dtype=np.float32), "identf_c")
    tril_c = nc.inline_tensor(np.triu(np.ones((128, 128))).astype(BF), "tril_c")
    e256_c = nc.inline_tensor(np.broadcast_to(
        np.arange(32, dtype=np.float32) * CL, (128, 32)).copy(), "e256_c")

    with ExitStack() as ctx:
        C = Ctx(nc, ctx)
        block = ctx.enter_context(nc.Block())
        C.ident = C.sb("ident", [128, 128], BF16)
        identf = C.sb("identf", [128, 128])
        tril = C.sb("tril", [128, 128], BF16)
        e256 = C.sb("e256", [128, 32])
        cts = C.sb("cts", [8, 32], BF16)
        mkcs = C.sb("mkcs", [8, 1], BF16)
        offs = C.sb("offs", [32, 1])
        run = C.sb("run", [32, 1])
        lg2 = [C.sb("lgA", [128, 32]), C.sb("lgB", [128, 32])]
        m1 = C.sb("m1", [128, 1])
        esum = C.sb("esum", [128, 1])
        gate = C.sb("gate", [128, NT])
        elg = C.sb("elg", [128, 32])
        ohb = C.sb("ohb", [128, 32], BF16)
        oh32 = C.sb("oh32", [128, 32])
        cf = C.sb("cf", [32, 128])
        ct_t = C.sb("ct_t", [128, 32])
        gt_t = C.sb("gt_t", [128, 32])
        keep = C.sb("keep", [128, 32])
        tmp32 = C.sb("tmp32", [128, 32])
        dstf = C.sb("dstf", [128, 1])
        dstu = C.sb("dstu", [128, NT], U32)
        ytile = [C.sb("ytA", [128, D], BF16), C.sb("ytB", [128, D], BF16)]
        bufT = C.sb("bufT", [128, 8 * CL], BF16)
        midT = C.sb("midT", [128, 8 * CL], BF16)
        silt = C.sb("silt", [128, CL])
        wge = C.sb("wge", [128, 8 * 1024], BF16)
        wue = C.sb("wue", [128, 8 * 1024], BF16)
        wde = C.sb("wde", [128, 8 * 1024], BF16)
        obuf = [C.sb("obA", [128, D]), C.sb("obB", [128, D])]
        x1t = [C.sb("x1A", [128, D]), C.sb("x1B", [128, D])]
        gott = [C.sb("goA", [128, D]), C.sb("goB", [128, D])]

        @block.sync
        def _(sync):
            C.inc(sync.dma_start(out=C.ident[:], in_=ident_c.ap()), "cld", 16)
            C.inc(sync.dma_start(out=identf[:], in_=identf_c.ap()), "cld", 16)
            C.inc(sync.dma_start(out=tril[:], in_=tril_c.ap()), "cld", 16)
            C.inc(sync.dma_start(out=e256[:], in_=e256_c.ap()), "cld", 16)

        @block.gpsimd
        def _(gpsimd):
            gpsimd = C.wrap(gpsimd, 'g')
            C.inc(gpsimd.dma_start(out=cts[:], in_=cta), "cldg", 16)
            C.inc(gpsimd.dma_start(out=mkcs[:], in_=mkc), "cldg", 16)
        CLD = C.cnt["cld"]
        CLDG = C.cnt["cldg"]

        @block.tensor
        def _(tensor):
            tensor.wait_ge(C.sem("cld"), CLD)
            tensor.wait_ge(C.sem("cldg"), CLDG)

        @block.vector
        def _(vector):
            vector.wait_ge(C.sem("cld"), CLD)
            vector.wait_ge(C.sem("cldg"), CLDG)

        @block.scalar
        def _(scalar):
            scalar.wait_ge(C.sem("cld"), CLD)
            scalar.wait_ge(C.sem("cldg"), CLDG)

        pof = C.ps("pof", [32, 128])
        ptr = C.ps("ptr", [128, 128])

        @block.tensor
        def _(tensor):
            C.inc(tensor.matmul(pof[0:32, 0:1], cts[:], mkcs[:], start=True,
                                stop=True), "oft", 1)

        @block.vector
        def _(vector):
            vector = C.wrap(vector, 'v')
            vector.wait_ge(C.sem("oft"), 1)
            vector.tensor_copy(offs[:], pof[0:32, 0:1])
            vector.memset(run[:], 0.0)
            C.inc(vector.drain(), "ofc", 1)

        @block.sync
        def _(sync):
            for t in range(NT):
                if t >= 2:
                    sync.wait_ge(C.sem("exg"), t - 1)
                C.inc(sync.dma_start(out=lg2[t % 2][:],
                                     in_=lgd[t * 128:(t + 1) * 128, :]),
                      "lgl%d" % (t % 2), 16)

        PC0 = C.n("pc")
        TR10 = C.n("tr1")

        @block.vector
        def _(vector):
            vector = C.wrap(vector, 'v')
            vector.wait_ge(C.sem("ofc"), 1)
            for t in range(NT):
                vector.wait_ge(C.sem("lgl%d" % (t % 2)), 16 * (t // 2 + 1))
                vector.reduce_max(m1[:], lg2[t % 2][:], axis=AX)
                vector.drain()
                vector.tensor_scalar(ohb[:], lg2[t % 2][:], m1[:], 0.0,
                                     ALU.subtract, ALU.is_ge)
                vector.tensor_scalar(m1[:], m1[:], -1.0, None, ALU.mult)
                C.inc(vector.drain(), "ohd", 1)
                vector.wait_ge(C.sem("pc"), PC0 + t + 1)
                vector.tensor_scalar(cf[:], pof[:], run[:], None, ALU.add)
                vector.drain()
                C.inc(vector.tensor_copy(run[:], cf[:, 127:128]), "cfd", 1)
                vector.wait_ge(C.sem("tr1"), TR10 + t + 1)
                vector.tensor_scalar(cf[:], cf[:], offs[:], None, ALU.add)
                C.inc(vector.drain(), "cfd", 1)
                vector.wait_ge(C.sem("exg"), t + 1)
                vector.reciprocal(gate[:, t:t + 1], esum[:])
                vector.drain()
                vector.wait_ge(C.sem("trp"), 2 * (t + 1))
                vector.tensor_copy(ct_t[:], ptr[:, 0:32])
                vector.tensor_copy(gt_t[:], ptr[:, 64:96])
                C.inc(vector.drain(), "trc", 1)
                vector.tensor_copy(oh32[:], ohb[:])
                vector.tensor_scalar(keep[:], gt_t[:], 640.0, None, ALU.is_le)
                vector.tensor_tensor(keep[:], keep[:], oh32[:], ALU.mult)
                vector.tensor_scalar(tmp32[:], ct_t[:], float(CL), None,
                                     ALU.is_le)
                vector.tensor_tensor(keep[:], keep[:], tmp32[:], ALU.mult)
                vector.tensor_scalar(tmp32[:], ct_t[:], -1.0, None, ALU.add)
                vector.tensor_tensor(tmp32[:], tmp32[:], e256[:], ALU.add)
                vector.tensor_tensor(tmp32[:], tmp32[:], keep[:], ALU.mult)
                vector.reduce_sum(dstf[:], tmp32[:], axis=AX)
                vector.reduce_sum(m1[:], keep[:], axis=AX)
                vector.drain()
                vector.tensor_scalar(m1[:], m1[:], -float(E * CL),
                                     float(E * CL), ALU.mult, ALU.add)
                vector.tensor_tensor(dstf[:], dstf[:], m1[:], ALU.add)
                C.inc(vector.tensor_copy(dstu[:, t:t + 1], dstf[:]), "dst", 1)

        @block.scalar
        def _(scalar, OHD0=C.n("ohd") - NT):
            scalar = C.wrap(scalar, 's')
            for t in range(NT):
                scalar.wait_ge(C.sem("ohd"), OHD0 + t + 1)
                C.inc(scalar.activation(elg[:], lg2[t % 2][:], AF.Exp,
                                        bias=m1[:], accum_out=esum[:]),
                      "exg", 1)

        @block.tensor
        def _(tensor, OHD0=C.n("ohd") - NT, TRC0=C.n("trc") - NT):
            for t in range(NT):
                tensor.wait_ge(C.sem("ohd"), OHD0 + t + 1)
                C.inc(tensor.matmul(pof[:], ohb[:], tril[:], start=True,
                                    stop=True), "pc", 1)
                tensor.wait_ge(C.sem("cfd"), 2 * t + 1)
                C.inc(tensor.transpose(ptr[:, 0:32], cf[:], identf[0:32, 0:32]),
                      "tr1", 1)
                tensor.wait_ge(C.sem("cfd"), 2 * t + 2)
                C.inc(tensor.transpose(ptr[:, 64:96], cf[:],
                                       identf[0:32, 0:32]), "trp", 2)
                if t < NT - 1:
                    tensor.wait_ge(C.sem("trc"), TRC0 + t + 1)


        @block.gpsimd
        def _(gpsimd, DST0=C.n("dst") - NT):
            gpsimd = C.wrap(gpsimd, 'g')
            gpsimd.memset(ytile[0][:], 0.0)
            gpsimd.drain()
            for r in range(65):
                C.inc(gpsimd.dma_start(out=bufin.ap()[r * 128:(r + 1) * 128, :],
                                       in_=ytile[0][:]), "bz", 16)
            gpsimd.wait_ge(C.sem("bz"), 65 * 16)
            for t in range(NT):
                C.inc(gpsimd.dma_start(out=ytile[t % 2][:],
                                       in_=yd[t * 128:(t + 1) * 128, :]),
                      "yl%d" % (t % 2), 16)
                gpsimd.wait_ge(C.sem("dst"), DST0 + t + 1)
                gpsimd.wait_ge(C.sem("yl%d" % (t % 2)), 16 * (t // 2 + 1))
                gpsimd.wait_ge(C.sem("sct"), 16 * t)
                C.inc(gpsimd.indirect_dma_start(
                    out=bufin.ap(), out_offset=IndirectOffsetOnAxis(
                        dstu[:, t:t + 1], 0),
                    in_=ytile[t % 2][:], in_offset=None), "sct", 16)

        pg = [C.ps("pg0", [128, CL]), C.ps("pg1", [128, CL])]
        pdo = [C.ps("pd0", [128, 512]), C.ps("pd1", [128, 512])]
        ptb = C.ps("ptb", [128, 128], BF16)

        for e in range(E):
            EWL = C.n("ewl")

            @block.gpsimd
            def _(gpsimd, e=e, PREV=C.n("eod")):
                gpsimd = C.wrap(gpsimd, 'g')
                gpsimd.wait_ge(C.sem("eod"), PREV)
                if e == 0:
                    gpsimd.wait_ge(C.sem("sct"), C.n("sct"))
                for ch in range(8):
                    C.inc(gpsimd.dma_start(
                        out=wge[:, ch * 1024:(ch + 1) * 1024],
                        in_=wgd[e * 1024 + ch * 128: e * 1024 + (ch + 1) * 128, :]),
                        "ewl", 16)
                    C.inc(gpsimd.dma_start(
                        out=wue[:, ch * 1024:(ch + 1) * 1024],
                        in_=wud[e * 1024 + ch * 128: e * 1024 + (ch + 1) * 128, :]),
                        "ewl", 16)
                    C.inc(gpsimd.dma_start(
                        out=wde[:, ch * 1024:(ch + 1) * 1024],
                        in_=wdd[e * 1024 + ch * 128: e * 1024 + (ch + 1) * 128, :]),
                        "ewl", 16)
                for half in range(2):
                    C.inc(gpsimd.dma_start(
                        out=ytile[half][:],
                        in_=bufin.ap()[e * CL + half * 128:
                                       e * CL + (half + 1) * 128, :]),
                        "ebl", 16)
            EWL_T = C.n("ewl")
            EBL_T = C.n("ebl")

            @block.tensor
            def _(tensor, e=e, EWL_T=EWL_T, EBL_T=EBL_T, BTC0=C.n("btc"),
                  MDC0=C.n("mdc"), EVO0=C.n("evo")):
                tensor.wait_ge(C.sem("ebl"), EBL_T)
                if e == 0:
                    tensor.wait_ge(C.sem("trc"), C.n("trc"))
                for half in range(2):
                    for f in range(8):
                        i = half * 8 + f
                        if i >= 1:
                            tensor.wait_ge(C.sem("btc"), BTC0 + i)
                        tensor.transpose(ptb[:], ytile[half][:, f * 128:(f + 1) * 128],
                                         C.ident[:])
                        C.inc(tensor.drain(), "btr", 1)
                tensor.wait_ge(C.sem("ewl"), EWL_T)
                tensor.wait_ge(C.sem("btc"), BTC0 + 16)
                for ft in range(8):
                    for which, wsb in ((0, wge), (1, wue)):
                        pp = pg[which]
                        for ch in range(8):
                            ins = tensor.matmul(
                                pp[:],
                                wsb[:, ch * 1024 + ft * 128: ch * 1024 + (ft + 1) * 128],
                                bufT[:, ch * CL: ch * CL + CL],
                                start=(ch == 0), stop=(ch == 7))
                    C.inc(ins, "gu", 1)
                    tensor.wait_ge(C.sem("mdc"), MDC0 + ft + 1)
                for half in range(2):
                    for hf in range(2):
                        pp = pdo[hf]
                        for ch in range(8):
                            ins = tensor.matmul(
                                pp[:],
                                midT[:, ch * CL + half * 128: ch * CL + half * 128 + 128],
                                wde[:, ch * 1024 + hf * 512: ch * 1024 + hf * 512 + 512],
                                start=(ch == 0), stop=(ch == 7))
                        C.inc(ins, "dn", 1)
                    tensor.wait_ge(C.sem("evo"), EVO0 + half + 1)

            @block.scalar
            def _(scalar, e=e, BTR0=C.n("btr") - 16, GU0=C.n("gu") - 8):
                scalar = C.wrap(scalar, 's')
                for i in range(16):
                    scalar.wait_ge(C.sem("btr"), BTR0 + i + 1)
                    half, f = i // 8, i % 8
                    scalar.copy(bufT[:, f * CL + half * 128: f * CL + half * 128 + 128],
                                ptb[:])
                    C.inc(scalar.drain(), "btc", 1)
                for ft in range(8):
                    scalar.wait_ge(C.sem("gu"), GU0 + ft + 1)
                    C.inc(scalar.activation(silt[:], pg[0][:], AF.Silu),
                          "sil", 1)

            @block.vector
            def _(vector, e=e, SIL0=C.n("sil") - 8, DN0=C.n("dn") - 4):
                vector = C.wrap(vector, 'v')
                for ft in range(8):
                    vector.wait_ge(C.sem("sil"), SIL0 + ft + 1)
                    C.inc(vector.tensor_tensor(
                        midT[:, ft * CL:(ft + 1) * CL], silt[:], pg[1][:],
                        ALU.mult), "mdc", 1)
                for half in range(2):
                    for hf in range(2):
                        vector.wait_ge(C.sem("dn"), DN0 + half * 2 + hf + 1)
                        vector.tensor_copy(
                            ytile[0][:, hf * 512:(hf + 1) * 512]
                            if False else obuf[half][:, hf * 512:(hf + 1) * 512],
                            pdo[hf][:])
                    C.inc(vector.drain(), "evo", 1)

            @block.gpsimd
            def _(gpsimd, e=e, EVO0=C.n("evo") - 2):
                gpsimd = C.wrap(gpsimd, 'g')
                for half in range(2):
                    gpsimd.wait_ge(C.sem("evo"), EVO0 + half + 1)
                    C.inc(gpsimd.dma_start(
                        out=bufout.ap()[e * CL + half * 128:
                                        e * CL + (half + 1) * 128, :],
                        in_=obuf[half][:]), "eod", 16)

        XGH = [C.n("xg0"), C.n("xg1")]

        @block.gpsimd
        def _(gpsimd, EOD=C.n("eod"), DST0=C.n("dst") - NT):
            gpsimd = C.wrap(gpsimd, 'g')
            gpsimd.memset(x1t[0][:], 0.0)
            gpsimd.drain()
            C.inc(gpsimd.dma_start(out=bufout.ap()[E * CL:E * CL + 128, :],
                                   in_=x1t[0][:]), "zr", 16)
            gpsimd.wait_ge(C.sem("eod"), EOD)
            gpsimd.wait_ge(C.sem("zr"), 16)
            for t in range(NT):
                if t >= 2:
                    gpsimd.wait_ge(C.sem("fin%d" % (t % 2)), t // 2)
                    gpsimd.wait_ge(C.sem("fst%d" % (t % 2)), 16 * (t // 2))
                C.inc(gpsimd.dma_start(out=x1t[t % 2][:],
                                       in_=x1d[t * 128:(t + 1) * 128, :]),
                      "xg%d" % (t % 2), 16)
                C.inc(gpsimd.dma_start(out=ytile[t % 2][:],
                                       in_=yd[t * 128:(t + 1) * 128, :]),
                      "xg%d" % (t % 2), 16)
                C.inc(gpsimd.indirect_dma_start(
                    out=gott[t % 2][:], out_offset=None,
                    in_=bufout.ap(),
                    in_offset=IndirectOffsetOnAxis(dstu[:, t:t + 1], 0)),
                    "xg%d" % (t % 2), 16)

        @block.vector
        def _(vector):
            vector = C.wrap(vector, 'v')
            XG = XGH
            for t in range(NT):
                pq = t % 2
                vector.wait_ge(C.sem("xg%d" % pq), XG[pq] + 48 * (t // 2 + 1))
                if t >= 2:
                    vector.wait_ge(C.sem("fst%d" % pq), 16 * (t // 2))
                gcol = gate[:, t:t + 1]
                vector.tensor_scalar(gott[pq][:], gott[pq][:], gcol, None,
                                     ALU.mult)
                vector.tensor_tensor(gott[pq][:], gott[pq][:], ytile[pq][:],
                                     ALU.add)
                C.inc(vector.tensor_tensor(gott[pq][:], gott[pq][:],
                                           x1t[pq][:], ALU.add),
                      "fin%d" % pq, 1)

        @block.sync
        def _(sync):
            nf = [0, 0]
            for t in range(NT):
                pq = t % 2
                nf[pq] += 1
                sync.wait_ge(C.sem("fin%d" % pq), nf[pq])
                C.inc(sync.dma_start(out=x2o[t * 128:(t + 1) * 128, :],
                                     in_=gott[pq][:]), "fst%d" % pq, 16)
            sync.wait_ge(C.sem("fst0"), C.n("fst0"))
            sync.wait_ge(C.sem("fst1"), C.n("fst1"))
    return nc


_NC_CACHE = {}


def _get(name, fn):
    if name not in _NC_CACHE:
        _NC_CACHE[name] = fn()
    return _NC_CACHE[name]


def _blockT(oT_slice):
    return np.ascontiguousarray(
        oT_slice.reshape(8, 128, 16, 128).transpose(0, 2, 1, 3).reshape(-1, 128))


def kernel(**inputs):
    I = {k: np.asarray(v) for k, v in inputs.items()}
    x0 = np.ascontiguousarray(I["L0_attn_x"], dtype=np.float32)
    AC = np.ascontiguousarray

    def run(nc, maps):
        return run_bass_kernel_spmd(nc, maps, core_ids=list(range(8)))

    def hslice(w, p, base=0):
        h0 = p * 8
        return AC(w[:, base + h0 * 64: base + (h0 + 8) * 64])

    W = I["L0_attn_W_kqv"]
    m = I["L0_attn_m"].reshape(16)
    nc = _get("A0", build_A0)
    maps = []
    for c in range(8):
        b, p = c // 2, c % 2
        maps.append(dict(
            xb=AC(x0[b]), wk=hslice(W, p, 0), wq=hslice(W, p, 1024),
            wv=hslice(W, p, 2048),
            mh=AC(m[p * 8:(p + 1) * 8].reshape(4, 2).T)))
    res = run(nc, maps)
    o0 = np.zeros((4, S, D), np.float32)
    for c in range(8):
        b, p = c // 2, c % 2
        o0[b][:, p * 512:(p + 1) * 512] = np.asarray(res.results[c]["oT"]).T

    nc = _get("F0a", build_F0a)
    maps = []
    for c in range(8):
        b, p = c // 2, c % 2
        sl = slice(p * TPC, (p + 1) * TPC)
        maps.append(dict(xd=AC(x0[b][sl]), otd=AC(o0[b][sl]),
                         wrd=AC(I["L0_ffn_w_router"])))
    res = run(nc, maps)
    x1 = [np.asarray(res.results[c]["x1o"]) for c in range(8)]
    yy = [np.asarray(res.results[c]["yo"]) for c in range(8)]
    lgs = [np.asarray(res.results[c]["lgo"]) for c in range(8)]
    cts = np.stack([np.asarray(res.results[c]["cto"]).ravel() for c in range(8)])

    nc = _get("F0b", build_F0b)
    maps = []
    for c in range(8):
        mk = np.zeros((8, 1), np.float32)
        mk[:c] = 1.0
        maps.append(dict(
            x1d=x1[c], yd=yy[c], lgd=lgs[c], cta=AC(cts.astype(np.float32)),
            mkc=mk,
            wgd=AC(I["L0_ffn_w_gate"].reshape(32 * 1024, 1024)),
            wud=AC(I["L0_ffn_w_up"].reshape(32 * 1024, 1024)),
            wdd=AC(I["L0_ffn_w_down"].reshape(32 * 1024, 1024))))
    res = run(nc, maps)
    x2 = [np.asarray(res.results[c]["x2o"]) for c in range(8)]

    Wq1 = I["L1_attn_w_qkv"]
    nc = _get("A1", build_A1)
    maps = []
    for c in range(8):
        b, p = c // 2, c % 2
        xb = np.concatenate([x2[2 * b], x2[2 * b + 1]], axis=0)
        maps.append(dict(xb=AC(xb), wq=hslice(Wq1, p, 0),
                         wk=hslice(Wq1, p, 1024), wv=hslice(Wq1, p, 2048)))
    res = run(nc, maps)
    o1T = np.zeros((4, D, S), np.float32)
    for c in range(8):
        b, p = c // 2, c % 2
        o1T[b][p * 512:(p + 1) * 512] = np.asarray(res.results[c]["oT"])

    w8 = I["L1_ffn_weight"]
    w8u = w8.view(np.uint8) if w8.dtype != np.uint8 else w8
    nc = _get("F1", build_F1)
    maps = []
    for c in range(8):
        b, p = c // 2, c % 2
        sl = slice(p * TPC, (p + 1) * TPC)
        maps.append(dict(
            xd=x2[c], oTd=_blockT(o1T[b][:, sl]),
            wod=AC(I["L1_attn_w_out"]),
            w8d=AC(w8u.T),
            wsd=np.full((128, 1), float(I["L1_ffn_weight_scale"]), np.float32),
            bd=AC(I["L1_ffn_bias"].reshape(1, D))))
    res = run(nc, maps)
    x3 = [np.asarray(res.results[c]["x2o"]) for c in range(8)]

    Wq2 = I["L2_attn_Wq"].reshape(D, 1024)
    Wku = I["L2_attn_Wk_up"].reshape(256, 1024)
    Wvu = I["L2_attn_Wv_up"].reshape(256, 1024)
    nc = _get("A2", build_A2)
    maps = []
    for c in range(8):
        b, p = c // 2, c % 2
        xb = np.concatenate([x3[2 * b], x3[2 * b + 1]], axis=0)
        maps.append(dict(xb=AC(xb), wq=hslice(Wq2, p),
                         wkvd=AC(I["L2_attn_Wkv_down"]),
                         wku=hslice(Wku, p), wvu=hslice(Wvu, p)))
    res = run(nc, maps)
    o2T = np.zeros((4, D, S), np.float32)
    for c in range(8):
        b, p = c // 2, c % 2
        o2T[b][p * 512:(p + 1) * 512] = np.asarray(res.results[c]["oT"])

    def blk4096(w):
        wt = AC(w.T)
        ci, co = wt.shape
        return AC(wt.reshape(ci // 128, 128, co // 128, 128)
                  .transpose(0, 2, 1, 3).reshape(-1, 128))
    nc = _get("F2", build_F2)
    f2w = dict(
        wod=AC(I["L2_attn_Wout"].reshape(1024, D)),
        w1bk=blk4096(I["L2_ffn_w1_w"]), w2bk=blk4096(I["L2_ffn_w2_w"]),
        h1bk=blk4096(I["L2_ffn_h1_w"]), h2bk=blk4096(I["L2_ffn_h2_w"]),
        w3bk=AC(I["L2_ffn_w3_w"].T),
        s1d=AC(I["L2_ffn_w1_s"].reshape(32, 128).T),
        s2d=AC(I["L2_ffn_w2_s"].reshape(32, 128).T),
        sh1d=AC(I["L2_ffn_h1_s"].reshape(32, 128).T),
        sh2d=AC(I["L2_ffn_h2_s"].reshape(32, 128).T),
        s3d=AC(I["L2_ffn_w3_s"].reshape(1, 1024)))
    maps = []
    for c in range(8):
        b, p = c // 2, c % 2
        sl = slice(p * TPC, (p + 1) * TPC)
        maps.append(dict(xd=x3[c], oTd=_blockT(o2T[b][:, sl]), **f2w))
    res = run(nc, maps)
    x4 = [np.asarray(res.results[c]["x2o"]) for c in range(8)]

    nc = _get("A3", build_A3)
    maps = []
    for c in range(8):
        b, p = c // 2, c % 2
        xb = np.concatenate([x4[2 * b], x4[2 * b + 1]], axis=0)
        maps.append(dict(xb=AC(xb), wq=hslice(I["L3_attn_Wq"], p),
                         wk=hslice(I["L3_attn_Wk"], p),
                         wv=hslice(I["L3_attn_Wv"], p)))
    res = run(nc, maps)
    o3T = np.zeros((4, D, S), np.float32)
    for c in range(8):
        b, p = c // 2, c % 2
        o3T[b][p * 512:(p + 1) * 512] = np.asarray(res.results[c]["oT"])

    nc = _get("F3", build_F3)
    f3w = dict(
        wod=AC(I["L3_attn_Wo"]),
        wrd=AC(I["L3_ffn_Wr"]),
        brd=AC(I["L3_ffn_br"].reshape(1, 8)),
        w1d=AC(I["L3_ffn_W1"].reshape(8 * 1024, 2048)),
        b1d=AC(I["L3_ffn_b1"].reshape(8, 16, 128).transpose(2, 0, 1)
               .reshape(128, 128)),
        w2d=AC(I["L3_ffn_W2"].reshape(8 * 2048, 1024)),
        b2d=AC(I["L3_ffn_b2"]))
    maps = []
    for c in range(8):
        b, p = c // 2, c % 2
        sl = slice(p * TPC, (p + 1) * TPC)
        maps.append(dict(xd=x4[c], oTd=_blockT(o3T[b][:, sl]), **f3w))
    res = run(nc, maps)
    out = np.zeros((4, S, D), np.float32)
    for c in range(8):
        b, p = c // 2, c % 2
        out[b][p * TPC:(p + 1) * TPC] = np.asarray(res.results[c]["x2o"])
    return out


# version 04 (final)
# speedup vs baseline: 1.0013x; 1.0013x over previous
import numpy as np
import ml_dtypes
import concourse.bass as bass
import concourse.mybir as mybir
from concourse.bass import IndirectOffsetOnAxis
from concourse.bass_utils import run_bass_kernel_spmd
from contextlib import ExitStack

F32 = mybir.dt.float32
BF16 = mybir.dt.bfloat16
FP8 = mybir.dt.float8e4
U32 = mybir.dt.uint32
U8 = mybir.dt.uint8
AX = mybir.AxisListType.X
AF = mybir.ActivationFunctionType
ALU = mybir.AluOpType

BF = ml_dtypes.bfloat16
NEG = -1.0e30

D = 1024
S = 4096
TPC = 2048


def _ap_name(a):
    t = getattr(a, "tensor", None)
    if t is not None:
        return getattr(t, "name", None)
    return getattr(a, "name", None)


class EngProxy:

    _OUT1 = {"tensor_tensor", "tensor_scalar", "tensor_copy", "reduce_max",
             "reduce_sum", "tensor_reduce", "reciprocal", "copy", "select",
             "activation", "scalar_tensor_tensor", "iota", "memset",
             "affine_select", "transpose", "partition_broadcast"}

    def __init__(self, eng, dirty):
        self._e = eng
        self._d = dirty

    def __getattr__(self, name):
        fn = getattr(self._e, name)
        if name in ("wait_ge", "nop", "engine_nop", "dma_start",
                    "indirect_dma_start", "dma_start_transpose", "matmul",
                    "then_inc", "sem_inc", "collective_compute"):
            return fn
        if name == "drain":
            def drain():
                self._d[0].clear()
                self._d[1].clear()
                return self._e.drain()
            return drain
        if name not in self._OUT1:
            return fn

        def wrapped(*args, **kwargs):
            vals = list(args) + list(kwargs.values())
            outs = []
            if args:
                outs.append(args[0])
            if "out" in kwargs:
                outs.append(kwargs["out"])
            if "accum_out" in kwargs and kwargs["accum_out"] is not None:
                outs.append(kwargs["accum_out"])
            out_ids = {id(v) for v in outs}
            ins = [v for v in vals if v is not None and id(v) not in out_ids]
            names_in = {n for n in (_ap_name(v) for v in ins) if n}
            names_out = {n for n in (_ap_name(v) for v in outs) if n}
            wr, rd = self._d
            if (names_in & wr) or (names_out & wr) or (names_out & rd):
                self._e.drain()
                wr.clear()
                rd.clear()
            r = fn(*args, **kwargs)
            wr |= names_out
            rd |= names_in
            return r
        return wrapped


class Ctx:

    def __init__(self, nc, ctx):
        self.nc = nc
        self.ctx = ctx
        self.sems = {}
        self.cnt = {}
        self.dirty = {}

    def sem(self, name):
        if name not in self.sems:
            self.sems[name] = self.ctx.enter_context(self.nc.semaphore(name))
            self.cnt[name] = 0
        return self.sems[name]

    def inc(self, instr, name, amt=1):
        instr.then_inc(self.sem(name), amt)
        self.cnt[name] += amt
        return self.cnt[name]

    def n(self, name):
        return self.cnt.get(name, 0)

    def sb(self, name, shape, dtype=F32):
        return self.ctx.enter_context(self.nc.sbuf_tensor(name, shape, dtype))

    def ps(self, name, shape, dtype=F32):
        return self.ctx.enter_context(self.nc.psum_tensor(name, shape, dtype))

    def wrap(self, eng, kind):
        return EngProxy(eng, self.dirty.setdefault(kind, [set(), set()]))


def emit_ln_yT(C, block, x_dram, n_tiles, yT, ptps, y_tok=None, x_sb=None,
               prev_oT=None, name="ln"):
    ntok = n_tiles * 128
    xa = C.sb(name + "_xa", [128, D])
    xb2 = C.sb(name + "_xb", [128, D])
    mean = C.sb(name + "_m", [128, 1])
    var = C.sb(name + "_v", [128, 1])
    sdv = C.sb(name + "_sd", [128, 1])
    inv = C.sb(name + "_i", [128, 1])
    eps = C.sb(name + "_eps", [128, 1])
    sq = C.sb(name + "_sq", [128, 512])
    var2 = C.sb(name + "_v2", [128, 1])
    yb0 = C.sb(name + "_yb0", [128, D], BF16)
    yb1 = C.sb(name + "_yb1", [128, D], BF16)

    @block.sync
    def _(sync):
        for t in range(n_tiles):
            buf = xa if t % 2 == 0 else xb2
            if t >= 2:
                sync.wait_ge(C.sem(name + "_y"), t - 1)
            C.inc(sync.dma_start(out=buf[:], in_=x_dram[t * 128:(t + 1) * 128, :]),
                  name + "_ld%d" % (t % 2), 16)

    @block.vector
    def _(vector):
        vector = C.wrap(vector, 'v')
        C.inc(vector.memset(eps[:], 1.0e-5), name + "_eps", 1)
        for t in range(n_tiles):
            buf = xa if t % 2 == 0 else xb2
            yb = yb0 if t % 2 == 0 else yb1
            vector.wait_ge(C.sem(name + "_ld%d" % (t % 2)), 16 * (t // 2 + 1))
            if x_sb is not None:
                vector.tensor_copy(x_sb[:, t * D:(t + 1) * D], buf[:])
            vector.reduce_sum(mean[:], buf[:], axis=AX)
            vector.drain()
            vector.tensor_scalar(mean[:], mean[:], 1.0 / D, None, ALU.mult)
            C.inc(vector.tensor_scalar(buf[:], buf[:], mean[:], None, ALU.subtract),
                  name + "_xc", 1)
            vector.wait_ge(C.sem(name + "_sv"), t + 1)
            vector.reciprocal(inv[:], sdv[:])
            vector.drain()
            if t >= 2:
                vector.wait_ge(C.sem(name + "_tp"), 2 * (t - 1))
            C.inc(vector.tensor_scalar(yb[:], buf[:], inv[:], None, ALU.mult),
                  name + "_y", 1)
            if y_tok is not None:
                vector.tensor_copy(y_tok[:, t * D:(t + 1) * D], yb[:])

    def _copy_groups(scalar, t):
        for g in range(2):
            gi = 2 * t + g
            scalar.wait_ge(C.sem(name + "_tp"), gi + 1)
            ptp = ptps[gi % 2]
            for f4 in range(4):
                f = g * 4 + f4
                scalar.copy(
                    yT[:, f * ntok + t * 128: f * ntok + (t + 1) * 128],
                    ptp[:, f4 * 128: f4 * 128 + 128])
            C.inc(scalar.drain(), name + "_tpc", 1)

    @block.scalar
    def _(scalar):
        scalar = C.wrap(scalar, 's')
        scalar.wait_ge(C.sem(name + "_eps"), 1)
        for t in range(n_tiles):
            scalar.wait_ge(C.sem(name + "_xc"), t + 1)
            buf = xa if t % 2 == 0 else xb2
            scalar.activation(sq[:], buf[:, 0:512], AF.Square, accum_out=var[:])
            scalar.activation(sq[:], buf[:, 512:1024], AF.Square,
                              accum_out=var2[:])
            scalar.drain()
            scalar.activation(var[:], var[:], AF.Relu, bias=var2[:])
            C.inc(scalar.activation(sdv[:], var[:], AF.Sqrt, scale=1.0 / D,
                                    bias=eps[:]), name + "_sv", 1)
            if t >= 1:
                _copy_groups(scalar, t - 1)
        _copy_groups(scalar, n_tiles - 1)

    @block.tensor
    def _(tensor):
        for t in range(n_tiles):
            tensor.wait_ge(C.sem(name + "_y"), t + 1)
            yb = yb0 if t % 2 == 0 else yb1
            for g in range(2):
                gi = 2 * t + g
                if gi >= 2:
                    tensor.wait_ge(C.sem(name + "_tpc"), gi - 1)
                ptp = ptps[gi % 2]
                for f4 in range(4):
                    f = g * 4 + f4
                    tensor.transpose(
                        ptp[:, f4 * 128: f4 * 128 + 128],
                        yb[:, f * 128:(f + 1) * 128], C.ident[:])
                C.inc(tensor.drain(), name + "_tp", 1)




def build_A0(npairs=4, nheads=2, ln_tiles=32, do_w=True):
    nc = bass.Bass("TRN2", target_bir_lowering=False)
    xb = nc.dram_tensor("xb", [S, D], F32, kind="ExternalInput").ap()
    wk = nc.dram_tensor("wk", [D, 512], F32, kind="ExternalInput").ap()
    wq = nc.dram_tensor("wq", [D, 512], F32, kind="ExternalInput").ap()
    wv = nc.dram_tensor("wv", [D, 512], F32, kind="ExternalInput").ap()
    mh = nc.dram_tensor("mh", [2, 4], F32, kind="ExternalInput").ap()
    oT = nc.dram_tensor("oT", [512, S], F32, kind="ExternalOutput").ap()

    ident_c = nc.inline_tensor(np.eye(128, dtype=BF), "ident_c")
    iota_c = nc.inline_tensor(
        np.broadcast_to(np.arange(512, dtype=np.float32), (2, 512)).copy(),
        "iota_c")
    mvs = []
    for v in range(4):
        r = np.arange(128)[:, None]
        f = np.arange(512)[None, :]
        mvs.append(np.where(f < 128 * v + r, NEG, 0.0).astype(np.float32))
    mask_c = nc.inline_tensor(np.concatenate(mvs, axis=1).astype(BF), "mask_c")
    ones64_c = nc.inline_tensor(np.ones((1, 64), dtype=np.float32), "ones64_c")
    bones_c = nc.inline_tensor(
        np.concatenate([np.concatenate([np.ones((64, 1)), np.zeros((64, 1))], 1),
                        np.concatenate([np.zeros((64, 1)), np.ones((64, 1))], 1)],
                       0).astype(np.float32), "bones_c")

    NQC = S // 512

    with ExitStack() as ctx:
        C = Ctx(nc, ctx)
        C.ident = C.sb("ident", [128, 128], BF16)
        rowsb = C.sb("rowsb", [128, S], BF16)
        rowsb2 = C.sb("rowsb2", [32, S], BF16)
        iota512 = C.sb("iota512", [2, 512])
        ach = C.sb("ach", [2, 512])
        bch = C.sb("bch", [2, 512])
        cch = C.sb("cch", [2, 512])
        masks = C.sb("masks", [128, 2048], BF16)
        ones64 = C.sb("ones64", [1, 64])
        bones = C.sb("bones", [128, 2])
        smh = C.sb("smh", [2, 4])
        negma = C.sb("negma", [2, 4])
        yT = C.sb("yT", [128, 8 * S], BF16)

        block = ctx.enter_context(nc.Block())

        @block.sync
        def _(sync):
            C.inc(sync.dma_start(out=C.ident[:], in_=ident_c.ap()), "cld", 16)
            C.inc(sync.dma_start(out=iota512[:], in_=iota_c.ap()), "cld", 16)
            C.inc(sync.dma_start(out=masks[:], in_=mask_c.ap()), "cld", 16)
            C.inc(sync.dma_start(out=ones64[:], in_=ones64_c.ap()), "cld", 16)
            C.inc(sync.dma_start(out=bones[:], in_=bones_c.ap()), "cld", 16)
            C.inc(sync.dma_start(out=smh[:], in_=mh), "cld", 16)

        @block.tensor
        def _(tensor):
            tensor.wait_ge(C.sem("cld"), C.cnt["cld"])

        @block.vector
        def _(vector):
            vector = C.wrap(vector, 'v')
            vector.wait_ge(C.sem("cld"), C.cnt["cld"])

        @block.scalar
        def _(scalar):
            scalar = C.wrap(scalar, 's')
            scalar.wait_ge(C.sem("cld"), C.cnt["cld"])
            C.inc(scalar.activation(negma[:], smh[:], AF.Relu, scale=-1.0),
                  "negm", 1)

        ptps = [C.ps("ptpA", [128, 512], BF16), C.ps("ptpB", [128, 512], BF16)]
        emit_ln_yT(C, block, xb, ln_tiles, yT, ptps, name="ln")

        wkb = C.sb("wkb", [128, 8 * 512], BF16)
        wqb = C.sb("wqb", [128, 8 * 512], BF16)
        wvb = C.sb("wvb", [128, 8 * 512], BF16)

        @block.gpsimd
        def _(gpsimd):
            gpsimd = C.wrap(gpsimd, 'g')
            if not do_w:
                return
            for (w_d, w_s) in ((wk, wkb), (wq, wqb), (wv, wvb)):
                for c in range(8):
                    C.inc(gpsimd.dma_start(out=w_s[:, c * 512:(c + 1) * 512],
                                           in_=w_d[c * 128:(c + 1) * 128, :]),
                          "wld", 16)
        WLD = C.n("wld")

        kaug = [C.sb("kaug0", [69, S], BF16), C.sb("kaug1", [69, S], BF16)]
        qaug = [C.sb("qaug0", [69, S], BF16), C.sb("qaug1", [69, S], BF16)]
        vtok = [C.sb("vtok0", [128, 32 * 65], BF16),
                C.sb("vtok1", [128, 32 * 65], BF16)]
        rhib1 = rowsb[0:2, :]
        rhib2 = rowsb[32:34, :]
        mkhi = rowsb[64:66, :]
        mkmid = rowsb[96:98, :]
        mklo = rowsb2[0:2, :]
        negm = C.sb("negm", [2, 1])
        kmax = C.sb("kmax", [2, 1])
        sqv = C.sb("sqv", [128, 512])
        pT = [C.sb("pT0", [128, 512], BF16), C.sb("pT1", [128, 512], BF16)]
        stmp = [C.sb("stmp0", [128, 512]), C.sb("stmp1", [128, 512])]
        obuf = C.sb("obuf", [64, 512])
        pbcs = C.sb("pbcs", [64, 512])
        rrow = cch[0:1, :]

        pproj = [C.ps("pproj0", [128, 512]), C.ps("pproj1", [128, 512])]
        pshared = C.ps("pshared", [64, 512])
        pnrm = pshared[0:2, :]
        pbc = pshared[0:64, :]
        ps2 = [C.ps("ps0", [128, 512]), C.ps("ps1", [128, 512])]
        ppv = C.ps("ppv", [65, 512])

        for hp in range(npairs):
            PJ0 = C.n("pj"); PJC0 = C.n("pjc"); SQ0 = C.n("sq")
            NRM0 = C.n("nrm")

            @block.tensor
            def _(tensor, hp=hp, PJC0=PJC0, SQ0=SQ0, ON_PREV=C.n("on")):
                tensor.wait_ge(C.sem("wld"), WLD)
                tensor.wait_ge(C.sem("ln_tpc"), 64)
                tensor.wait_ge(C.sem("on"), ON_PREV)

                def nrm_mm(jn):
                    tensor.wait_ge(C.sem("sq"), SQ0 + jn + 1)
                    if jn >= 1:
                        tensor.wait_ge(C.sem("pjc"), PJC0 + jn)
                    C.inc(tensor.matmul(pnrm, bones[:], sqv[:], start=True,
                                        stop=True), "nrm", 1)
                j = 0
                for which, wsb in (("k", wkb), ("q", wqb)):
                    for tc in range(NQC):
                        if j >= 2:
                            tensor.wait_ge(C.sem("pjc"), PJC0 + j - 1)
                            tensor.wait_ge(C.sem("sq"), SQ0 + j - 1)
                        pp = pproj[j % 2]
                        for ch in range(8):
                            ins = tensor.matmul(
                                pp[:],
                                wsb[:, ch * 512 + hp * 128: ch * 512 + hp * 128 + 128],
                                yT[:, ch * S + tc * 512: ch * S + (tc + 1) * 512],
                                start=(ch == 0), stop=(ch == 7))
                        C.inc(ins, "pj", 1)
                        j += 1
                        if j >= 2 and j - 2 < 2 * NQC:
                            nrm_mm(j - 2)
                for tt in range(32):
                    if j >= 2:
                        tensor.wait_ge(C.sem("pjc"), PJC0 + j - 1)
                        tensor.wait_ge(C.sem("sq"), SQ0 + min(j - 1, 2 * NQC))
                    pp = pproj[j % 2]
                    for ch in range(8):
                        ins = tensor.matmul(
                            pp[:, 0:128],
                            yT[:, ch * S + tt * 128: ch * S + (tt + 1) * 128],
                            wvb[:, ch * 512 + hp * 128: ch * 512 + hp * 128 + 128],
                            start=(ch == 0), stop=(ch == 7))
                    C.inc(ins, "pj", 1)
                    j += 1
                    if j - 2 < 2 * NQC:
                        nrm_mm(j - 2)

            UC0 = C.n("uc"); US0 = C.n("us")

            @block.scalar
            def _(scalar, PJ0=PJ0, NRM0=NRM0, UC0=UC0):
                scalar = C.wrap(scalar, 's')
                def square(j):
                    scalar.wait_ge(C.sem("pj"), PJ0 + j + 1)
                    if j >= 1:
                        scalar.wait_ge(C.sem("nrm"), NRM0 + j)
                    pp = pproj[j % 2]
                    C.inc(scalar.activation(sqv[:], pp[:], AF.Square), "sq", 1)
                for j in range(NQC + 1):
                    square(j)
                for tc in range(NQC):
                    scalar.wait_ge(C.sem("uc"), UC0 + tc + 1)
                    C.inc(scalar.activation(ach[:], ach[:], AF.Sqrt,
                                            scale=1.0 / 1024.0), "us", 1)
                    if NQC + 1 + tc < 2 * NQC:
                        square(NQC + 1 + tc)


            @block.vector
            def _(vector, hp=hp, PJ0=PJ0, NRM0=NRM0, US0=US0):
                vector = C.wrap(vector, 'v')
                mh2 = smh[:, hp:hp + 1]
                nm2 = negma[:, hp:hp + 1]
                vector.wait_ge(C.sem("negm"), 1)
                vector.memset(kmax[:], 0.0)
                j = 0
                for wi, which in enumerate(("k", "q")):
                    dst = kaug if which == "k" else qaug
                    for tc in range(NQC):
                        vector.wait_ge(C.sem("pj"), PJ0 + j + 1)
                        pp = pproj[j % 2]
                        if which == "q":
                            vector.tensor_scalar(
                                dst[0][0:64, tc * 512:(tc + 1) * 512],
                                pp[0:64, :], 1.0 / 32.0, None, ALU.mult)
                            vector.tensor_scalar(
                                dst[1][0:64, tc * 512:(tc + 1) * 512],
                                pp[64:128, :], 1.0 / 32.0, None, ALU.mult)
                        else:
                            vector.tensor_copy(dst[0][0:64, tc * 512:(tc + 1) * 512],
                                               pp[0:64, :])
                            vector.tensor_copy(dst[1][0:64, tc * 512:(tc + 1) * 512],
                                               pp[64:128, :])
                        vector.wait_ge(C.sem("nrm"), NRM0 + j + 1)
                        if which == "k":
                            vector.reduce_max(bch[:, 0:1], pnrm, axis=AX)
                            vector.drain()
                            ins = vector.tensor_tensor(kmax[:], kmax[:],
                                                       bch[:, 0:1], ALU.max)
                        else:
                            sl = slice(tc * 512, (tc + 1) * 512)
                            vector.tensor_copy(ach[:], pnrm)
                            C.inc(vector.tensor_scalar(ach[:], ach[:], kmax[:],
                                                       None, ALU.mult), "uc", 1)
                            vector.tensor_scalar(bch[:], iota512[:],
                                                 float(512 * tc), None, ALU.add)
                            vector.wait_ge(C.sem("us"), US0 + tc + 1)
                            vector.tensor_scalar(cch[:], bch[:], nm2, None,
                                                 ALU.mult)
                            vector.tensor_tensor(ach[:], ach[:], cch[:], ALU.add)
                            vector.tensor_scalar(cch[:], bch[:], mh2, None,
                                                 ALU.mult)
                            vector.tensor_tensor(ach[:], ach[:], cch[:], ALU.add)
                            vector.tensor_scalar(ach[:], ach[:], -1.0, None,
                                                 ALU.mult)
                            vector.tensor_copy(rhib1[:, sl], ach[:])
                            vector.tensor_copy(cch[:], rhib1[:, sl])
                            vector.tensor_tensor(ach[:], ach[:], cch[:],
                                                 ALU.subtract)
                            vector.tensor_copy(rhib2[:, sl], ach[:])
                            vector.tensor_scalar(cch[:], bch[:], mh2, None,
                                                 ALU.mult)
                            vector.tensor_copy(mkhi[:, sl], cch[:])
                            vector.tensor_copy(ach[:], mkhi[:, sl])
                            vector.tensor_tensor(cch[:], cch[:], ach[:],
                                                 ALU.subtract)
                            vector.tensor_copy(mkmid[:, sl], cch[:])
                            vector.tensor_copy(ach[:], mkmid[:, sl])
                            vector.tensor_tensor(cch[:], cch[:], ach[:],
                                                 ALU.subtract)
                            ins = vector.tensor_copy(mklo[:, sl], cch[:])
                        C.inc(ins, "pjc", 1)
                        j += 1
                for tt in range(32):
                    vector.wait_ge(C.sem("pj"), PJ0 + j + 1)
                    pp = pproj[j % 2]
                    vector.tensor_copy(vtok[0][:, tt * 65: tt * 65 + 64],
                                       pp[:, 0:64])
                    vector.tensor_copy(vtok[1][:, tt * 65: tt * 65 + 64],
                                       pp[:, 64:128])
                    C.inc(vector.drain(), "pjc", 1)
                    j += 1
                vector.memset(vtok[0][:, 64::65], 1.0)
                vector.memset(vtok[1][:, 64::65], 1.0)
                C.inc(vector.drain(), "vdone", 1)

            @block.vector
            def _(vector):
                vector = C.wrap(vector, 'v')
                vector.memset(qaug[0][64:69, :], 1.0)
                vector.memset(qaug[1][64:69, :], 1.0)
                vector.memset(kaug[0][64:69, :], 1.0)
                vector.memset(kaug[1][64:69, :], 1.0)
                C.inc(vector.drain(), "aug", 1)

            @block.gpsimd
            def _(gpsimd, AUG_V=None, PJC_V=None):
                gpsimd = C.wrap(gpsimd, 'g')
                AUG_V = C.n("aug")
                PJC_V = PJC0 + 2 * NQC
                gpsimd.wait_ge(C.sem("aug"), AUG_V)
                gpsimd.wait_ge(C.sem("pjc"), PJC_V)
                for i in range(2):
                    C.inc(gpsimd.dma_start(out=kaug[i][64:65, :],
                                           in_=rowsb[64 + i:65 + i, :]), "augd", 16)
                    C.inc(gpsimd.dma_start(out=kaug[i][65:66, :],
                                           in_=rowsb[96 + i:97 + i, :]), "augd", 16)
                    C.inc(gpsimd.dma_start(out=kaug[i][66:67, :],
                                           in_=rowsb2[0 + i:1 + i, :]), "augd", 16)
                    C.inc(gpsimd.dma_start(out=qaug[i][67:68, :],
                                           in_=rowsb[0 + i:1 + i, :]), "augd", 16)
                    C.inc(gpsimd.dma_start(out=qaug[i][68:69, :],
                                           in_=rowsb[32 + i:33 + i, :]), "augd", 16)

            AUG_T = C.n("augd")
            VD_T = C.n("vdone")

            for hi in range(nheads):
                ka, qa, vt = kaug[hi], qaug[hi], vtok[hi]
                h_abs = hp * 2 + hi
                S0 = C.n("s"); E0 = C.n("exp"); M0 = C.n("msk")
                P0 = C.n("pvd"); R0 = C.n("rec"); B0 = C.n("bc")
                O0 = C.n("on")
                OSp = [C.n("osc0"), C.n("osc1")]

                def g_of(qc, tk):
                    return sum(4 * q + 4 for q in range(qc)) + tk

                @block.tensor
                def _(tensor, ka=ka, qa=qa, vt=vt, g_of=g_of, AUG_T=AUG_T,
                      VD_T=VD_T, E0=E0, R0=R0, O0=O0):
                    tensor.wait_ge(C.sem("augd"), AUG_T)
                    tensor.wait_ge(C.sem("vdone"), VD_T)
                    tensor.wait_ge(C.sem("exp"), E0)
                    tensor.wait_ge(C.sem("on"), O0)
                    for qc in range(NQC):
                        ntk = 4 * qc + 4
                        if qc > 0:
                            tensor.wait_ge(C.sem("on"), O0 + qc)
                        for tk in range(ntk):
                            gi = g_of(qc, tk)
                            if gi >= 2:
                                tensor.wait_ge(C.sem("exp"), E0 + gi - 1)
                            C.inc(tensor.matmul(ps2[gi % 2][:],
                                          ka[:, tk * 128:(tk + 1) * 128],
                                          qa[:, qc * 512:(qc + 1) * 512],
                                          start=True, stop=True), "s", 1)
                            if tk >= 1:
                                tensor.wait_ge(C.sem("exp"), E0 + gi)
                                tensor.matmul(ppv[:],
                                              vt[:, (tk - 1) * 65:(tk - 1) * 65 + 65],
                                              pT[(gi - 1) % 2][:],
                                              start=(tk == 1), stop=False)
                        gl = g_of(qc, ntk - 1)
                        tensor.wait_ge(C.sem("exp"), E0 + gl + 1)
                        C.inc(tensor.matmul(
                            ppv[:], vt[:, (ntk - 1) * 65:(ntk - 1) * 65 + 65],
                            pT[gl % 2][:], start=False, stop=True), "pvd", 1)
                        tensor.wait_ge(C.sem("rec"), R0 + qc + 1)
                        C.inc(tensor.matmul(pbc, ones64[:], rrow, start=True,
                                            stop=True), "bc", 1)

                @block.scalar
                def _(scalar, g_of=g_of, S0=S0, M0=M0):
                    scalar = C.wrap(scalar, 's')
                    for qc in range(NQC):
                        ntk = 4 * qc + 4
                        for tk in range(ntk):
                            gi = g_of(qc, tk)
                            scalar.wait_ge(C.sem("s"), S0 + gi + 1)
                            if tk >= 4 * qc:
                                v = tk - 4 * qc
                                scalar.wait_ge(C.sem("msk"), M0 + 4 * qc + v + 1)
                                C.inc(scalar.activation(pT[gi % 2][:],
                                                        stmp[v % 2][:],
                                                        AF.Exp), "exp", 1)
                            else:
                                C.inc(scalar.activation(pT[gi % 2][:],
                                                        ps2[gi % 2][:], AF.Exp),
                                      "exp", 1)

                @block.vector
                def _(vector, g_of=g_of, S0=S0, E0=E0, P0=P0, B0=B0, OSp=OSp):
                    vector = C.wrap(vector, 'v')
                    vector.wait_ge(C.sem("exp"), E0)
                    vector.wait_ge(C.sem("osc0"), OSp[0])
                    vector.wait_ge(C.sem("osc1"), OSp[1])
                    for qc in range(NQC):
                        ntk = 4 * qc + 4
                        for tk in range(4 * qc, ntk):
                            gi = g_of(qc, tk)
                            v = tk - 4 * qc
                            vector.wait_ge(C.sem("s"), S0 + gi + 1)
                            if v >= 2:
                                vector.wait_ge(C.sem("exp"),
                                               E0 + g_of(qc, 4 * qc + v - 2) + 1)
                            elif qc > 0:
                                vector.wait_ge(C.sem("exp"),
                                               E0 + g_of(qc - 1,
                                                         4 * (qc - 1) + v + 2) + 1)
                            C.inc(vector.tensor_tensor(
                                stmp[v % 2][:], ps2[gi % 2][:],
                                masks[:, v * 512:(v + 1) * 512], ALU.add),
                                "msk", 1)
                        vector.wait_ge(C.sem("pvd"), P0 + qc + 1)
                        vector.reciprocal(rrow, ppv[64:65, :])
                        C.inc(vector.drain(), "rec", 1)
                        vector.wait_ge(C.sem("bc"), B0 + qc + 1)
                        vector.tensor_copy(pbcs[:], pbc)
                        if qc > 0:
                            pq = (qc - 1) % 2
                            vector.wait_ge(C.sem("osc%d" % pq),
                                           OSp[pq] + 16 * ((qc - 1) // 2 + 1))
                        C.inc(vector.tensor_tensor(obuf[:], ppv[0:64, :],
                                                    pbcs[:], ALU.mult), "on", 1)

                @block.sync
                def _(sync, h_abs=h_abs, O0=O0):
                    for qc in range(NQC):
                        sync.wait_ge(C.sem("on"), O0 + qc + 1)
                        C.inc(sync.dma_start(
                            out=oT[h_abs * 64:(h_abs + 1) * 64,
                                   qc * 512:(qc + 1) * 512],
                            in_=obuf[:]), "osc%d" % (qc % 2), 16)

        @block.sync
        def _(sync):
            sync.wait_ge(C.sem("osc0"), C.n("osc0"))
            sync.wait_ge(C.sem("osc1"), C.n("osc1"))

    return nc



_A0_CACHE = {}


def run_A0(x, W_kqv, m):
    if "nc" not in _A0_CACHE:
        _A0_CACHE["nc"] = build_A0()
    nc = _A0_CACHE["nc"]
    in_maps = []
    for c in range(8):
        b, p = c // 2, c % 2
        h0 = p * 8
        in_maps.append(dict(
            xb=np.ascontiguousarray(x[b]),
            wk=np.ascontiguousarray(W_kqv[:, h0 * 64:(h0 + 8) * 64]),
            wq=np.ascontiguousarray(W_kqv[:, 1024 + h0 * 64: 1024 + (h0 + 8) * 64]),
            wv=np.ascontiguousarray(W_kqv[:, 2048 + h0 * 64: 2048 + (h0 + 8) * 64]),
            mh=np.ascontiguousarray(m.reshape(16)[h0:h0 + 8].reshape(4, 2).T),
        ))
    res = run_bass_kernel_spmd(nc, in_maps, core_ids=list(range(8)))
    out = np.zeros((4, S, D), np.float32)
    for c in range(8):
        b, p = c // 2, c % 2
        oT = np.asarray(res.results[c]["oT"])
        out[b][:, p * 512:(p + 1) * 512] = oT.T
    return out






def emit_consts_attn(C, nc, block, extra=()):
    ident_c = nc.inline_tensor(np.eye(128, dtype=BF), "ident_c")
    mvs, m01 = [], []
    for v in range(4):
        r = np.arange(128)[:, None]
        f = np.arange(512)[None, :]
        bad = f < 128 * v + r
        mvs.append(np.where(bad, NEG, 0.0).astype(np.float32))
        m01.append(np.where(bad, 0.0, 1.0).astype(np.float32))
    mask_c = nc.inline_tensor(np.concatenate(mvs, axis=1).astype(BF), "mask_c")
    mask01_c = nc.inline_tensor(np.concatenate(m01, axis=1).astype(BF), "mask01_c")
    ones64_c = nc.inline_tensor(np.ones((1, 64), dtype=np.float32), "ones64_c")
    C.ident = C.sb("ident", [128, 128], BF16)
    masks = C.sb("masks", [128, 2048], BF16)
    masks01 = C.sb("masks01", [128, 2048], BF16)
    ones64 = C.sb("ones64", [1, 64])

    @block.sync
    def _(sync):
        C.inc(sync.dma_start(out=C.ident[:], in_=ident_c.ap()), "cld", 16)
        C.inc(sync.dma_start(out=masks[:], in_=mask_c.ap()), "cld", 16)
        C.inc(sync.dma_start(out=masks01[:], in_=mask01_c.ap()), "cld", 16)
        C.inc(sync.dma_start(out=ones64[:], in_=ones64_c.ap()), "cld", 16)

    @block.tensor
    def _(tensor):
        tensor.wait_ge(C.sem("cld"), C.cnt["cld"])

    @block.vector
    def _(vector):
        vector.wait_ge(C.sem("cld"), C.cnt["cld"])

    @block.scalar
    def _(scalar):
        scalar.wait_ge(C.sem("cld"), C.cnt["cld"])
    return dict(masks=masks, masks01=masks01, ones64=ones64)


def emit_wcast(C, block, pairs, name="wld"):
    @block.gpsimd
    def _(gpsimd):
        gpsimd = C.wrap(gpsimd, 'g')
        for (w_d, w_s, rows, cols) in pairs:
            for c in range(rows // 128):
                C.inc(gpsimd.dma_start(
                    out=w_s[:, c * cols:(c + 1) * cols],
                    in_=w_d[c * 128:(c + 1) * 128, :]), name, 16)
    return C.n(name)


def emit_proj_pair(C, block, hp, yT, wsb_list, dsts, vtok, ntok, pproj,
                   scale_q=None, ctr_chunks=8, src_w=512):
    NQ = ntok // 512
    PJ0 = C.n("pj")
    PJC0 = C.n("pjc")
    wk_sb, wq_sb, wv_sb = wsb_list
    k0, k1, q0, q1 = dsts

    @block.tensor
    def _(tensor):
        j = 0
        for wsb in (wk_sb, wq_sb):
            for tc in range(NQ):
                if j >= 2:
                    tensor.wait_ge(C.sem("pjc"), PJC0 + j - 1)
                pp = pproj[j % 2]
                for ch in range(ctr_chunks):
                    ins = tensor.matmul(
                        pp[:],
                        wsb[:, ch * src_w + hp * 128: ch * src_w + hp * 128 + 128],
                        yT[:, ch * ntok + tc * 512: ch * ntok + (tc + 1) * 512],
                        start=(ch == 0), stop=(ch == ctr_chunks - 1))
                C.inc(ins, "pj", 1)
                j += 1
        for tt in range(ntok // 128):
            if j >= 2:
                tensor.wait_ge(C.sem("pjc"), PJC0 + j - 1)
            pp = pproj[j % 2]
            for ch in range(ctr_chunks):
                ins = tensor.matmul(
                    pp[:, 0:128],
                    yT[:, ch * ntok + tt * 128: ch * ntok + (tt + 1) * 128],
                    wv_sb[:, ch * src_w + hp * 128: ch * src_w + hp * 128 + 128],
                    start=(ch == 0), stop=(ch == ctr_chunks - 1))
            C.inc(ins, "pj", 1)
            j += 1

    vst = 65 if vtok[0] is not None and C.vones else 64

    @block.vector
    def _(vector):
        vector = C.wrap(vector, 'v')
        j = 0
        for wi, dst in enumerate(((k0, k1), (q0, q1))):
            for tc in range(NQ):
                vector.wait_ge(C.sem("pj"), PJ0 + j + 1)
                pp = pproj[j % 2]
                sl = slice(tc * 512, (tc + 1) * 512)
                if wi == 1 and scale_q is not None:
                    vector.tensor_scalar(dst[0][0:64, sl], pp[0:64, :],
                                         scale_q, None, ALU.mult)
                    vector.tensor_scalar(dst[1][0:64, sl], pp[64:128, :],
                                         scale_q, None, ALU.mult)
                else:
                    vector.tensor_copy(dst[0][0:64, sl], pp[0:64, :])
                    vector.tensor_copy(dst[1][0:64, sl], pp[64:128, :])
                C.inc(vector.drain(), "pjc", 1)
                j += 1
        for tt in range(ntok // 128):
            vector.wait_ge(C.sem("pj"), PJ0 + j + 1)
            pp = pproj[j % 2]
            vector.tensor_copy(vtok[0][:, tt * vst: tt * vst + 64], pp[:, 0:64])
            vector.tensor_copy(vtok[1][:, tt * vst: tt * vst + 64], pp[:, 64:128])
            C.inc(vector.drain(), "pjc", 1)
            j += 1
        if vst == 65:
            vector.memset(vtok[0][:, 64::65], 1.0)
            vector.memset(vtok[1][:, 64::65], 1.0)
        C.inc(vector.drain(), "vdone", 1)
    return C.n("vdone")


def causal_softmax_head(C, block, consts, kp, qp, vt, oT, h_abs, ps2, ppv,
                        pbc, stmp, pT, obuf, pbcs, rrow, softmax=True):
    NQC = S // 512
    masks = consts["masks"]
    masks01 = consts["masks01"]
    ones64 = consts["ones64"]
    S0 = C.n("s"); E0 = C.n("exp"); P0 = C.n("pvd"); R0 = C.n("rec")
    B0 = C.n("bc"); O0 = C.n("on"); M0 = C.n("msk")
    OSp = [C.n("osc0"), C.n("osc1")]
    vst = 65 if softmax else 64

    def g_of(qc, tk):
        return sum(4 * q + 4 for q in range(qc)) + tk

    @block.tensor
    def _(tensor):
        tensor.wait_ge(C.sem("vdone"), C.n("vdone"))
        tensor.wait_ge(C.sem("exp"), E0)
        tensor.wait_ge(C.sem("on"), O0)
        for qc in range(NQC):
            ntk = 4 * qc + 4
            if qc > 0:
                tensor.wait_ge(C.sem("on"), O0 + qc)
            for tk in range(ntk):
                gi = g_of(qc, tk)
                if gi >= 2:
                    tensor.wait_ge(C.sem("exp"), E0 + gi - 1)
                C.inc(tensor.matmul(ps2[gi % 2][:],
                                    kp[:, tk * 128:(tk + 1) * 128],
                                    qp[:, qc * 512:(qc + 1) * 512],
                                    start=True, stop=True), "s", 1)
                if tk >= 1:
                    tensor.wait_ge(C.sem("exp"), E0 + gi)
                    tensor.matmul(ppv[0:vst, :],
                                  vt[:, (tk - 1) * vst:(tk - 1) * vst + vst],
                                  pT[(gi - 1) % 2][:],
                                  start=(tk == 1), stop=False)
            gl = g_of(qc, ntk - 1)
            tensor.wait_ge(C.sem("exp"), E0 + gl + 1)
            C.inc(tensor.matmul(ppv[0:vst, :],
                                vt[:, (ntk - 1) * vst:(ntk - 1) * vst + vst],
                                pT[gl % 2][:], start=False, stop=True), "pvd", 1)
            if softmax:
                tensor.wait_ge(C.sem("rec"), R0 + qc + 1)
                C.inc(tensor.matmul(pbc, ones64[:], rrow, start=True,
                                    stop=True), "bc", 1)

    if softmax:
        @block.scalar
        def _(scalar):
            scalar = C.wrap(scalar, 's')
            for qc in range(NQC):
                ntk = 4 * qc + 4
                for tk in range(ntk):
                    gi = g_of(qc, tk)
                    scalar.wait_ge(C.sem("s"), S0 + gi + 1)
                    if tk >= 4 * qc:
                        v = tk - 4 * qc
                        scalar.wait_ge(C.sem("msk"), M0 + 4 * qc + v + 1)
                        C.inc(scalar.activation(pT[gi % 2][:], stmp[v % 2][:],
                                                AF.Exp), "exp", 1)
                    else:
                        C.inc(scalar.activation(pT[gi % 2][:],
                                                ps2[gi % 2][:], AF.Exp), "exp", 1)

    @block.vector
    def _(vector):
        vector = C.wrap(vector, 'v')
        vector.wait_ge(C.sem("exp"), E0)
        vector.wait_ge(C.sem("osc0"), OSp[0])
        vector.wait_ge(C.sem("osc1"), OSp[1])
        for qc in range(NQC):
            ntk = 4 * qc + 4
            for tk in range((0 if not softmax else 4 * qc), ntk):
                gi = g_of(qc, tk)
                vector.wait_ge(C.sem("s"), S0 + gi + 1)
                if softmax:
                    v = tk - 4 * qc
                    if v >= 2:
                        vector.wait_ge(C.sem("exp"), E0 + g_of(qc, 4 * qc + v - 2) + 1)
                    elif qc > 0:
                        vector.wait_ge(C.sem("exp"),
                                       E0 + g_of(qc - 1, 4 * (qc - 1) + v + 2) + 1)
                    C.inc(vector.tensor_tensor(
                        stmp[v % 2][:], ps2[gi % 2][:],
                        masks[:, v * 512:(v + 1) * 512], ALU.add), "msk", 1)
                else:
                    if gi >= 2:
                        vector.wait_ge(C.sem("exp"), E0 + gi - 1)
                    if tk >= 4 * qc:
                        v = tk - 4 * qc
                        C.inc(vector.tensor_tensor(
                            pT[gi % 2][:], ps2[gi % 2][:],
                            masks01[:, v * 512:(v + 1) * 512], ALU.mult),
                            "exp", 1)
                    else:
                        C.inc(vector.tensor_copy(pT[gi % 2][:],
                                                 ps2[gi % 2][:]), "exp", 1)
            vector.wait_ge(C.sem("pvd"), P0 + qc + 1)
            if softmax:
                vector.reciprocal(rrow, ppv[64:65, :])
                C.inc(vector.drain(), "rec", 1)
                vector.wait_ge(C.sem("bc"), B0 + qc + 1)
                vector.tensor_copy(pbcs[:], pbc)
            if qc > 0:
                pq = (qc - 1) % 2
                vector.wait_ge(C.sem("osc%d" % pq),
                               OSp[pq] + 16 * ((qc - 1) // 2 + 1))
            if softmax:
                C.inc(vector.tensor_tensor(obuf[:], ppv[0:64, :], pbcs[:],
                                           ALU.mult), "on", 1)
            else:
                C.inc(vector.tensor_copy(obuf[:], ppv[0:64, :]), "on", 1)

    O0n = O0

    @block.sync
    def _(sync):
        for qc in range(NQC):
            sync.wait_ge(C.sem("on"), O0n + qc + 1)
            C.inc(sync.dma_start(
                out=oT[h_abs * 64:(h_abs + 1) * 64, qc * 512:(qc + 1) * 512],
                in_=obuf[:]), "osc%d" % (qc % 2), 16)


def build_A2():
    nc = bass.Bass("TRN2", target_bir_lowering=False)
    xb = nc.dram_tensor("xb", [S, D], F32, kind="ExternalInput").ap()
    wq = nc.dram_tensor("wq", [D, 512], F32, kind="ExternalInput").ap()
    wkvd = nc.dram_tensor("wkvd", [D, 256], F32, kind="ExternalInput").ap()
    wku = nc.dram_tensor("wku", [256, 512], F32, kind="ExternalInput").ap()
    wvu = nc.dram_tensor("wvu", [256, 512], F32, kind="ExternalInput").ap()
    oT = nc.dram_tensor("oT", [512, S], F32, kind="ExternalOutput").ap()

    with ExitStack() as ctx:
        C = Ctx(nc, ctx)
        C.vones = True
        block = ctx.enter_context(nc.Block())
        consts = emit_consts_attn(C, nc, block)
        yT = C.sb("yT", [128, 8 * S], BF16)
        ptps = [C.ps("ptpA", [128, 512], BF16), C.ps("ptpB", [128, 512], BF16)]
        emit_ln_yT(C, block, xb, 32, yT, ptps, name="ln")
        wqb = C.sb("wqb", [128, 8 * 512], BF16)
        wkvdb = C.sb("wkvdb", [128, 8 * 256], BF16)
        wkub = C.sb("wkub", [128, 2 * 512], BF16)
        wvub = C.sb("wvub", [128, 2 * 512], BF16)
        WLD = emit_wcast(C, block, [(wq, wqb, 1024, 512), (wkvd, wkvdb, 1024, 256),
                                    (wku, wkub, 256, 512), (wvu, wvub, 256, 512)])
        latT = C.sb("latT", [128, 2 * S], BF16)
        pproj = [C.ps("pproj0", [128, 512]), C.ps("pproj1", [128, 512])]

        LT0 = C.n("lt"); LTC0 = C.n("ltc")

        @block.tensor
        def _(tensor):
            tensor.wait_ge(C.sem("wld"), WLD)
            tensor.wait_ge(C.sem("ln_tpc"), 64)
            j = 0
            for lt in range(2):
                for tc in range(8):
                    if j >= 2:
                        tensor.wait_ge(C.sem("ltc"), LTC0 + j - 1)
                    pp = pproj[j % 2]
                    for ch in range(8):
                        ins = tensor.matmul(
                            pp[:],
                            wkvdb[:, ch * 256 + lt * 128: ch * 256 + lt * 128 + 128],
                            yT[:, ch * S + tc * 512: ch * S + (tc + 1) * 512],
                            start=(ch == 0), stop=(ch == 7))
                    C.inc(ins, "lt", 1)
                    j += 1

        @block.vector
        def _(vector):
            vector = C.wrap(vector, 'v')
            for j in range(16):
                vector.wait_ge(C.sem("lt"), LT0 + j + 1)
                lt, tc = j // 8, j % 8
                vector.tensor_copy(
                    latT[:, lt * S + tc * 512: lt * S + (tc + 1) * 512],
                    pproj[j % 2][:])
                C.inc(vector.drain(), "ltc", 1)

        kp = [C.sb("kp0", [64, S], BF16), C.sb("kp1", [64, S], BF16)]
        qp = [C.sb("qp0", [64, S], BF16), C.sb("qp1", [64, S], BF16)]
        vtok = [C.sb("vtok0", [128, 32 * 65], BF16),
                C.sb("vtok1", [128, 32 * 65], BF16)]
        ps2 = [C.ps("ps0", [128, 512]), C.ps("ps1", [128, 512])]
        ppv = C.ps("ppv", [65, 512])
        pbc = C.ps("pbc", [64, 512])[0:64, :]
        stmp = [C.sb("stmp0", [128, 512]), C.sb("stmp1", [128, 512])]
        pT = [C.sb("pT0", [128, 512], BF16), C.sb("pT1", [128, 512], BF16)]
        obuf = C.sb("obuf", [64, 512])
        pbcs = C.sb("pbcs", [64, 512])
        rrowt = C.sb("rrowt", [1, 512])
        rrow = rrowt[0:1, :]

        for hp in range(4):
            PJ0 = C.n("pj"); PJC0 = C.n("pjc")

            @block.tensor
            def _(tensor, hp=hp, PJC0=PJC0, LTC_ALL=C.n("ltc"),
                  ON_PREV=C.n("on")):
                tensor.wait_ge(C.sem("ltc"), LTC_ALL)
                tensor.wait_ge(C.sem("on"), ON_PREV)
                j = 0
                for tc in range(8):
                    if j >= 2:
                        tensor.wait_ge(C.sem("pjc"), PJC0 + j - 1)
                    pp = pproj[j % 2]
                    for ch in range(8):
                        ins = tensor.matmul(
                            pp[:],
                            wqb[:, ch * 512 + hp * 128: ch * 512 + hp * 128 + 128],
                            yT[:, ch * S + tc * 512: ch * S + (tc + 1) * 512],
                            start=(ch == 0), stop=(ch == 7))
                    C.inc(ins, "pj", 1)
                    j += 1
                for tc in range(8):
                    if j >= 2:
                        tensor.wait_ge(C.sem("pjc"), PJC0 + j - 1)
                    pp = pproj[j % 2]
                    for ch in range(2):
                        ins = tensor.matmul(
                            pp[:],
                            wkub[:, ch * 512 + hp * 128: ch * 512 + hp * 128 + 128],
                            latT[:, ch * S + tc * 512: ch * S + (tc + 1) * 512],
                            start=(ch == 0), stop=(ch == 1))
                    C.inc(ins, "pj", 1)
                    j += 1
                for tt in range(32):
                    if j >= 2:
                        tensor.wait_ge(C.sem("pjc"), PJC0 + j - 1)
                    pp = pproj[j % 2]
                    for ch in range(2):
                        ins = tensor.matmul(
                            pp[:, 0:128],
                            latT[:, ch * S + tt * 128: ch * S + (tt + 1) * 128],
                            wvub[:, ch * 512 + hp * 128: ch * 512 + hp * 128 + 128],
                            start=(ch == 0), stop=(ch == 1))
                    C.inc(ins, "pj", 1)
                    j += 1

            @block.vector
            def _(vector, hp=hp, PJ0=PJ0):
                vector = C.wrap(vector, 'v')
                j = 0
                for wi in range(2):
                    dst = qp if wi == 0 else kp
                    for tc in range(8):
                        vector.wait_ge(C.sem("pj"), PJ0 + j + 1)
                        pp = pproj[j % 2]
                        sl = slice(tc * 512, (tc + 1) * 512)
                        if wi == 0:
                            vector.tensor_scalar(dst[0][0:64, sl], pp[0:64, :],
                                                 0.125, None, ALU.mult)
                            vector.tensor_scalar(dst[1][0:64, sl], pp[64:128, :],
                                                 0.125, None, ALU.mult)
                        else:
                            vector.tensor_copy(dst[0][0:64, sl], pp[0:64, :])
                            vector.tensor_copy(dst[1][0:64, sl], pp[64:128, :])
                        C.inc(vector.drain(), "pjc", 1)
                        j += 1
                for tt in range(32):
                    vector.wait_ge(C.sem("pj"), PJ0 + j + 1)
                    pp = pproj[j % 2]
                    vector.tensor_copy(vtok[0][:, tt * 65: tt * 65 + 64],
                                       pp[:, 0:64])
                    vector.tensor_copy(vtok[1][:, tt * 65: tt * 65 + 64],
                                       pp[:, 64:128])
                    C.inc(vector.drain(), "pjc", 1)
                    j += 1
                vector.memset(vtok[0][:, 64::65], 1.0)
                vector.memset(vtok[1][:, 64::65], 1.0)
                C.inc(vector.drain(), "vdone", 1)

            for hi in range(2):
                causal_softmax_head(C, block, consts, kp[hi], qp[hi], vtok[hi],
                                    oT, hp * 2 + hi, ps2, ppv, pbc, stmp, pT,
                                    obuf, pbcs, rrow, softmax=True)

        @block.sync
        def _(sync):
            sync.wait_ge(C.sem("osc0"), C.n("osc0"))
            sync.wait_ge(C.sem("osc1"), C.n("osc1"))
    return nc


def build_A3():
    nc = bass.Bass("TRN2", target_bir_lowering=False)
    xb = nc.dram_tensor("xb", [S, D], F32, kind="ExternalInput").ap()
    wqd = nc.dram_tensor("wq", [D, 512], F32, kind="ExternalInput").ap()
    wkd = nc.dram_tensor("wk", [D, 512], F32, kind="ExternalInput").ap()
    wvd = nc.dram_tensor("wv", [D, 512], F32, kind="ExternalInput").ap()
    oT = nc.dram_tensor("oT", [512, S], F32, kind="ExternalOutput").ap()

    with ExitStack() as ctx:
        C = Ctx(nc, ctx)
        C.vones = False
        block = ctx.enter_context(nc.Block())
        consts = emit_consts_attn(C, nc, block)
        yT = C.sb("yT", [128, 8 * S], BF16)
        ptps = [C.ps("ptpA", [128, 512], BF16), C.ps("ptpB", [128, 512], BF16)]
        emit_ln_yT(C, block, xb, 32, yT, ptps, name="ln")
        wqb = C.sb("wqb", [128, 8 * 512], BF16)
        wkb = C.sb("wkb", [128, 8 * 512], BF16)
        wvb = C.sb("wvb", [128, 8 * 512], BF16)
        WLD = emit_wcast(C, block, [(wqd, wqb, 1024, 512), (wkd, wkb, 1024, 512),
                                    (wvd, wvb, 1024, 512)])
        kp = [C.sb("kp0", [64, S], BF16), C.sb("kp1", [64, S], BF16)]
        qp = [C.sb("qp0", [64, S], BF16), C.sb("qp1", [64, S], BF16)]
        vtok = [C.sb("vtok0", [128, 32 * 64], BF16),
                C.sb("vtok1", [128, 32 * 64], BF16)]
        pproj = [C.ps("pproj0", [128, 512]), C.ps("pproj1", [128, 512])]
        ps2 = [C.ps("ps0", [128, 512]), C.ps("ps1", [128, 512])]
        ppv = C.ps("ppv", [65, 512])
        pbc = None
        stmp = None
        pT = [C.sb("pT0", [128, 512], BF16), C.sb("pT1", [128, 512], BF16)]
        obuf = C.sb("obuf", [64, 512])
        pbcs = None
        rrow = None

        for hp in range(4):
            @block.tensor
            def _(tensor, WLD=WLD, ON_PREV=C.n("on")):
                tensor.wait_ge(C.sem("wld"), WLD)
                tensor.wait_ge(C.sem("ln_tpc"), 64)
                tensor.wait_ge(C.sem("on"), ON_PREV)
            emit_proj_pair(C, block, hp, yT, (wkb, wqb, wvb),
                           (kp[0], kp[1], qp[0], qp[1]), vtok, S, pproj)
            for hi in range(2):
                causal_softmax_head(C, block, consts, kp[hi], qp[hi], vtok[hi],
                                    oT, hp * 2 + hi, ps2, ppv, pbc, stmp, pT,
                                    obuf, pbcs, rrow, softmax=False)

        @block.sync
        def _(sync):
            sync.wait_ge(C.sem("osc0"), C.n("osc0"))
            sync.wait_ge(C.sem("osc1"), C.n("osc1"))
    return nc


def build_A1():
    nc = bass.Bass("TRN2", target_bir_lowering=False)
    xb = nc.dram_tensor("xb", [S, D], F32, kind="ExternalInput").ap()
    wqd = nc.dram_tensor("wq", [D, 512], F32, kind="ExternalInput").ap()
    wkd = nc.dram_tensor("wk", [D, 512], F32, kind="ExternalInput").ap()
    wvd = nc.dram_tensor("wv", [D, 512], F32, kind="ExternalInput").ap()
    oT = nc.dram_tensor("oT", [512, S], F32, kind="ExternalOutput").ap()

    with ExitStack() as ctx:
        C = Ctx(nc, ctx)
        C.vones = True
        block = ctx.enter_context(nc.Block())
        consts = emit_consts_attn(C, nc, block)
        ones64 = consts["ones64"]
        yT = C.sb("yT", [128, 8 * S], BF16)
        ptps = [C.ps("ptpA", [128, 512], BF16), C.ps("ptpB", [128, 512], BF16)]
        emit_ln_yT(C, block, xb, 32, yT, ptps, name="ln")
        wqb = C.sb("wqb", [128, 8 * 512], BF16)
        wkb = C.sb("wkb", [128, 8 * 512], BF16)
        wvb = C.sb("wvb", [128, 8 * 512], BF16)
        WLD = emit_wcast(C, block, [(wqd, wqb, 1024, 512), (wkd, wkb, 1024, 512),
                                    (wvd, wvb, 1024, 512)])
        kp = [C.sb("kp0", [64, S], BF16), C.sb("kp1", [64, S], BF16)]
        qp = [C.sb("qp0", [64, S], BF16), C.sb("qp1", [64, S], BF16)]
        vtok = [C.sb("vtok0", [128, 32 * 65], BF16),
                C.sb("vtok1", [128, 32 * 65], BF16)]
        vg = [C.sb("vg0", [128, 4 * 8 * 65], BF16),
              C.sb("vg1", [128, 4 * 8 * 65], BF16)]
        pproj = [C.ps("pproj0", [128, 512]), C.ps("pproj1", [128, 512])]
        ps2 = [C.ps("ps0", [128, 512]), C.ps("ps1", [128, 512])]
        ppv = C.ps("ppv", [65, 512])
        pbch = C.ps("pbc", [64, 512])
        pbc = pbch[0:64, :]
        pT = [C.sb("pT0", [128, 512], BF16), C.sb("pT1", [128, 512], BF16)]
        pTg = [C.sb("pTg0", [128, 1024], BF16), C.sb("pTg1", [128, 1024], BF16)]
        obufs = [C.sb("obufA", [64, 512]), C.sb("obufB", [64, 512])]
        pbcs = C.sb("pbcs", [64, 512])
        rrowt = C.sb("rrowt", [1, 512])
        rrow = rrowt[0:1, :]
        oacc = C.sb("oacc", [64, S])
        NQC = S // 512

        for hp in range(4):
            PJ0 = C.n("pj"); PJC0 = C.n("pjc")

            @block.tensor
            def _(tensor, WLD=WLD, ON_PREV=C.n("oend"), PJGC_ALL=C.n("pjgc")):
                tensor.wait_ge(C.sem("wld"), WLD)
                tensor.wait_ge(C.sem("ln_tpc"), 64)
                tensor.wait_ge(C.sem("oend"), ON_PREV)
                tensor.wait_ge(C.sem("pjgc"), PJGC_ALL)
            emit_proj_pair(C, block, hp, yT, (wkb, wqb, wvb),
                           (kp[0], kp[1], qp[0], qp[1]), vtok, S, pproj,
                           scale_q=0.125)

            PJG0 = C.n("pjg"); PJGC0 = C.n("pjgc")

            @block.tensor
            def _(tensor, hp=hp, PJGC0=PJGC0, PJC_ALL=C.n("pjc")):
                tensor.wait_ge(C.sem("pjc"), PJC_ALL)
                j = 0
                for g in range(4):
                    for tt in range(8):
                        if j >= 2:
                            tensor.wait_ge(C.sem("pjgc"), PJGC0 + j - 1)
                        pp = pproj[j % 2]
                        for ch in range(8):
                            base = ch * S + g
                            ins = tensor.matmul(
                                pp[:, 0:128],
                                yT[:, base + tt * 512: base + tt * 512 + 509: 4],
                                wvb[:, ch * 512 + hp * 128: ch * 512 + hp * 128 + 128],
                                start=(ch == 0), stop=(ch == 7))
                        C.inc(ins, "pjg", 1)
                        j += 1

            @block.vector
            def _(vector, PJG0=PJG0):
                vector = C.wrap(vector, 'v')
                j = 0
                for g in range(4):
                    for tt in range(8):
                        vector.wait_ge(C.sem("pjg"), PJG0 + j + 1)
                        pp = pproj[j % 2]
                        o0 = (g * 8 + tt) * 65
                        vector.tensor_copy(vg[0][:, o0:o0 + 64], pp[:, 0:64])
                        vector.tensor_copy(vg[1][:, o0:o0 + 64], pp[:, 64:128])
                        C.inc(vector.drain(), "pjgc", 1)
                        j += 1
                vector.memset(vg[0][:, 64::65], 1.0)
                vector.memset(vg[1][:, 64::65], 1.0)
                C.inc(vector.drain(), "vgdone", 1)

            for hi in range(2):
                kph, qph, vth, vgh = kp[hi], qp[hi], vtok[hi], vg[hi]
                h_abs = hp * 2 + hi
                S0 = C.n("s"); E0 = C.n("exp"); P0 = C.n("pvd")
                R0 = C.n("rec"); B0 = C.n("bc"); O0 = C.n("on")
                VD = C.n("vdone"); VGD = C.n("vgdone")

                psets = [[ps2[0], ps2[1]], [pproj[0], pproj[1]]]

                @block.tensor
                def _(tensor, kph=kph, qph=qph, vth=vth, E0=E0, R0=R0, O0=O0,
                      VD=VD, VGD=VGD):
                    tensor.wait_ge(C.sem("vdone"), VD)
                    tensor.wait_ge(C.sem("vgdone"), VGD)
                    tensor.wait_ge(C.sem("exp"), E0)
                    tensor.wait_ge(C.sem("on"), O0)
                    ng = 0
                    NG = 16
                    for qc in range(NQC):
                        if qc > 0:
                            tensor.wait_ge(C.sem("on"), O0 + qc)
                        for g in range(NG):
                            if ng >= 2:
                                tensor.wait_ge(C.sem("exp"), E0 + ng - 1)
                            for i in range(2):
                                tk = g * 2 + i
                                tensor.matmul(psets[ng % 2][i][:],
                                              kph[:, tk * 128:(tk + 1) * 128],
                                              qph[:, qc * 512:(qc + 1) * 512],
                                              start=True, stop=True)
                            C.inc(tensor.drain(), "s", 1)
                            if g >= 1:
                                tensor.wait_ge(C.sem("exp"), E0 + ng)
                                for i in range(2):
                                    tk = (g - 1) * 2 + i
                                    tensor.matmul(
                                        ppv[:],
                                        vth[:, tk * 65: tk * 65 + 65],
                                        pTg[(ng - 1) % 2][:, i * 512:(i + 1) * 512],
                                        start=(tk == 0), stop=False)
                            ng += 1
                        tensor.wait_ge(C.sem("exp"), E0 + ng)
                        for i in range(2):
                            tk = 30 + i
                            C.inc(tensor.matmul(
                                ppv[:], vth[:, tk * 65: tk * 65 + 65],
                                pTg[(ng - 1) % 2][:, i * 512:(i + 1) * 512],
                                start=False, stop=(i == 1)), "pvd",
                                1 if i == 1 else 0)
                        tensor.wait_ge(C.sem("rec"), R0 + qc + 1)
                        C.inc(tensor.matmul(pbc, ones64[:], rrow, start=True,
                                            stop=True), "bc", 1)

                @block.scalar
                def _(scalar, S0=S0):
                    scalar = C.wrap(scalar, 's')
                    for ng in range(NQC * 16):
                        scalar.wait_ge(C.sem("s"), S0 + ng + 1)
                        for i in range(2):
                            scalar.activation(
                                pTg[ng % 2][:, i * 512:(i + 1) * 512],
                                psets[ng % 2][i][:], AF.Exp)
                        C.inc(scalar.drain(), "exp", 1)

                @block.vector
                def _(vector, E0=E0, P0=P0, B0=B0):
                    vector = C.wrap(vector, 'v')
                    vector.wait_ge(C.sem("exp"), E0)
                    for qc in range(NQC):
                        vector.wait_ge(C.sem("pvd"), P0 + qc + 1)
                        vector.reciprocal(rrow, ppv[64:65, :])
                        C.inc(vector.drain(), "rec", 1)
                        vector.wait_ge(C.sem("bc"), B0 + qc + 1)
                        vector.tensor_copy(pbcs[:], pbc)
                        C.inc(vector.tensor_tensor(
                            oacc[:, qc * 512:(qc + 1) * 512], ppv[0:64, :],
                            pbcs[:], ALU.mult), "on", 1)

                SL0 = C.n("sl"); EL0 = C.n("expl"); PL0 = C.n("pvl")
                RL0 = C.n("recl"); BL0 = C.n("bcl"); OL0 = C.n("onl")

                @block.tensor
                def _(tensor, kph=kph, qph=qph, vth=vth, EL0=EL0, RL0=RL0,
                      OL0=OL0, O_ALL=C.n("on")):
                    tensor.wait_ge(C.sem("on"), O_ALL)
                    for w in range(32):
                        if w >= 2:
                            tensor.wait_ge(C.sem("expl"), EL0 + w - 1)
                        sl = slice(w * 128, (w + 1) * 128)
                        C.inc(tensor.matmul(ps2[w % 2][:, 0:128], kph[:, sl],
                                            qph[:, sl], start=True, stop=True),
                              "sl", 1)
                        tensor.wait_ge(C.sem("expl"), EL0 + w + 1)
                        if w > 0:
                            tensor.wait_ge(C.sem("onl"), OL0 + w)
                        C.inc(tensor.matmul(ppv[:, 0:128],
                                            vth[:, w * 65: w * 65 + 65],
                                            pT[w % 2][:, 0:128],
                                            start=True, stop=True), "pvl", 1)
                        tensor.wait_ge(C.sem("recl"), RL0 + w + 1)
                        C.inc(tensor.matmul(pbc[:, 0:128], ones64[:],
                                            rrowt[0:1, 0:128],
                                            start=True, stop=True), "bcl", 1)

                @block.scalar
                def _(scalar, SL0=SL0):
                    scalar = C.wrap(scalar, 's')
                    for w in range(32):
                        scalar.wait_ge(C.sem("sl"), SL0 + w + 1)
                        C.inc(scalar.activation(pT[w % 2][:, 0:128],
                                                ps2[w % 2][:, 0:128], AF.Exp),
                              "expl", 1)

                @block.vector
                def _(vector, PL0=PL0, BL0=BL0):
                    vector = C.wrap(vector, 'v')
                    for w in range(32):
                        vector.wait_ge(C.sem("pvl"), PL0 + w + 1)
                        vector.reciprocal(rrowt[0:1, 0:128], ppv[64:65, 0:128])
                        C.inc(vector.drain(), "recl", 1)
                        vector.wait_ge(C.sem("bcl"), BL0 + w + 1)
                        vector.tensor_copy(pbcs[:, 0:128], pbc[:, 0:128])
                        sl = slice(w * 128, (w + 1) * 128)
                        vector.tensor_tensor(pbcs[:, 0:128], ppv[0:64, 0:128],
                                             pbcs[:, 0:128], ALU.mult)
                        C.inc(vector.tensor_tensor(oacc[:, sl], oacc[:, sl],
                                                   pbcs[:, 0:128], ALU.add),
                              "onl", 1)

                SD0 = C.n("sd"); ED0 = C.n("expd"); PD0 = C.n("pvdd")
                RD0 = C.n("recd"); BD0 = C.n("bcd"); OD0 = C.n("ond")
                OSCp = [C.n("osc0"), C.n("osc1")]

                @block.tensor
                def _(tensor, kph=kph, qph=qph, vgh=vgh, ED0=ED0, RD0=RD0,
                      OD0=OD0, OL_ALL=C.n("onl")):
                    tensor.wait_ge(C.sem("onl"), OL_ALL)
                    gi = 0
                    nq = 0
                    for g in range(4):
                        for qc in range(2):
                            if nq > 0:
                                tensor.wait_ge(C.sem("ond"), OD0 + nq)
                            for tk in range(8):
                                if gi >= 2:
                                    tensor.wait_ge(C.sem("expd"), ED0 + gi - 1)
                                kgs = kph[0:64, g + tk * 512: g + tk * 512 + 509: 4]
                                qgs = qph[0:64, g + qc * 2048: g + qc * 2048 + 2045: 4]
                                C.inc(tensor.matmul(ps2[gi % 2][:], kgs, qgs,
                                                    start=True, stop=True), "sd", 1)
                                if tk >= 1:
                                    tensor.wait_ge(C.sem("expd"), ED0 + gi)
                                    o0 = (g * 8 + (tk - 1)) * 65
                                    tensor.matmul(ppv[:], vgh[:, o0:o0 + 65],
                                                  pT[(gi - 1) % 2][:],
                                                  start=(tk == 1), stop=False)
                                gi += 1
                            tensor.wait_ge(C.sem("expd"), ED0 + gi)
                            o0 = (g * 8 + 7) * 65
                            C.inc(tensor.matmul(ppv[:], vgh[:, o0:o0 + 65],
                                                pT[(gi - 1) % 2][:], start=False,
                                                stop=True), "pvdd", 1)
                            nq += 1
                            tensor.wait_ge(C.sem("recd"), RD0 + nq)
                            C.inc(tensor.matmul(pbc, ones64[:], rrow, start=True,
                                                stop=True), "bcd", 1)

                @block.scalar
                def _(scalar, SD0=SD0):
                    scalar = C.wrap(scalar, 's')
                    for gi in range(4 * 2 * 8):
                        scalar.wait_ge(C.sem("sd"), SD0 + gi + 1)
                        C.inc(scalar.activation(pT[gi % 2][:], ps2[gi % 2][:],
                                                AF.Exp), "expd", 1)

                @block.vector
                def _(vector, PD0=PD0, BD0=BD0, h_abs=h_abs):
                    vector = C.wrap(vector, 'v')
                    nq = 0
                    for g in range(4):
                        for qc in range(2):
                            vector.wait_ge(C.sem("pvdd"), PD0 + nq + 1)
                            vector.reciprocal(rrow, ppv[64:65, :])
                            C.inc(vector.drain(), "recd", 1)
                            vector.wait_ge(C.sem("bcd"), BD0 + nq + 1)
                            vector.tensor_copy(pbcs[:], pbc)
                            vector.tensor_tensor(pbcs[:], ppv[0:64, :], pbcs[:],
                                                 ALU.mult)
                            osl = oacc[0:64, g + qc * 2048: g + qc * 2048 + 2045: 4]
                            C.inc(vector.tensor_tensor(osl, osl, pbcs[:],
                                                       ALU.add), "ond", 1)
                            nq += 1
                    for qc in range(NQC):
                        pq = qc % 2
                        if qc >= 2:
                            vector.wait_ge(C.sem("osc%d" % pq),
                                           OSCp[pq] + 16 * ((qc - 2) // 2 + 1))
                        else:
                            vector.wait_ge(C.sem("osc%d" % pq), OSCp[pq])
                        C.inc(vector.tensor_scalar(
                            obufs[pq][:], oacc[:, qc * 512:(qc + 1) * 512],
                            1.0 / 3.0, None, ALU.mult), "oconv", 1)

                OC0 = C.n("oconv") - NQC

                @block.sync
                def _(sync, h_abs=h_abs, OC0=OC0):
                    for qc in range(NQC):
                        sync.wait_ge(C.sem("oconv"), OC0 + qc + 1)
                        C.inc(sync.dma_start(
                            out=oT[h_abs * 64:(h_abs + 1) * 64,
                                   qc * 512:(qc + 1) * 512],
                            in_=obufs[qc % 2][:]), "osc%d" % (qc % 2), 16)

                @block.tensor
                def _(tensor, OE=C.n("oconv")):
                    tensor.wait_ge(C.sem("oconv"), OE)
                    C.inc(tensor.drain(), "oend", 1)

        @block.sync
        def _(sync):
            sync.wait_ge(C.sem("osc0"), C.n("osc0"))
            sync.wait_ge(C.sem("osc1"), C.n("osc1"))
    return nc


def emit_resid_ln(C, block, x_dram, x1_out, n_tiles, yT, ptps, pproj,
                  o_dram=None, w_dram=None, otok_dram=None, y_out=None,
                  x_sb=None, fp8=None, name="rl"):
    ntok = n_tiles * 128
    xa = C.sb(name + "_xa", [128, D])
    xb2 = C.sb(name + "_xb", [128, D])
    if otok_dram is not None:
        ot0 = C.sb(name + "_ot0", [128, D])
        ot1 = C.sb(name + "_ot1", [128, D])
    else:
        ot0 = ot1 = None
    xp0 = C.sb(name + "_xp0", [128, D])
    xp1 = C.sb(name + "_xp1", [128, D])
    mean = C.sb(name + "_m", [128, 1])
    var = C.sb(name + "_v", [128, 1])
    var2 = C.sb(name + "_v2", [128, 1])
    sdv = C.sb(name + "_sd", [128, 1])
    inv = C.sb(name + "_i", [128, 1])
    eps = C.sb(name + "_eps", [128, 1])
    sq = C.sb(name + "_sq", [128, 512])
    yb0 = C.sb(name + "_yb0", [128, D], BF16)
    yb1 = C.sb(name + "_yb1", [128, D], BF16)
    C.rl_bufs = (xa, xb2, xp0, xp1)
    if fp8 is not None:
        absy = C.sb(name + "_ab", [128, D])
        fsc = C.sb(name + "_fs", [128, 1])
        ysc0 = C.sb(name + "_ys0", [128, D], BF16)
        ysc1 = C.sb(name + "_ys1", [128, D], BF16)
        q8t = C.sb(name + "_q8", [128, 128], FP8)

    use_wout = o_dram is not None
    if use_wout:
        otb = [C.sb(name + "_otbA", [128, 8 * 128], BF16),
               C.sb(name + "_otbB", [128, 8 * 128], BF16)]
        wob = C.sb(name + "_wob", [128, 8 * D], BF16)
        emit_wcast(C, block, [(w_dram, wob, 1024, D)], name + "_wl")
        WL = C.n(name + "_wl")



    @block.sync
    def _(sync):
        for t in range(n_tiles):
            buf = xa if t % 2 == 0 else xb2
            ot = ot0 if t % 2 == 0 else ot1
            if t >= 2:
                sync.wait_ge(C.sem(name + "_y"), t - 1)
            C.inc(sync.dma_start(out=buf[:], in_=x_dram[t * 128:(t + 1) * 128, :]),
                  name + "_ld%d" % (t % 2), 16)
            if otok_dram is not None:
                C.inc(sync.dma_start(out=ot[:],
                                     in_=otok_dram[t * 128:(t + 1) * 128, :]),
                      name + "_ld%d" % (t % 2), 16)

    @block.vector
    def _(vector):
        vector = C.wrap(vector, 'v')
        C.inc(vector.memset(eps[:], 1.0e-5), name + "_eps", 1)
        nld = 2 if otok_dram is not None else 1
        for t in range(n_tiles):
            buf = xa if t % 2 == 0 else xb2
            ot = ot0 if t % 2 == 0 else ot1
            xp = xp0 if t % 2 == 0 else xp1
            yb = yb0 if t % 2 == 0 else yb1
            vector.wait_ge(C.sem(name + "_ld%d" % (t % 2)),
                           16 * nld * (t // 2 + 1))
            if t >= 2:
                nst = 2 if y_out is not None else 1
                vector.wait_ge(C.sem(name + "_st%d" % (t % 2)),
                               16 * nst * (t // 2))
            if use_wout:
                vector.wait_ge(C.sem(name + "_pw"), t + 1)
                vector.tensor_tensor(xp[:, 0:512], buf[:, 0:512], pproj[0][:],
                                     ALU.add)
                vector.tensor_tensor(xp[:, 512:1024], buf[:, 512:1024],
                                     pproj[1][:], ALU.add)
            elif otok_dram is not None:
                vector.tensor_tensor(xp[:], buf[:], ot[:], ALU.add)
            else:
                vector.tensor_copy(xp[:], buf[:])
            if x_sb is not None:
                vector.tensor_copy(x_sb[:, t * D:(t + 1) * D], xp[:])
            C.inc(vector.drain(), name + "_xpv", 1)
            vector.reduce_sum(mean[:], xp[:], axis=AX)
            vector.drain()
            vector.tensor_scalar(mean[:], mean[:], 1.0 / D, None, ALU.mult)
            C.inc(vector.tensor_scalar(buf[:], xp[:], mean[:], None,
                                       ALU.subtract), name + "_xc", 1)
            vector.wait_ge(C.sem(name + "_sv"), t + 1)
            vector.reciprocal(inv[:], sdv[:])
            vector.drain()
            if t >= 2:
                vector.wait_ge(C.sem(name + "_tp"), 2 * (t - 1))
            C.inc(vector.tensor_scalar(yb[:], buf[:], inv[:], None, ALU.mult),
                  name + "_y", 1)
            if fp8 is not None:
                ysc = ysc0 if t % 2 == 0 else ysc1
                vector.wait_ge(C.sem(name + "_aby"), t + 1)
                vector.reduce_max(fsc[:], absy[:], axis=AX)
                vector.drain()
                vector.tensor_copy(fp8["amaxs"][:, t:t + 1], fsc[:])
                vector.reciprocal(fsc[:], fsc[:])
                vector.drain()
                vector.tensor_scalar(fsc[:], fsc[:], 224.0, None, ALU.mult)
                C.inc(vector.tensor_scalar(ysc[:], yb[:], fsc[:], None,
                                           ALU.mult), name + "_yq", 1)

    @block.gpsimd
    def _(gpsimd):
        gpsimd = C.wrap(gpsimd, 'g')

        def store(t):
            xp = xp0 if t % 2 == 0 else xp1
            gpsimd.wait_ge(C.sem(name + "_xpv"), t + 1)
            C.inc(gpsimd.dma_start(out=x1_out[t * 128:(t + 1) * 128, :],
                                   in_=xp[:]), name + "_st%d" % (t % 2), 16)
            if y_out is not None:
                yb = yb0 if t % 2 == 0 else yb1
                gpsimd.wait_ge(C.sem(name + "_y"), t + 1)
                C.inc(gpsimd.dma_start(out=y_out[t * 128:(t + 1) * 128, :],
                                       in_=yb[:]), name + "_st%d" % (t % 2), 16)
        for t in range(n_tiles):
            if use_wout:
                if t >= 2:
                    gpsimd.wait_ge(C.sem(name + "_pw"), t - 1)
                for ch in range(8):
                    C.inc(gpsimd.dma_start(
                        out=otb[t % 2][:, ch * 128:(ch + 1) * 128],
                        in_=o_dram[(ch * n_tiles + t) * 128:
                                   (ch * n_tiles + t + 1) * 128, :]),
                        name + "_ol%d" % (t % 2), 16)
            if t >= 1:
                store(t - 1)
        store(n_tiles - 1)

    def _copy_groups(scalar, t):
        for g in range(2):
            gi = 2 * t + g
            scalar.wait_ge(C.sem(name + "_tp"), gi + 1)
            ptp = ptps[gi % 2]
            for f4 in range(4):
                f = g * 4 + f4
                dst = yT[:, f * ntok + t * 128: f * ntok + (t + 1) * 128]
                if fp8 is not None:
                    scalar.copy(q8t[:], ptp[:, f4 * 128: f4 * 128 + 128])
                    scalar.drain()
                    scalar.copy(dst, q8t[:])
                else:
                    scalar.copy(dst, ptp[:, f4 * 128: f4 * 128 + 128])
            C.inc(scalar.drain(), name + "_tpc", 1)

    @block.scalar
    def _(scalar):
        scalar = C.wrap(scalar, 's')
        scalar.wait_ge(C.sem(name + "_eps"), 1)
        for t in range(n_tiles):
            xp = xp0 if t % 2 == 0 else xp1
            yb = yb0 if t % 2 == 0 else yb1
            buf = xa if t % 2 == 0 else xb2
            scalar.wait_ge(C.sem(name + "_xc"), t + 1)
            scalar.activation(sq[:], buf[:, 0:512], AF.Square, accum_out=var[:])
            scalar.activation(sq[:], buf[:, 512:1024], AF.Square,
                              accum_out=var2[:])
            scalar.drain()
            scalar.activation(var[:], var[:], AF.Relu, bias=var2[:])
            C.inc(scalar.activation(sdv[:], var[:], AF.Sqrt, scale=1.0 / D,
                                    bias=eps[:]), name + "_sv", 1)
            if fp8 is not None:
                scalar.wait_ge(C.sem(name + "_y"), t + 1)
                C.inc(scalar.activation(absy[:], yb[:], AF.Abs), name + "_aby", 1)
            if t >= 1:
                _copy_groups(scalar, t - 1)
        _copy_groups(scalar, n_tiles - 1)

    @block.tensor
    def _(tensor):
        if use_wout:
            tensor.wait_ge(C.sem(name + "_wl"), WL)
        ysem = name + ("_yq" if fp8 is not None else "_y")

        def tp_tile(t):
            tensor.wait_ge(C.sem(ysem), t + 1)
            if fp8 is not None:
                src = ysc0 if t % 2 == 0 else ysc1
            else:
                src = yb0 if t % 2 == 0 else yb1
            for g in range(2):
                gi = 2 * t + g
                if gi >= 2:
                    tensor.wait_ge(C.sem(name + "_tpc"), gi - 1)
                ptp = ptps[gi % 2]
                for f4 in range(4):
                    f = g * 4 + f4
                    tensor.transpose(ptp[:, f4 * 128: f4 * 128 + 128],
                                     src[:, f * 128:(f + 1) * 128], C.ident[:])
                C.inc(tensor.drain(), name + "_tp", 1)
        for t in range(n_tiles):
            if use_wout:
                if t >= 1:
                    tensor.wait_ge(C.sem(name + "_xpv"), t)
                tensor.wait_ge(C.sem(name + "_ol%d" % (t % 2)),
                               16 * 8 * (t // 2 + 1))
                for half in range(2):
                    pp = pproj[half]
                    for ch in range(8):
                        tensor.matmul(
                            pp[:],
                            otb[t % 2][:, ch * 128:(ch + 1) * 128],
                            wob[:, ch * D + half * 512: ch * D + half * 512 + 512],
                            start=(ch == 0), stop=(ch == 7))
                C.inc(tensor.drain(), name + "_pw", 1)
            if t >= 1:
                tp_tile(t - 1)
        tp_tile(n_tiles - 1)
    return


def build_F1():
    nc = bass.Bass("TRN2", target_bir_lowering=False)
    NT = TPC // 128
    xd = nc.dram_tensor("xd", [TPC, D], F32, kind="ExternalInput").ap()
    oTd = nc.dram_tensor("oTd", [8 * NT * 128, 128], F32,
                         kind="ExternalInput").ap()
    wod = nc.dram_tensor("wod", [D, D], F32, kind="ExternalInput").ap()
    w8d = nc.dram_tensor("w8d", [D, D], U8, kind="ExternalInput").ap()
    wsd = nc.dram_tensor("wsd", [128, 1], F32, kind="ExternalInput").ap()
    bd = nc.dram_tensor("bd", [1, D], F32, kind="ExternalInput").ap()
    x1o = nc.dram_tensor("x1o", [TPC, D], F32, kind="ExternalOutput").ap()
    x2o = nc.dram_tensor("x2o", [TPC, D], F32, kind="ExternalOutput").ap()
    ident_c = nc.inline_tensor(np.eye(128, dtype=BF), "ident_c")
    ones1_c = nc.inline_tensor(np.ones((1, 128), dtype=BF), "ones1_c")

    with ExitStack() as ctx:
        C = Ctx(nc, ctx)
        block = ctx.enter_context(nc.Block())
        C.ident = C.sb("ident", [128, 128], BF16)
        ones1 = C.sb("ones1", [1, 128], BF16)
        ws128 = C.sb("ws128", [128, 1])
        brow = C.sb("brow", [1, D])
        biasb = C.sb("biasb", [128, D])
        amaxs = C.sb("amaxs", [128, NT])
        fsct = C.sb("fsct", [128, NT])
        w8s = C.sb("w8s", [128, 8 * D], U8)
        wdec = C.sb("wdec", [128, 8 * D], BF16)
        yT = C.sb("yT", [128, 8 * TPC], BF16)
        xr = [C.sb("xrA", [128, D]), C.sb("xrB", [128, D])]

        @block.sync
        def _(sync):
            C.inc(sync.dma_start(out=C.ident[:], in_=ident_c.ap()), "cld", 16)
            C.inc(sync.dma_start(out=ones1[:], in_=ones1_c.ap()), "cld", 16)
            C.inc(sync.dma_start(out=ws128[:], in_=wsd), "cld", 16)
            C.inc(sync.dma_start(out=brow[:], in_=bd), "cld", 16)
            for c in range(8):
                C.inc(sync.dma_start(out=w8s[:, c * D:(c + 1) * D],
                                     in_=w8d[c * 128:(c + 1) * 128, :]), "cld", 16)

        @block.tensor
        def _(tensor):
            tensor.wait_ge(C.sem("cld"), C.cnt["cld"])

        @block.vector
        def _(vector):
            vector.wait_ge(C.sem("cld"), C.cnt["cld"])

        @block.scalar
        def _(scalar):
            scalar.wait_ge(C.sem("cld"), C.cnt["cld"])

        dtmp = C.sb("dtmp", [128, D])
        d_n = C.sb("d_n", [128, D])
        d_s = C.sb("d_s", [128, D])
        d_e = C.sb("d_e", [128, D])
        d_m = C.sb("d_m", [128, D])
        d_p = C.sb("d_p", [128, D])

        @block.vector
        def _(vector):
            vector = C.wrap(vector, 'v')
            for c in range(8):
                sl = slice(c * D, (c + 1) * D)
                vector.tensor_copy(d_n[:], w8s[:, sl])
                vector.tensor_scalar(d_s[:], d_n[:], 127.5, None, ALU.is_gt)
                vector.tensor_scalar(dtmp[:], d_s[:], -128.0, None, ALU.mult)
                vector.tensor_tensor(d_n[:], d_n[:], dtmp[:], ALU.add)
                vector.memset(d_e[:], 0.0)
                vector.tensor_copy(d_m[:], d_n[:])
                for k in (64.0, 32.0, 16.0, 8.0):
                    vector.tensor_scalar(dtmp[:], d_m[:], k - 0.5, None,
                                         ALU.is_ge)
                    vector.tensor_scalar(d_p[:], dtmp[:], -k, None, ALU.mult)
                    vector.tensor_tensor(d_m[:], d_m[:], d_p[:], ALU.add)
                    vector.tensor_scalar(d_p[:], dtmp[:], k / 8.0, None,
                                         ALU.mult)
                    vector.tensor_tensor(d_e[:], d_e[:], d_p[:], ALU.add)
                vector.tensor_scalar(dtmp[:], d_e[:], 0.5, None, ALU.is_ge)
                vector.tensor_scalar(dtmp[:], dtmp[:], 8.0, None, ALU.mult)
                vector.tensor_tensor(d_m[:], d_m[:], dtmp[:], ALU.add)
                vector.tensor_scalar(d_e[:], d_e[:], 1.0, -1.0, ALU.max, ALU.add)
                vector.memset(d_p[:], 1.0)
                for k, fac in ((8, 255.0), (4, 15.0), (2, 3.0), (1, 1.0)):
                    vector.tensor_scalar(dtmp[:], d_e[:], float(k) - 0.5, None,
                                         ALU.is_ge)
                    vector.tensor_scalar(d_n[:], dtmp[:], float(-k), None,
                                         ALU.mult)
                    vector.tensor_tensor(d_e[:], d_e[:], d_n[:], ALU.add)
                    vector.tensor_scalar(dtmp[:], dtmp[:], fac, 1.0, ALU.mult,
                                         ALU.add)
                    vector.tensor_tensor(d_p[:], d_p[:], dtmp[:], ALU.mult)
                vector.tensor_scalar(d_s[:], d_s[:], -2.0, 1.0, ALU.mult, ALU.add)
                vector.tensor_tensor(d_p[:], d_p[:], d_m[:], ALU.mult)
                vector.tensor_tensor(d_p[:], d_p[:], d_s[:], ALU.mult)
                vector.tensor_scalar(wdec[:, sl], d_p[:], float(2.0 ** -9),
                                     None, ALU.mult)
            C.inc(vector.drain(), "dec", 1)

        ptps = [C.ps("ptpA", [128, 512], BF16), C.ps("ptpB", [128, 512], BF16)]
        pproj = [C.ps("pproj0", [128, 512]), C.ps("pproj1", [128, 512])]
        emit_resid_ln(C, block, xd, x1o, NT, yT, ptps, pproj,
                      o_dram=oTd, w_dram=wod,
                      fp8=dict(amaxs=amaxs), name="rl")

        browb = C.sb("browb", [1, D], BF16)
        pb2 = [C.ps("pb0", [128, 512]), C.ps("pb1", [128, 512])]

        @block.scalar
        def _(scalar):
            scalar = C.wrap(scalar, 's')
            C.inc(scalar.copy(browb[:], brow[:]), "bb", 1)

        @block.tensor
        def _(tensor, DEC=C.n("dec"), YQ=C.n("rl_yq")):
            tensor.wait_ge(C.sem("bb"), 1)
            tensor.matmul(pb2[0][:], ones1[:], browb[0:1, 0:512], start=True,
                          stop=True)
            C.inc(tensor.matmul(pb2[1][:], ones1[:], browb[0:1, 512:1024],
                                start=True, stop=True), "bbt", 1)

        @block.vector
        def _(vector):
            vector = C.wrap(vector, 'v')
            vector.wait_ge(C.sem("bbt"), 1)
            vector.tensor_copy(biasb[:, 0:512], pb2[0][:])
            vector.tensor_copy(biasb[:, 512:1024], pb2[1][:])
            vector.wait_ge(C.sem("rl_yq"), NT)
            vector.tensor_scalar(fsct[:], amaxs[:], 2.0 / 448.0, None, ALU.mult)
            C.inc(vector.tensor_tensor(fsct[:], fsct[:],
                                       ws128[:].broadcast_to([128, NT]),
                                       ALU.mult), "fsc", 1)

        MM0 = C.n("mm")

        @block.tensor
        def _(tensor, DEC=C.n("dec")):
            tensor.wait_ge(C.sem("dec"), DEC)
            tensor.wait_ge(C.sem("rl_tpc"), 2 * NT)
            tensor.wait_ge(C.sem("fsc"), 1)
            for t in range(NT):
                if t >= 1:
                    tensor.wait_ge(C.sem("mmc"), t)
                for half in range(2):
                    pp = pb2[half]
                    for ch in range(8):
                        ins = tensor.matmul(
                            pp[:],
                            yT[:, ch * TPC + t * 128: ch * TPC + (t + 1) * 128],
                            wdec[:, ch * D + half * 512: ch * D + half * 512 + 512],
                            start=(ch == 0), stop=(ch == 7))
                C.inc(tensor.drain(), "mm", 1)

        ocmb = [C.sb("ocmbA", [128, D]), C.sb("ocmbB", [128, D])]

        @block.gpsimd
        def _(gpsimd):
            gpsimd = C.wrap(gpsimd, 'g')
            for t in range(NT):
                gpsimd.wait_ge(C.sem("rl_st%d" % (t % 2)), 16 * (t // 2 + 1))
                if t >= 2:
                    gpsimd.wait_ge(C.sem("fin%d" % (t % 2)), t // 2)
                C.inc(gpsimd.dma_start(out=xr[t % 2][:],
                                       in_=x1o[t * 128:(t + 1) * 128, :]),
                      "xl%d" % (t % 2), 16)

        @block.vector
        def _(vector, MM0=MM0):
            vector = C.wrap(vector, 'v')
            for t in range(NT):
                ob = ocmb[t % 2]
                vector.wait_ge(C.sem("xl%d" % (t % 2)), 16 * (t // 2 + 1))
                vector.wait_ge(C.sem("mm"), MM0 + t + 1)
                if t >= 2:
                    vector.wait_ge(C.sem("fst%d" % (t % 2)), 16 * (t // 2))
                fcol = fsct[:, t:t + 1]
                vector.tensor_scalar(ob[:, 0:512], pb2[0][:], fcol, None,
                                     ALU.mult)
                vector.tensor_scalar(ob[:, 512:1024], pb2[1][:], fcol, None,
                                     ALU.mult)
                C.inc(vector.drain(), "mmc", 1)
                vector.tensor_tensor(ob[:], ob[:], biasb[:], ALU.add)
                C.inc(vector.tensor_tensor(ob[:], ob[:], xr[t % 2][:], ALU.add),
                      "fin%d" % (t % 2), 1)

        @block.sync
        def _(sync):
            for t in range(NT):
                sync.wait_ge(C.sem("fin%d" % (t % 2)), t // 2 + 1)
                C.inc(sync.dma_start(out=x2o[t * 128:(t + 1) * 128, :],
                                     in_=ocmb[t % 2][:]),
                      "fst%d" % (t % 2), 16)
            sync.wait_ge(C.sem("fst0"), C.n("fst0"))
            sync.wait_ge(C.sem("fst1"), C.n("fst1"))
            sync.wait_ge(C.sem("rl_st0"), C.n("rl_st0"))
            sync.wait_ge(C.sem("rl_st1"), C.n("rl_st1"))
    return nc


def build_F3():
    nc = bass.Bass("TRN2", target_bir_lowering=False)
    NT = TPC // 128
    xd = nc.dram_tensor("xd", [TPC, D], F32, kind="ExternalInput").ap()
    oTd = nc.dram_tensor("oTd", [8 * NT * 128, 128], F32,
                         kind="ExternalInput").ap()
    wod = nc.dram_tensor("wod", [D, D], F32, kind="ExternalInput").ap()
    wrd = nc.dram_tensor("wrd", [D, 8], F32, kind="ExternalInput").ap()
    brd = nc.dram_tensor("brd", [1, 8], F32, kind="ExternalInput").ap()
    w1d = nc.dram_tensor("w1d", [8 * 1024, 2048], F32, kind="ExternalInput").ap()
    b1d = nc.dram_tensor("b1d", [128, 8 * 16], F32, kind="ExternalInput").ap()
    w2d = nc.dram_tensor("w2d", [8 * 2048, 1024], F32, kind="ExternalInput").ap()
    b2d = nc.dram_tensor("b2d", [8, D], F32, kind="ExternalInput").ap()
    x1o = nc.dram_tensor("x1o", [TPC, D], F32, kind="ExternalOutput").ap()
    x2o = nc.dram_tensor("x2o", [TPC, D], F32, kind="ExternalOutput").ap()
    ident_c = nc.inline_tensor(np.eye(128, dtype=BF), "ident_c")
    ones1_c = nc.inline_tensor(np.ones((1, 128), dtype=BF), "ones1_c")

    with ExitStack() as ctx:
        C = Ctx(nc, ctx)
        block = ctx.enter_context(nc.Block())
        C.ident = C.sb("ident", [128, 128], BF16)
        ones1 = C.sb("ones1", [1, 128], BF16)
        yT = C.sb("yT", [128, 8 * TPC], BF16)
        wrb = C.sb("wrb", [128, 8 * 8], BF16)
        brow = C.sb("brow", [1, 8], BF16)
        b2b = C.sb("b2b", [8, D], BF16)
        b1s = C.sb("b1s", [128, 8 * 16])
        acc = C.sb("acc", [128, NT * D], BF16)
        cw = C.sb("cw", [128, NT * 8])
        cwT = C.sb("cwT", [8, TPC], BF16)
        probs = C.sb("probs", [128, 8])
        pm = C.sb("pm", [128, 8])
        m1 = C.sb("m1", [128, 1])
        esum = C.sb("esum", [128, 1])
        w1e = C.sb("w1e", [128, 8 * 2048], BF16)
        w2e = C.sb("w2e", [128, 16 * 1024], BF16)
        hTs = [C.sb("hTsA", [128, 16 * 128], BF16),
               C.sb("hTsB", [128, 16 * 128], BF16)]
        etmp = C.sb("etmp", [128, 512])

        @block.sync
        def _(sync):
            C.inc(sync.dma_start(out=C.ident[:], in_=ident_c.ap()), "cld", 16)
            C.inc(sync.dma_start(out=ones1[:], in_=ones1_c.ap()), "cld", 16)
            C.inc(sync.dma_start(out=b1s[:], in_=b1d), "cld", 16)

        @block.gpsimd
        def _(gpsimd):
            gpsimd = C.wrap(gpsimd, 'g')
            C.inc(gpsimd.dma_start(out=brow[:], in_=brd), "cldg", 16)
            for c in range(8):
                C.inc(gpsimd.dma_start(out=wrb[:, c * 8:(c + 1) * 8],
                                       in_=wrd[c * 128:(c + 1) * 128, :]),
                      "cldg", 16)
            C.inc(gpsimd.dma_start(out=b2b[:], in_=b2d), "cldg", 16)
        CLD = C.cnt["cld"]
        CLDG = C.cnt["cldg"]

        @block.tensor
        def _(tensor):
            tensor.wait_ge(C.sem("cld"), CLD)
            tensor.wait_ge(C.sem("cldg"), CLDG)

        @block.vector
        def _(vector):
            vector.wait_ge(C.sem("cld"), CLD)
            vector.wait_ge(C.sem("cldg"), CLDG)

        @block.scalar
        def _(scalar):
            scalar.wait_ge(C.sem("cld"), CLD)
            scalar.wait_ge(C.sem("cldg"), CLDG)

        ptps = [C.ps("ptpA", [128, 512], BF16), C.ps("ptpB", [128, 512], BF16)]
        pproj = [C.ps("pproj0", [128, 512]), C.ps("pproj1", [128, 512])]
        emit_resid_ln(C, block, xd, x1o, NT, yT, ptps, pproj, o_dram=oTd,
                      w_dram=wod, name="rl")
        xr = [C.rl_bufs[0], C.rl_bufs[1]]
        ocmb = [C.rl_bufs[2], C.rl_bufs[3]]

        po2 = [C.ps("po0", [128, 512]), C.ps("po1", [128, 512])]
        prt = po2[0][:, 0:8]
        pbr = po2[1][:, 0:8]

        @block.tensor
        def _(tensor):
            tensor.wait_ge(C.sem("rl_tpc"), 2 * NT)
            C.inc(tensor.matmul(pbr, ones1[:], brow[0:1, :], start=True,
                                stop=True), "brt", 1)
            for t in range(NT):
                if t >= 1:
                    tensor.wait_ge(C.sem("rtc"), t)
                for ch in range(8):
                    ins = tensor.matmul(
                        prt,
                        yT[:, ch * TPC + t * 128: ch * TPC + (t + 1) * 128],
                        wrb[:, ch * 8:(ch + 1) * 8],
                        start=(ch == 0), stop=(ch == 7))
                C.inc(ins, "rt", 1)

        brs = C.sb("brs", [128, 8])

        @block.vector
        def _(vector, RT0=C.n("rt") - NT):
            vector = C.wrap(vector, 'v')
            vector.wait_ge(C.sem("brt"), 1)
            vector.tensor_copy(brs[:], pbr)
            for t in range(NT):
                vector.wait_ge(C.sem("rt"), RT0 + t + 1)
                vector.tensor_tensor(probs[:], prt, brs[:], ALU.add)
                C.inc(vector.drain(), "rtc", 1)
                vector.wait_ge(C.sem("rtx"), t + 1)
                vector.reciprocal(esum[:], esum[:])
                vector.drain()
                vector.tensor_scalar(probs[:], probs[:], esum[:], None, ALU.mult)
                vector.drain()
                vector.reduce_max(m1[:], probs[:], axis=AX)
                vector.drain()
                vector.tensor_scalar(pm[:], probs[:], m1[:], 0.0, ALU.subtract,
                                     ALU.is_ge)
                vector.tensor_scalar(pm[:], pm[:], -1.0, 1.0, ALU.mult, ALU.add)
                vector.tensor_tensor(pm[:], probs[:], pm[:], ALU.mult)
                vector.reduce_max(m1[:], pm[:], axis=AX)
                vector.drain()
                vector.tensor_scalar(pm[:], probs[:], m1[:], 0.0, ALU.subtract,
                                     ALU.is_ge)
                C.inc(vector.tensor_tensor(cw[:, t * 8:(t + 1) * 8], probs[:],
                                           pm[:], ALU.mult), "cwd", 1)

        @block.scalar
        def _(scalar, RTC0=C.n("rtc") - NT):
            scalar = C.wrap(scalar, 's')
            for t in range(NT):
                scalar.wait_ge(C.sem("rtc"), RTC0 + t + 1)
                C.inc(scalar.activation(probs[:], probs[:], AF.Exp,
                                        accum_out=esum[:]), "rtx", 1)

        pcw = None
        cwb = C.sb("cwb", [128, 8], BF16)

        @block.vector
        def _(vector, CWD0=C.n("cwd") - NT):
            vector = C.wrap(vector, 'v')
            vector.memset(acc[:], 0.0)
            C.inc(vector.drain(), "acz", 1)
            for t in range(NT):
                vector.wait_ge(C.sem("cwd"), CWD0 + t + 1)
                if t >= 1:
                    vector.wait_ge(C.sem("cwt"), t)
                C.inc(vector.tensor_copy(cwb[:], cw[:, t * 8:(t + 1) * 8]),
                      "cwb", 1)
                vector.wait_ge(C.sem("cwt2"), t)
                C.inc(vector.engine_nop(), "cwgo", 1)

        @block.tensor
        def _(tensor, CWB0=C.n("cwb") - NT, CWT20=C.n("cwt2")):
            for t in range(NT):
                tensor.wait_ge(C.sem("cwb"), CWB0 + t + 1)
                if t >= 1:
                    tensor.wait_ge(C.sem("cwt2"), CWT20 + t)
                tensor.transpose(ptps[0][0:8, 0:128], cwb[:], C.ident[:])
                C.inc(tensor.drain(), "cwt", 1)

        @block.scalar
        def _(scalar, CWT0=C.n("cwt") - NT):
            scalar = C.wrap(scalar, 's')
            for t in range(NT):
                scalar.wait_ge(C.sem("cwt"), CWT0 + t + 1)
                scalar.copy(cwT[0:8, t * 128:(t + 1) * 128], ptps[0][0:8, 0:128])
                C.inc(scalar.drain(), "cwt2", 1)

        pe = [pproj[0][:, 0:128], pproj[1][:, 0:128], po2[0][:, :], po2[1][:, :]]

        for e in range(8):
            EWL = C.n("ewl")

            @block.gpsimd
            def _(gpsimd, e=e, EDONE=C.n("eoc")):
                gpsimd = C.wrap(gpsimd, 'g')
                gpsimd.wait_ge(C.sem("eoc"), EDONE)
                for ch in range(8):
                    C.inc(gpsimd.dma_start(
                        out=w1e[:, ch * 2048:(ch + 1) * 2048],
                        in_=w1d[e * 1024 + ch * 128: e * 1024 + (ch + 1) * 128, :]),
                        "ewl", 16)
                for ch in range(16):
                    C.inc(gpsimd.dma_start(
                        out=w2e[:, ch * 1024:(ch + 1) * 1024],
                        in_=w2d[e * 2048 + ch * 128: e * 2048 + (ch + 1) * 128, :]),
                        "ewl", 16)
            EWL_T = C.n("ewl")

            @block.tensor
            def _(tensor, e=e, EWL_T=EWL_T, ACZ=C.n("acz"), CWGO=C.n("cwgo"),
                  HC0=C.n("hc"), EAC0=C.n("eac")):
                tensor.wait_ge(C.sem("ewl"), EWL_T)
                tensor.wait_ge(C.sem("acz"), ACZ)
                tensor.wait_ge(C.sem("cwgo"), CWGO)
                nh = 0
                for t in range(NT):
                    hb = hTs[t % 2]
                    for ot in range(16):
                        pp = pe[ot % 2]
                        if nh >= 2:
                            tensor.wait_ge(C.sem("hc"), HC0 + nh - 1)
                        for ch in range(8):
                            ins = tensor.matmul(
                                pp,
                                w1e[:, ch * 2048 + ot * 128: ch * 2048 + (ot + 1) * 128],
                                yT[:, ch * TPC + t * 128: ch * TPC + (t + 1) * 128],
                                start=(ch == 0), stop=(ch == 7))
                        C.inc(ins, "eh", 1)
                        nh += 1
                    tensor.wait_ge(C.sem("hc"), HC0 + nh)
                    po = pe[2 + (t % 2)]
                    if t >= 2:
                        tensor.wait_ge(C.sem("eac"), EAC0 + 2 * (t - 1))
                    for hf in range(2):
                        for ch in range(16):
                            ins = tensor.matmul(
                                po,
                                hb[:, ch * 128:(ch + 1) * 128],
                                w2e[:, ch * 1024 + hf * 512: ch * 1024 + hf * 512 + 512],
                                start=(ch == 0), stop=(ch == 15))
                        C.inc(ins, "eo", 1)
                        if hf == 0:
                            tensor.wait_ge(C.sem("eac"), EAC0 + 2 * t + 1)
                        po = pe[2 + (t % 2)]

            @block.scalar
            def _(scalar, e=e, EH0=C.n("eh") - 16 * NT, EO0e=C.n("eo") - 2 * NT):
                scalar = C.wrap(scalar, 's')
                nh = 0
                for t in range(NT):
                    hb = hTs[t % 2]
                    if t >= 2:
                        scalar.wait_ge(C.sem("eo"), EO0e + 2 * (t - 1))
                    for ot in range(16):
                        scalar.wait_ge(C.sem("eh"), EH0 + nh + 1)
                        scalar.activation(hb[:, ot * 128:(ot + 1) * 128],
                                          pe[ot % 2], AF.Relu,
                                          bias=b1s[:, e * 16 + ot: e * 16 + ot + 1])
                        C.inc(scalar.drain(), "hc", 1)
                        nh += 1

            @block.vector
            def _(vector, e=e, EO0=C.n("eo") - 2 * NT):
                vector = C.wrap(vector, 'v')
                n = 0
                for t in range(NT):
                    for hf in range(2):
                        vector.wait_ge(C.sem("eo"), EO0 + n + 1)
                        po = pe[2 + (t % 2)]
                        cwcol = cw[:, t * 8 + e: t * 8 + e + 1]
                        sl = slice(t * D + hf * 512, t * D + hf * 512 + 512)
                        vector.tensor_scalar(etmp[:], po, cwcol, None,
                                             ALU.mult)
                        vector.tensor_tensor(acc[:, sl], acc[:, sl], etmp[:],
                                             ALU.add)
                        C.inc(vector.drain(), "eac", 1)
                        n += 1
                C.inc(vector.drain(), "eoc", 1)

        pb2t = [po2[0], po2[1]]

        @block.tensor
        def _(tensor, EOC=C.n("eoc"), FB0=C.n("fb")):
            tensor.wait_ge(C.sem("eoc"), EOC)
            for t in range(NT):
                if t >= 2:
                    tensor.wait_ge(C.sem("fin%d" % (t % 2)), t // 2 - 1 + 1)
                pp = pb2t[t % 2]
                tensor.matmul(pp[:], cwT[0:8, t * 128:(t + 1) * 128],
                              b2b[:, 0:512], start=True, stop=True)
                C.inc(tensor.matmul(pproj[t % 2][:],
                                    cwT[0:8, t * 128:(t + 1) * 128],
                                    b2b[:, 512:1024], start=True, stop=True),
                      "fb", 1)

        @block.gpsimd
        def _(gpsimd, EOC=C.n("eoc"), ST0=C.n("rl_st0"), ST1=C.n("rl_st1")):
            gpsimd = C.wrap(gpsimd, 'g')
            gpsimd.wait_ge(C.sem("eoc"), EOC)
            gpsimd.wait_ge(C.sem("rl_st0"), ST0)
            gpsimd.wait_ge(C.sem("rl_st1"), ST1)
            gpsimd.wait_ge(C.sem("rl_tpc"), 2 * NT)
            for t in range(NT):
                if t >= 2:
                    gpsimd.wait_ge(C.sem("fin%d" % (t % 2)), t // 2)
                C.inc(gpsimd.dma_start(out=xr[t % 2][:],
                                       in_=x1o[t * 128:(t + 1) * 128, :]),
                      "xl%d" % (t % 2), 16)

        @block.vector
        def _(vector, FB0=C.n("fb") - NT, ST0=C.n("rl_st0"), ST1=C.n("rl_st1")):
            vector = C.wrap(vector, 'v')
            vector.wait_ge(C.sem("rl_st0"), ST0)
            vector.wait_ge(C.sem("rl_st1"), ST1)
            vector.wait_ge(C.sem("rl_tpc"), 2 * NT)
            for t in range(NT):
                ob = ocmb[t % 2]
                vector.wait_ge(C.sem("fb"), FB0 + t + 1)
                vector.wait_ge(C.sem("xl%d" % (t % 2)), 16 * (t // 2 + 1))
                if t >= 2:
                    vector.wait_ge(C.sem("fst%d" % (t % 2)), 16 * (t // 2))
                vector.tensor_tensor(ob[:, 0:512], xr[t % 2][:, 0:512],
                                     pb2t[t % 2][:], ALU.add)
                vector.tensor_tensor(ob[:, 512:1024], xr[t % 2][:, 512:1024],
                                     pproj[t % 2][:], ALU.add)
                C.inc(vector.tensor_tensor(ob[:], ob[:],
                                           acc[:, t * D:(t + 1) * D], ALU.add),
                      "fin%d" % (t % 2), 1)

        @block.sync
        def _(sync):
            nf = [0, 0]
            for t in range(NT):
                nf[t % 2] += 1
                sync.wait_ge(C.sem("fin%d" % (t % 2)), nf[t % 2])
                C.inc(sync.dma_start(out=x2o[t * 128:(t + 1) * 128, :],
                                     in_=ocmb[t % 2][:]), "fst%d" % (t % 2), 16)
            sync.wait_ge(C.sem("fst0"), C.n("fst0"))
            sync.wait_ge(C.sem("fst1"), C.n("fst1"))
            sync.wait_ge(C.sem("rl_st0"), C.n("rl_st0"))
            sync.wait_ge(C.sem("rl_st1"), C.n("rl_st1"))
    return nc


def build_F2():
    nc = bass.Bass("TRN2", target_bir_lowering=False)
    NT = TPC // 128
    HB = 512
    xd = nc.dram_tensor("xd", [TPC, D], F32, kind="ExternalInput").ap()
    oTd = nc.dram_tensor("oTd", [8 * NT * 128, 128], F32, kind="ExternalInput").ap()
    wod = nc.dram_tensor("wod", [D, D], F32, kind="ExternalInput").ap()
    w1bk = nc.dram_tensor("w1bk", [8 * 32 * 128, 128], F32, kind="ExternalInput").ap()
    w2bk = nc.dram_tensor("w2bk", [8 * 32 * 128, 128], F32, kind="ExternalInput").ap()
    h1bk = nc.dram_tensor("h1bk", [32 * 32 * 128, 128], F32, kind="ExternalInput").ap()
    h2bk = nc.dram_tensor("h2bk", [32 * 32 * 128, 128], F32, kind="ExternalInput").ap()
    w3bk = nc.dram_tensor("w3bk", [32 * 128, 1024], F32, kind="ExternalInput").ap()
    s1d = nc.dram_tensor("s1d", [128, 32], F32, kind="ExternalInput").ap()
    s2d = nc.dram_tensor("s2d", [128, 32], F32, kind="ExternalInput").ap()
    sh1d = nc.dram_tensor("sh1d", [128, 32], F32, kind="ExternalInput").ap()
    sh2d = nc.dram_tensor("sh2d", [128, 32], F32, kind="ExternalInput").ap()
    s3d = nc.dram_tensor("s3d", [1, 1024], F32, kind="ExternalInput").ap()
    x1o = nc.dram_tensor("x1o", [TPC, D], F32, kind="ExternalOutput").ap()
    x2o = nc.dram_tensor("x2o", [TPC, D], F32, kind="ExternalOutput").ap()
    ident_c = nc.inline_tensor(np.eye(128, dtype=BF), "ident_c")
    ones1_c = nc.inline_tensor(np.ones((1, 128), dtype=BF), "ones1_c")
    SQ2 = float(1.0 / np.sqrt(2.0))

    with ExitStack() as ctx:
        C = Ctx(nc, ctx)
        block = ctx.enter_context(nc.Block())
        C.ident = C.sb("ident", [128, 128], BF16)
        ones1 = C.sb("ones1", [1, 128], BF16)
        yT = C.sb("yT", [128, 8 * TPC], BF16)
        s1s = C.sb("s1s", [128, 32])
        s2s = C.sb("s2s", [128, 32])
        sh1s = C.sb("sh1s", [128, 32])
        sh2s = C.sb("sh2s", [128, 32])
        s3row = C.sb("s3row", [1, 1024], BF16)
        s3b = C.sb("s3b", [128, 1024])
        hT = C.sb("hT", [128, 32 * 512], BF16)
        gT = C.sb("gT", [128, 32 * 512], BF16)
        NHB = HB // 128
        xout = C.sb("xout", [128, NHB * D])
        wst = [C.sb("wstA", [128, 16 * 128], BF16),
               C.sb("wstB", [128, 16 * 128], BF16)]
        w2st = [C.sb("w2stA", [128, 32 * 128], BF16),
                C.sb("w2stB", [128, 32 * 128], BF16)]
        w3st = [C.sb("w3stA", [128, 1024], BF16), C.sb("w3stB", [128, 1024], BF16)]
        ev1 = C.sb("ev1", [128, HB])
        ev2 = C.sb("ev2", [128, HB])
        ev3 = C.sb("ev3", [128, HB])
        assert HB == 512

        @block.sync
        def _(sync):
            C.inc(sync.dma_start(out=C.ident[:], in_=ident_c.ap()), "cld", 16)
            C.inc(sync.dma_start(out=ones1[:], in_=ones1_c.ap()), "cld", 16)
            C.inc(sync.dma_start(out=s1s[:], in_=s1d), "cld", 16)
            C.inc(sync.dma_start(out=s2s[:], in_=s2d), "cld", 16)
            C.inc(sync.dma_start(out=sh1s[:], in_=sh1d), "cld", 16)
            C.inc(sync.dma_start(out=sh2s[:], in_=sh2d), "cld", 16)

        @block.gpsimd
        def _(gpsimd):
            gpsimd = C.wrap(gpsimd, 'g')
            C.inc(gpsimd.dma_start(out=s3row[:], in_=s3d), "cldg", 16)
        CLD = C.cnt["cld"]
        CLDG = C.cnt["cldg"]

        @block.tensor
        def _(tensor):
            tensor.wait_ge(C.sem("cld"), CLD)
            tensor.wait_ge(C.sem("cldg"), CLDG)

        @block.vector
        def _(vector):
            vector.wait_ge(C.sem("cld"), CLD)
            vector.wait_ge(C.sem("cldg"), CLDG)

        @block.scalar
        def _(scalar):
            scalar.wait_ge(C.sem("cld"), CLD)
            scalar.wait_ge(C.sem("cldg"), CLDG)

        ptps = [C.ps("ptpA", [128, 512], BF16), C.ps("ptpB", [128, 512], BF16)]
        pproj = [C.ps("pproj0", [128, 512]), C.ps("pproj1", [128, 512])]
        emit_resid_ln(C, block, xd, x1o, NT, yT, ptps, pproj, o_dram=oTd,
                      w_dram=wod, name="rl")
        xr = [C.rl_bufs[0], C.rl_bufs[1]]
        ocmb = [C.rl_bufs[2], C.rl_bufs[3]]

        pa = [C.ps("pa0", [128, 512]), C.ps("pa1", [128, 512])]
        pb = [C.ps("pb0", [128, 512]), C.ps("pb1", [128, 512])]

        @block.tensor
        def _(tensor, RLT=2 * NT):
            tensor.wait_ge(C.sem("rl_tpc"), RLT)
            tensor.matmul(pa[0][:], ones1[:], s3row[0:1, 0:512], start=True,
                          stop=True)
            C.inc(tensor.matmul(pa[1][:], ones1[:], s3row[0:1, 512:1024],
                                start=True, stop=True), "s3t", 1)

        @block.vector
        def _(vector):
            vector = C.wrap(vector, 'v')
            vector.wait_ge(C.sem("s3t"), 1)
            vector.tensor_copy(s3b[:, 0:512], pa[0][:])
            C.inc(vector.tensor_copy(s3b[:, 512:1024], pa[1][:]), "s3c", 1)

        for half in range(TPC // HB):
            tb = half * (HB // 512)
            WSH = [C.n("ws0"), C.n("ws1")]


            @block.gpsimd
            def _(gpsimd, half=half, EV0=C.n("ev")):
                gpsimd = C.wrap(gpsimd, 'g')
                for ot in range(32):
                    if ot >= 2:
                        gpsimd.wait_ge(C.sem("ev"), EV0 + ot - 1)
                    dst = wst[ot % 2]
                    for ch in range(8):
                        C.inc(gpsimd.dma_start(
                            out=dst[:, ch * 128:(ch + 1) * 128],
                            in_=w1bk[(ch * 32 + ot) * 128:(ch * 32 + ot + 1) * 128, :]),
                            "ws%d" % (ot % 2), 16)
                        C.inc(gpsimd.dma_start(
                            out=dst[:, (8 + ch) * 128:(9 + ch) * 128],
                            in_=w2bk[(ch * 32 + ot) * 128:(ch * 32 + ot + 1) * 128, :]),
                            "ws%d" % (ot % 2), 16)

            @block.tensor
            def _(tensor, half=half, tb=tb, WS0=WSH[0], WS1=WSH[1],
                  EV0=C.n("ev"), RLT=2 * NT,
                  S3C=C.n("s3c"), S4P=C.n("s4x")):
                tensor.wait_ge(C.sem("rl_tpc"), RLT)
                tensor.wait_ge(C.sem("s3c"), S3C)
                tensor.wait_ge(C.sem("s4x"), S4P)
                WSp = [WS0, WS1]
                for ot in range(32):
                    tensor.wait_ge(C.sem("ws%d" % (ot % 2)),
                                   WSp[ot % 2] + 256 * (ot // 2 + 1))
                    if ot >= 2:
                        tensor.wait_ge(C.sem("ev"), EV0 + ot - 1)
                    src = wst[ot % 2]
                    for ch in range(8):
                        tensor.matmul(
                            pa[ot % 2][:], src[:, ch * 128:(ch + 1) * 128],
                            yT[:, ch * TPC + tb * 512:
                               ch * TPC + (tb + 1) * 512],
                            start=(ch == 0), stop=(ch == 7))
                    for ch in range(8):
                        tensor.matmul(
                            pb[ot % 2][:], src[:, (8 + ch) * 128:(9 + ch) * 128],
                            yT[:, ch * TPC + tb * 512:
                               ch * TPC + (tb + 1) * 512],
                            start=(ch == 0), stop=(ch == 7))
                    C.inc(tensor.drain(), "st1", 1)

            @block.vector
            def _(vector, half=half, ST10=C.n("st1") - 32, SIL0=C.n("sil")):
                vector = C.wrap(vector, 'v')
                for ot in range(32):
                    vector.wait_ge(C.sem("st1"), ST10 + ot + 1)
                    s1c = s1s[:, ot:ot + 1]
                    s2c = s2s[:, ot:ot + 1]
                    vector.tensor_scalar(ev1[:], pa[ot % 2][:], s1c, None,
                                         ALU.mult)
                    C.inc(vector.drain(), "evs", 1)
                    vector.tensor_scalar(ev2[:], pb[ot % 2][:], s2c, None,
                                         ALU.mult)
                    vector.wait_ge(C.sem("sil"), SIL0 + ot + 1)
                    C.inc(vector.tensor_tensor(hT[:, ot * HB:(ot + 1) * HB],
                                               ev3[:], ev2[:], ALU.mult),
                          "ev", 1)

            @block.scalar
            def _(scalar, EVS0=C.n("evs") - 32):
                scalar = C.wrap(scalar, 's')
                for ot in range(32):
                    scalar.wait_ge(C.sem("evs"), EVS0 + ot + 1)
                    C.inc(scalar.activation(ev3[:], ev1[:], AF.Silu), "sil", 1)

            for stg, (wbk, scol, src_t, dst_t) in enumerate(
                    ((h1bk, sh1s, hT, gT), (h2bk, sh2s, gT, hT))):
                W20p = [C.n("w2s0"), C.n("w2s1")]

                @block.gpsimd
                def _(gpsimd, wbk=wbk, EV0=C.n("ge"), ST2A=C.n("st2")):
                    gpsimd = C.wrap(gpsimd, 'g')
                    gpsimd.wait_ge(C.sem("st2"), ST2A)
                    for ot in range(32):
                        if ot >= 2:
                            gpsimd.wait_ge(C.sem("ge"), EV0 + ot - 1)
                        dst = w2st[ot % 2]
                        for ch in range(32):
                            C.inc(gpsimd.dma_start(
                                out=dst[:, ch * 128:(ch + 1) * 128],
                                in_=wbk[(ch * 32 + ot) * 128:
                                        (ch * 32 + ot + 1) * 128, :]),
                                "w2s%d" % (ot % 2), 16)

                @block.tensor
                def _(tensor, src_t=src_t, W20p=W20p, EV0=C.n("ge"),
                      EVA=C.n("ev"), EVG_T=C.n("evg")):
                    tensor.wait_ge(C.sem("ev"), EVA)
                    tensor.wait_ge(C.sem("ge"), EV0)
                    for ot in range(32):
                        tensor.wait_ge(C.sem("w2s%d" % (ot % 2)),
                                       W20p[ot % 2] + 512 * (ot // 2 + 1))
                        if ot >= 2:
                            tensor.wait_ge(C.sem("evg"), EVG_T + ot - 1)
                        src = w2st[ot % 2]
                        for ch in range(32):
                            tensor.matmul(
                                pa[ot % 2][:], src[:, ch * 128:(ch + 1) * 128],
                                src_t[:, ch * HB: ch * HB + 512],
                                start=(ch == 0), stop=(ch == 31))
                        C.inc(tensor.drain(), "st2", 1)

                @block.vector
                def _(vector, scol=scol, ST20=C.n("st2") - 32,
                      GE0=C.n("ge")):
                    vector = C.wrap(vector, 'v')
                    for ot in range(32):
                        vector.wait_ge(C.sem("st2"), ST20 + ot + 1)
                        sc = scol[:, ot:ot + 1]
                        if ot >= 1:
                            vector.wait_ge(C.sem("ge"), GE0 + ot)
                        vector.tensor_scalar(ev1[:], pa[ot % 2][:], sc, None,
                                             ALU.mult)
                        C.inc(vector.drain(), "evg", 1)

                @block.scalar
                def _(scalar, dst_t=dst_t, EVG0=C.n("evg") - 32):
                    scalar = C.wrap(scalar, 's')
                    for ot in range(32):
                        scalar.wait_ge(C.sem("evg"), EVG0 + ot + 1)
                        C.inc(scalar.activation(dst_t[:, ot * HB:(ot + 1) * HB],
                                                ev1[:], AF.Gelu), "ge", 1)

            @block.vector
            def _(vector):
                vector = C.wrap(vector, 'v')
                C.inc(vector.memset(xout[:], 0.0), "xz", 1)

            W30p = [C.n("w3s0"), C.n("w3s1")]

            @block.gpsimd
            def _(gpsimd, EV0=C.n("s4")):
                gpsimd = C.wrap(gpsimd, 'g')
                for ot in range(32):
                    if ot >= 2:
                        gpsimd.wait_ge(C.sem("s4"), EV0 + NHB * (ot - 1))
                    dst = w3st[ot % 2]
                    C.inc(gpsimd.dma_start(
                        out=dst[:],
                        in_=w3bk[ot * 128:(ot + 1) * 128, :]),
                        "w3s%d" % (ot % 2), 16)

            @block.tensor
            def _(tensor, W30p=W30p, GEA=C.n("ge"), XZ=C.n("xz"),
                  S40=C.n("s4x")):
                tensor.wait_ge(C.sem("ge"), GEA)
                tensor.wait_ge(C.sem("xz"), XZ)
                n = 0
                for ot in range(32):
                    tensor.wait_ge(C.sem("w3s%d" % (ot % 2)),
                                   W30p[ot % 2] + 16 * (ot // 2 + 1))
                    src = w3st[ot % 2]
                    for t in range(NHB):
                        if n >= 2:
                            tensor.wait_ge(C.sem("s4x"), S40 + n - 1)
                        pp = pb[n % 2]
                        tensor.matmul(pp[:],
                                      hT[:, ot * HB + t * 128: ot * HB + (t + 1) * 128],
                                      src[:, 0:512], start=True, stop=True)
                        C.inc(tensor.matmul(pa[n % 2][:],
                                            hT[:, ot * HB + t * 128:
                                               ot * HB + (t + 1) * 128],
                                            src[:, 512:1024], start=True,
                                            stop=True), "s4", 1)
                        n += 1

            @block.vector
            def _(vector, S40=C.n("s4") - 32 * (HB // 128)):
                vector = C.wrap(vector, 'v')
                n = 0
                for ot in range(32):
                    for t in range(NHB):
                        vector.wait_ge(C.sem("s4"), S40 + n + 1)
                        sl = slice(t * D, t * D + 512)
                        sl2 = slice(t * D + 512, t * D + 1024)
                        vector.tensor_tensor(xout[:, sl], xout[:, sl],
                                             pb[n % 2][:], ALU.add)
                        C.inc(vector.tensor_tensor(xout[:, sl2], xout[:, sl2],
                                                   pa[n % 2][:], ALU.add),
                              "s4x", 1)
                        n += 1

            XLH = [C.n("xl0"), C.n("xl1")]

            @block.gpsimd
            def _(gpsimd, half=half, S4X=C.n("s4x"), ST0=C.n("rl_st0"),
                  ST1=C.n("rl_st1")):
                gpsimd = C.wrap(gpsimd, 'g')
                gpsimd.wait_ge(C.sem("rl_st0"), ST0)
                gpsimd.wait_ge(C.sem("rl_st1"), ST1)
                gpsimd.wait_ge(C.sem("rl_tpc"), 2 * NT)
                gpsimd.wait_ge(C.sem("s4x"), S4X)
                FINb = [C.n("fin0"), C.n("fin1")]
                for t in range(NHB):
                    tg = half * NHB + t
                    if t >= 2:
                        gpsimd.wait_ge(C.sem("fin%d" % (t % 2)),
                                       FINb[t % 2] + t // 2)
                    C.inc(gpsimd.dma_start(out=xr[t % 2][:],
                                           in_=x1o[tg * 128:(tg + 1) * 128, :]),
                          "xl%d" % (t % 2), 16)

            @block.vector
            def _(vector, half=half, XL0=XLH[0], XL1=XLH[1],
                  FS0=C.n("fst0"), FS1=C.n("fst1")):
                vector = C.wrap(vector, 'v')
                XL = [XL0, XL1]
                FS = [FS0, FS1]
                for t in range(NHB):
                    pq = t % 2
                    vector.wait_ge(C.sem("xl%d" % pq), XL[pq] + 16 * (t // 2 + 1))
                    if t >= 2:
                        vector.wait_ge(C.sem("fst%d" % pq), FS[pq] + 16 * (t // 2))
                    else:
                        vector.wait_ge(C.sem("fst%d" % pq), FS[pq])
                    ob = ocmb[pq]
                    vector.tensor_tensor(ob[:], xout[:, t * D:(t + 1) * D],
                                         s3b[:], ALU.mult)
                    C.inc(vector.tensor_tensor(ob[:], ob[:], xr[pq][:], ALU.add),
                          "fin%d" % pq, 1)

            @block.sync
            def _(sync, half=half, FIN0=C.n("fin0"), FIN1=C.n("fin1")):
                FINb = [FIN0 - (NHB + 1) // 2, FIN1 - NHB // 2]
                for t in range(NHB):
                    pq = t % 2
                    tg = half * NHB + t
                    sync.wait_ge(C.sem("fin%d" % pq), FINb[pq] + t // 2 + 1)
                    C.inc(sync.dma_start(out=x2o[tg * 128:(tg + 1) * 128, :],
                                         in_=ocmb[pq][:]), "fst%d" % pq, 16)

        @block.sync
        def _(sync):
            sync.wait_ge(C.sem("fst0"), C.n("fst0"))
            sync.wait_ge(C.sem("fst1"), C.n("fst1"))
            sync.wait_ge(C.sem("rl_st0"), C.n("rl_st0"))
            sync.wait_ge(C.sem("rl_st1"), C.n("rl_st1"))
    return nc


def build_F0a():
    nc = bass.Bass("TRN2", target_bir_lowering=False)
    NT = TPC // 128
    xd = nc.dram_tensor("xd", [TPC, D], F32, kind="ExternalInput").ap()
    otd = nc.dram_tensor("otd", [TPC, D], F32, kind="ExternalInput").ap()
    wrd = nc.dram_tensor("wrd", [D, 32], F32, kind="ExternalInput").ap()
    x1o = nc.dram_tensor("x1o", [TPC, D], F32, kind="ExternalOutput").ap()
    yo = nc.dram_tensor("yo", [TPC, D], BF16, kind="ExternalOutput").ap()
    lgo = nc.dram_tensor("lgo", [TPC, 32], F32, kind="ExternalOutput").ap()
    cto = nc.dram_tensor("cto", [32, 1], F32, kind="ExternalOutput").ap()
    ident_c = nc.inline_tensor(np.eye(128, dtype=BF), "ident_c")
    ones_c = nc.inline_tensor(np.ones((128, 1), dtype=BF), "ones_c")

    with ExitStack() as ctx:
        C = Ctx(nc, ctx)
        block = ctx.enter_context(nc.Block())
        C.ident = C.sb("ident", [128, 128], BF16)
        onescol = C.sb("onescol", [128, 1], BF16)
        yT = C.sb("yT", [128, 8 * TPC], BF16)
        wrb = C.sb("wrb", [128, 8 * 32], BF16)
        lg2 = [C.sb("lgA", [128, 32]), C.sb("lgB", [128, 32])]
        m1 = C.sb("m1", [128, 1])
        ohb = C.sb("ohb", [128, 32], BF16)
        ctr = C.sb("ctr", [32, 1])

        @block.sync
        def _(sync):
            C.inc(sync.dma_start(out=C.ident[:], in_=ident_c.ap()), "cld", 16)
            C.inc(sync.dma_start(out=onescol[:], in_=ones_c.ap()), "cld", 16)

        @block.gpsimd
        def _(gpsimd):
            gpsimd = C.wrap(gpsimd, 'g')
            for c in range(8):
                C.inc(gpsimd.dma_start(out=wrb[:, c * 32:(c + 1) * 32],
                                       in_=wrd[c * 128:(c + 1) * 128, :]),
                      "cldg", 16)
        CLD = C.cnt["cld"]
        CLDG = C.cnt["cldg"]

        @block.tensor
        def _(tensor):
            tensor.wait_ge(C.sem("cld"), CLD)
            tensor.wait_ge(C.sem("cldg"), CLDG)

        @block.vector
        def _(vector):
            vector.wait_ge(C.sem("cld"), CLD)
            vector.wait_ge(C.sem("cldg"), CLDG)

        @block.scalar
        def _(scalar):
            scalar.wait_ge(C.sem("cld"), CLD)
            scalar.wait_ge(C.sem("cldg"), CLDG)

        ptps = [C.ps("ptpA", [128, 512], BF16), C.ps("ptpB", [128, 512], BF16)]
        pproj = [C.ps("pproj0", [128, 512]), C.ps("pproj1", [128, 512])]
        emit_resid_ln(C, block, xd, x1o, NT, yT, ptps, pproj, otok_dram=otd,
                      y_out=yo, name="rl")

        prt = pproj[0][:, 0:32]
        pct = pproj[1][0:32, 0:1]

        @block.tensor
        def _(tensor):
            tensor.wait_ge(C.sem("rl_tpc"), 2 * NT)
            for t in range(NT):
                if t >= 1:
                    tensor.wait_ge(C.sem("rtc"), t)
                for ch in range(8):
                    ins = tensor.matmul(
                        prt,
                        yT[:, ch * TPC + t * 128: ch * TPC + (t + 1) * 128],
                        wrb[:, ch * 32:(ch + 1) * 32],
                        start=(ch == 0), stop=(ch == 7))
                C.inc(ins, "rt", 1)
                tensor.wait_ge(C.sem("ohd"), t + 1)
                C.inc(tensor.matmul(pct, ohb[:], onescol[:],
                                    start=(t == 0), stop=(t == NT - 1)), "ct", 1)

        @block.vector
        def _(vector, RT0=C.n("rt") - NT, CT0=C.n("ct") - NT):
            vector = C.wrap(vector, 'v')
            for t in range(NT):
                vector.wait_ge(C.sem("rt"), RT0 + t + 1)
                lg = lg2[t % 2]
                if t >= 2:
                    vector.wait_ge(C.sem("lgs%d" % (t % 2)), 16 * (t // 2))
                vector.tensor_copy(lg[:], prt)
                C.inc(vector.drain(), "rtc", 1)
                vector.reduce_max(m1[:], lg[:], axis=AX)
                vector.drain()
                if t >= 1:
                    vector.wait_ge(C.sem("ct"), CT0 + t)
                C.inc(vector.tensor_scalar(ohb[:], lg[:], m1[:], 0.0,
                                           ALU.subtract, ALU.is_ge), "ohd", 1)

        @block.gpsimd
        def _(gpsimd, RTC0=C.n("rtc") - NT, OHD0=C.n("ohd") - NT):
            gpsimd = C.wrap(gpsimd, 'g')
            for t in range(NT):
                gpsimd.wait_ge(C.sem("ohd"), OHD0 + t + 1)
                C.inc(gpsimd.dma_start(out=lgo[t * 128:(t + 1) * 128, :],
                                       in_=lg2[t % 2][:]), "lgs%d" % (t % 2), 16)

        @block.vector
        def _(vector, CT=C.n("ct")):
            vector = C.wrap(vector, 'v')
            vector.wait_ge(C.sem("ct"), CT)
            C.inc(vector.tensor_copy(ctr[:], pct), "ctc", 1)

        @block.sync
        def _(sync):
            sync.wait_ge(C.sem("ctc"), 1)
            C.inc(sync.dma_start(out=cto, in_=ctr[:]), "cts", 16)
            sync.wait_ge(C.sem("cts"), 16)
            sync.wait_ge(C.sem("lgs0"), C.n("lgs0"))
            sync.wait_ge(C.sem("lgs1"), C.n("lgs1"))
            sync.wait_ge(C.sem("rl_st0"), C.n("rl_st0"))
            sync.wait_ge(C.sem("rl_st1"), C.n("rl_st1"))
    return nc


def build_F0b():
    nc = bass.Bass("TRN2", target_bir_lowering=False)
    NT = TPC // 128
    E, CL = 32, 256
    x1d = nc.dram_tensor("x1d", [TPC, D], F32, kind="ExternalInput").ap()
    yd = nc.dram_tensor("yd", [TPC, D], BF16, kind="ExternalInput").ap()
    lgd = nc.dram_tensor("lgd", [TPC, 32], F32, kind="ExternalInput").ap()
    cta = nc.dram_tensor("cta", [8, 32], F32, kind="ExternalInput").ap()
    mkc = nc.dram_tensor("mkc", [8, 1], F32, kind="ExternalInput").ap()
    wgd = nc.dram_tensor("wgd", [E * 1024, 1024], F32, kind="ExternalInput").ap()
    wud = nc.dram_tensor("wud", [E * 1024, 1024], F32, kind="ExternalInput").ap()
    wdd = nc.dram_tensor("wdd", [E * 1024, 1024], F32, kind="ExternalInput").ap()
    x2o = nc.dram_tensor("x2o", [TPC, D], F32, kind="ExternalOutput").ap()
    bufin = nc.dram_tensor("bufin", [E * CL + 128, D], BF16)
    bufout = nc.dram_tensor("bufout", [E * CL + 128, D], F32)
    ident_c = nc.inline_tensor(np.eye(128, dtype=BF), "ident_c")
    identf_c = nc.inline_tensor(np.eye(128, dtype=np.float32), "identf_c")
    tril_c = nc.inline_tensor(np.triu(np.ones((128, 128))).astype(BF), "tril_c")
    e256_c = nc.inline_tensor(np.broadcast_to(
        np.arange(32, dtype=np.float32) * CL, (128, 32)).copy(), "e256_c")

    with ExitStack() as ctx:
        C = Ctx(nc, ctx)
        block = ctx.enter_context(nc.Block())
        C.ident = C.sb("ident", [128, 128], BF16)
        identf = C.sb("identf", [128, 128])
        tril = C.sb("tril", [128, 128], BF16)
        e256 = C.sb("e256", [128, 32])
        cts = C.sb("cts", [8, 32], BF16)
        mkcs = C.sb("mkcs", [8, 1], BF16)
        offs = C.sb("offs", [32, 1])
        run = C.sb("run", [32, 1])
        lg2 = [C.sb("lgA", [128, 32]), C.sb("lgB", [128, 32])]
        m1 = C.sb("m1", [128, 1])
        esum = C.sb("esum", [128, 1])
        gate = C.sb("gate", [128, NT])
        elg = C.sb("elg", [128, 32])
        ohb = C.sb("ohb", [128, 32], BF16)
        oh32 = C.sb("oh32", [128, 32])
        cf = C.sb("cf", [32, 128])
        ct_t = C.sb("ct_t", [128, 32])
        gt_t = C.sb("gt_t", [128, 32])
        keep = C.sb("keep", [128, 32])
        tmp32 = C.sb("tmp32", [128, 32])
        dstf = C.sb("dstf", [128, 1])
        dstu = C.sb("dstu", [128, NT], U32)
        ytile = [C.sb("ytA", [128, D], BF16), C.sb("ytB", [128, D], BF16)]
        bufT = C.sb("bufT", [128, 8 * CL], BF16)
        midT = C.sb("midT", [128, 8 * CL], BF16)
        silt = C.sb("silt", [128, CL])
        wge2 = [C.sb("wgeA", [128, 8 * 1024], BF16),
                C.sb("wgeB", [128, 8 * 1024], BF16)]
        wue2 = [C.sb("wueA", [128, 8 * 1024], BF16),
                C.sb("wueB", [128, 8 * 1024], BF16)]
        wde2 = [C.sb("wdeA", [128, 8 * 1024], BF16),
                C.sb("wdeB", [128, 8 * 1024], BF16)]
        obuf = [C.sb("obA", [128, D]), C.sb("obB", [128, D])]
        x1t = [C.sb("x1A", [128, D]), C.sb("x1B", [128, D])]
        gott = [C.sb("goA", [128, D]), C.sb("goB", [128, D])]

        @block.sync
        def _(sync):
            C.inc(sync.dma_start(out=C.ident[:], in_=ident_c.ap()), "cld", 16)
            C.inc(sync.dma_start(out=identf[:], in_=identf_c.ap()), "cld", 16)
            C.inc(sync.dma_start(out=tril[:], in_=tril_c.ap()), "cld", 16)
            C.inc(sync.dma_start(out=e256[:], in_=e256_c.ap()), "cld", 16)

        @block.gpsimd
        def _(gpsimd):
            gpsimd = C.wrap(gpsimd, 'g')
            C.inc(gpsimd.dma_start(out=cts[:], in_=cta), "cldg", 16)
            C.inc(gpsimd.dma_start(out=mkcs[:], in_=mkc), "cldg", 16)
        CLD = C.cnt["cld"]
        CLDG = C.cnt["cldg"]

        @block.tensor
        def _(tensor):
            tensor.wait_ge(C.sem("cld"), CLD)
            tensor.wait_ge(C.sem("cldg"), CLDG)

        @block.vector
        def _(vector):
            vector.wait_ge(C.sem("cld"), CLD)
            vector.wait_ge(C.sem("cldg"), CLDG)

        @block.scalar
        def _(scalar):
            scalar.wait_ge(C.sem("cld"), CLD)
            scalar.wait_ge(C.sem("cldg"), CLDG)

        pof = C.ps("pof", [32, 128])
        ptr = C.ps("ptr", [128, 128])

        @block.tensor
        def _(tensor):
            C.inc(tensor.matmul(pof[0:32, 0:1], cts[:], mkcs[:], start=True,
                                stop=True), "oft", 1)

        @block.vector
        def _(vector):
            vector = C.wrap(vector, 'v')
            vector.wait_ge(C.sem("oft"), 1)
            vector.tensor_copy(offs[:], pof[0:32, 0:1])
            vector.memset(run[:], 0.0)
            C.inc(vector.drain(), "ofc", 1)

        @block.sync
        def _(sync):
            for t in range(NT):
                if t >= 2:
                    sync.wait_ge(C.sem("exg"), t - 1)
                C.inc(sync.dma_start(out=lg2[t % 2][:],
                                     in_=lgd[t * 128:(t + 1) * 128, :]),
                      "lgl%d" % (t % 2), 16)

        PC0 = C.n("pc")
        TR10 = C.n("tr1")

        @block.vector
        def _(vector):
            vector = C.wrap(vector, 'v')
            vector.wait_ge(C.sem("ofc"), 1)
            for t in range(NT):
                vector.wait_ge(C.sem("lgl%d" % (t % 2)), 16 * (t // 2 + 1))
                vector.reduce_max(m1[:], lg2[t % 2][:], axis=AX)
                vector.drain()
                vector.tensor_scalar(ohb[:], lg2[t % 2][:], m1[:], 0.0,
                                     ALU.subtract, ALU.is_ge)
                vector.tensor_scalar(m1[:], m1[:], -1.0, None, ALU.mult)
                C.inc(vector.drain(), "ohd", 1)
                vector.wait_ge(C.sem("pc"), PC0 + t + 1)
                vector.tensor_scalar(cf[:], pof[:], run[:], None, ALU.add)
                vector.drain()
                C.inc(vector.tensor_copy(run[:], cf[:, 127:128]), "cfd", 1)
                vector.wait_ge(C.sem("tr1"), TR10 + t + 1)
                vector.tensor_scalar(cf[:], cf[:], offs[:], None, ALU.add)
                C.inc(vector.drain(), "cfd", 1)
                vector.wait_ge(C.sem("exg"), t + 1)
                vector.reciprocal(gate[:, t:t + 1], esum[:])
                vector.drain()
                vector.wait_ge(C.sem("trp"), 2 * (t + 1))
                vector.tensor_copy(ct_t[:], ptr[:, 0:32])
                vector.tensor_copy(gt_t[:], ptr[:, 64:96])
                C.inc(vector.drain(), "trc", 1)
                vector.tensor_copy(oh32[:], ohb[:])
                vector.tensor_scalar(keep[:], gt_t[:], 640.0, None, ALU.is_le)
                vector.tensor_tensor(keep[:], keep[:], oh32[:], ALU.mult)
                vector.tensor_scalar(tmp32[:], ct_t[:], float(CL), None,
                                     ALU.is_le)
                vector.tensor_tensor(keep[:], keep[:], tmp32[:], ALU.mult)
                vector.tensor_scalar(tmp32[:], ct_t[:], -1.0, None, ALU.add)
                vector.tensor_tensor(tmp32[:], tmp32[:], e256[:], ALU.add)
                vector.tensor_tensor(tmp32[:], tmp32[:], keep[:], ALU.mult)
                vector.reduce_sum(dstf[:], tmp32[:], axis=AX)
                vector.reduce_sum(m1[:], keep[:], axis=AX)
                vector.drain()
                vector.tensor_scalar(m1[:], m1[:], -float(E * CL),
                                     float(E * CL), ALU.mult, ALU.add)
                vector.tensor_tensor(dstf[:], dstf[:], m1[:], ALU.add)
                C.inc(vector.tensor_copy(dstu[:, t:t + 1], dstf[:]), "dst", 1)

        @block.scalar
        def _(scalar, OHD0=C.n("ohd") - NT):
            scalar = C.wrap(scalar, 's')
            for t in range(NT):
                scalar.wait_ge(C.sem("ohd"), OHD0 + t + 1)
                C.inc(scalar.activation(elg[:], lg2[t % 2][:], AF.Exp,
                                        bias=m1[:], accum_out=esum[:]),
                      "exg", 1)

        @block.tensor
        def _(tensor, OHD0=C.n("ohd") - NT, TRC0=C.n("trc") - NT):
            for t in range(NT):
                tensor.wait_ge(C.sem("ohd"), OHD0 + t + 1)
                C.inc(tensor.matmul(pof[:], ohb[:], tril[:], start=True,
                                    stop=True), "pc", 1)
                tensor.wait_ge(C.sem("cfd"), 2 * t + 1)
                C.inc(tensor.transpose(ptr[:, 0:32], cf[:], identf[0:32, 0:32]),
                      "tr1", 1)
                tensor.wait_ge(C.sem("cfd"), 2 * t + 2)
                C.inc(tensor.transpose(ptr[:, 64:96], cf[:],
                                       identf[0:32, 0:32]), "trp", 2)
                if t < NT - 1:
                    tensor.wait_ge(C.sem("trc"), TRC0 + t + 1)


        @block.gpsimd
        def _(gpsimd, DST0=C.n("dst") - NT):
            gpsimd = C.wrap(gpsimd, 'g')
            gpsimd.memset(ytile[0][:], 0.0)
            gpsimd.drain()
            for r in range(65):
                C.inc(gpsimd.dma_start(out=bufin.ap()[r * 128:(r + 1) * 128, :],
                                       in_=ytile[0][:]), "bz", 16)
            gpsimd.wait_ge(C.sem("bz"), 65 * 16)
            for t in range(NT):
                C.inc(gpsimd.dma_start(out=ytile[t % 2][:],
                                       in_=yd[t * 128:(t + 1) * 128, :]),
                      "yl%d" % (t % 2), 16)
                gpsimd.wait_ge(C.sem("dst"), DST0 + t + 1)
                gpsimd.wait_ge(C.sem("yl%d" % (t % 2)), 16 * (t // 2 + 1))
                gpsimd.wait_ge(C.sem("sct"), 16 * t)
                C.inc(gpsimd.indirect_dma_start(
                    out=bufin.ap(), out_offset=IndirectOffsetOnAxis(
                        dstu[:, t:t + 1], 0),
                    in_=ytile[t % 2][:], in_offset=None), "sct", 16)

        pg = [C.ps("pg0", [128, CL]), C.ps("pg1", [128, CL])]
        pdo = [C.ps("pd0", [128, 512]), C.ps("pd1", [128, 512])]
        ptb = C.ps("ptb", [128, 128], BF16)

        for e in range(E):
            EWL = C.n("ewl%d" % (e % 2))
            wge = wge2[e % 2]
            wue = wue2[e % 2]
            wde = wde2[e % 2]

            @block.gpsimd
            def _(gpsimd, e=e, DNP=C.n("dn") - 4, BTP=C.n("btc") - 16,
                  wge=wge, wue=wue, wde=wde):
                gpsimd = C.wrap(gpsimd, 'g')
                if e >= 2:
                    gpsimd.wait_ge(C.sem("dn"), DNP)
                if e == 0:
                    gpsimd.wait_ge(C.sem("sct"), C.n("sct"))
                for ch in range(8):
                    C.inc(gpsimd.dma_start(
                        out=wge[:, ch * 1024:(ch + 1) * 1024],
                        in_=wgd[e * 1024 + ch * 128: e * 1024 + (ch + 1) * 128, :]),
                        "ewl%d" % (e % 2), 16)
                    C.inc(gpsimd.dma_start(
                        out=wue[:, ch * 1024:(ch + 1) * 1024],
                        in_=wud[e * 1024 + ch * 128: e * 1024 + (ch + 1) * 128, :]),
                        "ewl%d" % (e % 2), 16)
                    C.inc(gpsimd.dma_start(
                        out=wde[:, ch * 1024:(ch + 1) * 1024],
                        in_=wdd[e * 1024 + ch * 128: e * 1024 + (ch + 1) * 128, :]),
                        "ewl%d" % (e % 2), 16)
                if e >= 1:
                    gpsimd.wait_ge(C.sem("btc"), BTP + 16)
                for half in range(2):
                    C.inc(gpsimd.dma_start(
                        out=ytile[half][:],
                        in_=bufin.ap()[e * CL + half * 128:
                                       e * CL + (half + 1) * 128, :]),
                        "ebl", 16)
            EWL_T = C.n("ewl%d" % (e % 2))
            EBL_T = C.n("ebl")

            @block.tensor
            def _(tensor, e=e, EWL_T=EWL_T, EBL_T=EBL_T, BTC0=C.n("btc"),
                  MDC0=C.n("mdc"), EVO0=C.n("evo"), wge=wge, wue=wue, wde=wde):
                tensor.wait_ge(C.sem("ebl"), EBL_T)
                if e == 0:
                    tensor.wait_ge(C.sem("trc"), C.n("trc"))
                for half in range(2):
                    for f in range(8):
                        i = half * 8 + f
                        if i >= 1:
                            tensor.wait_ge(C.sem("btc"), BTC0 + i)
                        tensor.transpose(ptb[:], ytile[half][:, f * 128:(f + 1) * 128],
                                         C.ident[:])
                        C.inc(tensor.drain(), "btr", 1)
                tensor.wait_ge(C.sem("ewl%d" % (e % 2)), EWL_T)
                tensor.wait_ge(C.sem("btc"), BTC0 + 16)
                for ft in range(8):
                    for which, wsb in ((0, wge), (1, wue)):
                        pp = pg[which]
                        for ch in range(8):
                            ins = tensor.matmul(
                                pp[:],
                                wsb[:, ch * 1024 + ft * 128: ch * 1024 + (ft + 1) * 128],
                                bufT[:, ch * CL: ch * CL + CL],
                                start=(ch == 0), stop=(ch == 7))
                    C.inc(ins, "gu", 1)
                    tensor.wait_ge(C.sem("mdc"), MDC0 + ft + 1)
                for half in range(2):
                    for hf in range(2):
                        pp = pdo[hf]
                        for ch in range(8):
                            ins = tensor.matmul(
                                pp[:],
                                midT[:, ch * CL + half * 128: ch * CL + half * 128 + 128],
                                wde[:, ch * 1024 + hf * 512: ch * 1024 + hf * 512 + 512],
                                start=(ch == 0), stop=(ch == 7))
                        C.inc(ins, "dn", 1)
                    tensor.wait_ge(C.sem("evo"), EVO0 + half + 1)

            @block.scalar
            def _(scalar, e=e, BTR0=C.n("btr") - 16, GU0=C.n("gu") - 8):
                scalar = C.wrap(scalar, 's')
                for i in range(16):
                    scalar.wait_ge(C.sem("btr"), BTR0 + i + 1)
                    if e >= 1 and i == 0:
                        scalar.wait_ge(C.sem("gu"), GU0)
                    half, f = i // 8, i % 8
                    scalar.copy(bufT[:, f * CL + half * 128: f * CL + half * 128 + 128],
                                ptb[:])
                    C.inc(scalar.drain(), "btc", 1)
                for ft in range(8):
                    scalar.wait_ge(C.sem("gu"), GU0 + ft + 1)
                    C.inc(scalar.activation(silt[:], pg[0][:], AF.Silu),
                          "sil", 1)

            @block.vector
            def _(vector, e=e, SIL0=C.n("sil") - 8, DN0=C.n("dn") - 4,
                  EODP=C.n("eod")):
                vector = C.wrap(vector, 'v')
                if e >= 1:
                    vector.wait_ge(C.sem("eod"), EODP)
                for ft in range(8):
                    vector.wait_ge(C.sem("sil"), SIL0 + ft + 1)
                    C.inc(vector.tensor_tensor(
                        midT[:, ft * CL:(ft + 1) * CL], silt[:], pg[1][:],
                        ALU.mult), "mdc", 1)
                for half in range(2):
                    for hf in range(2):
                        vector.wait_ge(C.sem("dn"), DN0 + half * 2 + hf + 1)
                        vector.tensor_copy(
                            ytile[0][:, hf * 512:(hf + 1) * 512]
                            if False else obuf[half][:, hf * 512:(hf + 1) * 512],
                            pdo[hf][:])
                    C.inc(vector.drain(), "evo", 1)

            @block.gpsimd
            def _(gpsimd, e=e, EVO0=C.n("evo") - 2):
                gpsimd = C.wrap(gpsimd, 'g')
                for half in range(2):
                    gpsimd.wait_ge(C.sem("evo"), EVO0 + half + 1)
                    C.inc(gpsimd.dma_start(
                        out=bufout.ap()[e * CL + half * 128:
                                        e * CL + (half + 1) * 128, :],
                        in_=obuf[half][:]), "eod", 16)

        XGH = [C.n("xg0"), C.n("xg1")]

        @block.gpsimd
        def _(gpsimd, EOD=C.n("eod"), DST0=C.n("dst") - NT):
            gpsimd = C.wrap(gpsimd, 'g')
            gpsimd.memset(x1t[0][:], 0.0)
            gpsimd.drain()
            C.inc(gpsimd.dma_start(out=bufout.ap()[E * CL:E * CL + 128, :],
                                   in_=x1t[0][:]), "zr", 16)
            gpsimd.wait_ge(C.sem("eod"), EOD)
            gpsimd.wait_ge(C.sem("zr"), 16)
            for t in range(NT):
                if t >= 2:
                    gpsimd.wait_ge(C.sem("fin%d" % (t % 2)), t // 2)
                    gpsimd.wait_ge(C.sem("fst%d" % (t % 2)), 16 * (t // 2))
                C.inc(gpsimd.dma_start(out=x1t[t % 2][:],
                                       in_=x1d[t * 128:(t + 1) * 128, :]),
                      "xg%d" % (t % 2), 16)
                C.inc(gpsimd.dma_start(out=ytile[t % 2][:],
                                       in_=yd[t * 128:(t + 1) * 128, :]),
                      "xg%d" % (t % 2), 16)
                C.inc(gpsimd.indirect_dma_start(
                    out=gott[t % 2][:], out_offset=None,
                    in_=bufout.ap(),
                    in_offset=IndirectOffsetOnAxis(dstu[:, t:t + 1], 0)),
                    "xg%d" % (t % 2), 16)

        @block.vector
        def _(vector):
            vector = C.wrap(vector, 'v')
            XG = XGH
            for t in range(NT):
                pq = t % 2
                vector.wait_ge(C.sem("xg%d" % pq), XG[pq] + 48 * (t // 2 + 1))
                if t >= 2:
                    vector.wait_ge(C.sem("fst%d" % pq), 16 * (t // 2))
                gcol = gate[:, t:t + 1]
                vector.tensor_scalar(gott[pq][:], gott[pq][:], gcol, None,
                                     ALU.mult)
                vector.tensor_tensor(gott[pq][:], gott[pq][:], ytile[pq][:],
                                     ALU.add)
                C.inc(vector.tensor_tensor(gott[pq][:], gott[pq][:],
                                           x1t[pq][:], ALU.add),
                      "fin%d" % pq, 1)

        @block.sync
        def _(sync):
            nf = [0, 0]
            for t in range(NT):
                pq = t % 2
                nf[pq] += 1
                sync.wait_ge(C.sem("fin%d" % pq), nf[pq])
                C.inc(sync.dma_start(out=x2o[t * 128:(t + 1) * 128, :],
                                     in_=gott[pq][:]), "fst%d" % pq, 16)
            sync.wait_ge(C.sem("fst0"), C.n("fst0"))
            sync.wait_ge(C.sem("fst1"), C.n("fst1"))
    return nc


_NC_CACHE = {}


def _get(name, fn):
    if name not in _NC_CACHE:
        _NC_CACHE[name] = fn()
    return _NC_CACHE[name]


def _blockT(oT_slice):
    return np.ascontiguousarray(
        oT_slice.reshape(8, 128, 16, 128).transpose(0, 2, 1, 3).reshape(-1, 128))


def kernel(**inputs):
    I = {k: np.asarray(v) for k, v in inputs.items()}
    x0 = np.ascontiguousarray(I["L0_attn_x"], dtype=np.float32)
    AC = np.ascontiguousarray

    def run(nc, maps):
        return run_bass_kernel_spmd(nc, maps, core_ids=list(range(8)))

    def hslice(w, p, base=0):
        h0 = p * 8
        return AC(w[:, base + h0 * 64: base + (h0 + 8) * 64])

    W = I["L0_attn_W_kqv"]
    m = I["L0_attn_m"].reshape(16)
    nc = _get("A0", build_A0)
    maps = []
    for c in range(8):
        b, p = c // 2, c % 2
        maps.append(dict(
            xb=AC(x0[b]), wk=hslice(W, p, 0), wq=hslice(W, p, 1024),
            wv=hslice(W, p, 2048),
            mh=AC(m[p * 8:(p + 1) * 8].reshape(4, 2).T)))
    res = run(nc, maps)
    o0 = np.zeros((4, S, D), np.float32)
    for c in range(8):
        b, p = c // 2, c % 2
        o0[b][:, p * 512:(p + 1) * 512] = np.asarray(res.results[c]["oT"]).T

    nc = _get("F0a", build_F0a)
    maps = []
    for c in range(8):
        b, p = c // 2, c % 2
        sl = slice(p * TPC, (p + 1) * TPC)
        maps.append(dict(xd=AC(x0[b][sl]), otd=AC(o0[b][sl]),
                         wrd=AC(I["L0_ffn_w_router"])))
    res = run(nc, maps)
    x1 = [np.asarray(res.results[c]["x1o"]) for c in range(8)]
    yy = [np.asarray(res.results[c]["yo"]) for c in range(8)]
    lgs = [np.asarray(res.results[c]["lgo"]) for c in range(8)]
    cts = np.stack([np.asarray(res.results[c]["cto"]).ravel() for c in range(8)])

    nc = _get("F0b", build_F0b)
    maps = []
    for c in range(8):
        mk = np.zeros((8, 1), np.float32)
        mk[:c] = 1.0
        maps.append(dict(
            x1d=x1[c], yd=yy[c], lgd=lgs[c], cta=AC(cts.astype(np.float32)),
            mkc=mk,
            wgd=AC(I["L0_ffn_w_gate"].reshape(32 * 1024, 1024)),
            wud=AC(I["L0_ffn_w_up"].reshape(32 * 1024, 1024)),
            wdd=AC(I["L0_ffn_w_down"].reshape(32 * 1024, 1024))))
    res = run(nc, maps)
    x2 = [np.asarray(res.results[c]["x2o"]) for c in range(8)]

    Wq1 = I["L1_attn_w_qkv"]
    nc = _get("A1", build_A1)
    maps = []
    for c in range(8):
        b, p = c // 2, c % 2
        xb = np.concatenate([x2[2 * b], x2[2 * b + 1]], axis=0)
        maps.append(dict(xb=AC(xb), wq=hslice(Wq1, p, 0),
                         wk=hslice(Wq1, p, 1024), wv=hslice(Wq1, p, 2048)))
    res = run(nc, maps)
    o1T = np.zeros((4, D, S), np.float32)
    for c in range(8):
        b, p = c // 2, c % 2
        o1T[b][p * 512:(p + 1) * 512] = np.asarray(res.results[c]["oT"])

    w8 = I["L1_ffn_weight"]
    w8u = w8.view(np.uint8) if w8.dtype != np.uint8 else w8
    nc = _get("F1", build_F1)
    maps = []
    for c in range(8):
        b, p = c // 2, c % 2
        sl = slice(p * TPC, (p + 1) * TPC)
        maps.append(dict(
            xd=x2[c], oTd=_blockT(o1T[b][:, sl]),
            wod=AC(I["L1_attn_w_out"]),
            w8d=AC(w8u.T),
            wsd=np.full((128, 1), float(I["L1_ffn_weight_scale"]), np.float32),
            bd=AC(I["L1_ffn_bias"].reshape(1, D))))
    res = run(nc, maps)
    x3 = [np.asarray(res.results[c]["x2o"]) for c in range(8)]

    Wq2 = I["L2_attn_Wq"].reshape(D, 1024)
    Wku = I["L2_attn_Wk_up"].reshape(256, 1024)
    Wvu = I["L2_attn_Wv_up"].reshape(256, 1024)
    nc = _get("A2", build_A2)
    maps = []
    for c in range(8):
        b, p = c // 2, c % 2
        xb = np.concatenate([x3[2 * b], x3[2 * b + 1]], axis=0)
        maps.append(dict(xb=AC(xb), wq=hslice(Wq2, p),
                         wkvd=AC(I["L2_attn_Wkv_down"]),
                         wku=hslice(Wku, p), wvu=hslice(Wvu, p)))
    res = run(nc, maps)
    o2T = np.zeros((4, D, S), np.float32)
    for c in range(8):
        b, p = c // 2, c % 2
        o2T[b][p * 512:(p + 1) * 512] = np.asarray(res.results[c]["oT"])

    def blk4096(w):
        wt = AC(w.T)
        ci, co = wt.shape
        return AC(wt.reshape(ci // 128, 128, co // 128, 128)
                  .transpose(0, 2, 1, 3).reshape(-1, 128))
    nc = _get("F2", build_F2)
    f2w = dict(
        wod=AC(I["L2_attn_Wout"].reshape(1024, D)),
        w1bk=blk4096(I["L2_ffn_w1_w"]), w2bk=blk4096(I["L2_ffn_w2_w"]),
        h1bk=blk4096(I["L2_ffn_h1_w"]), h2bk=blk4096(I["L2_ffn_h2_w"]),
        w3bk=AC(I["L2_ffn_w3_w"].T),
        s1d=AC(I["L2_ffn_w1_s"].reshape(32, 128).T),
        s2d=AC(I["L2_ffn_w2_s"].reshape(32, 128).T),
        sh1d=AC(I["L2_ffn_h1_s"].reshape(32, 128).T),
        sh2d=AC(I["L2_ffn_h2_s"].reshape(32, 128).T),
        s3d=AC(I["L2_ffn_w3_s"].reshape(1, 1024)))
    maps = []
    for c in range(8):
        b, p = c // 2, c % 2
        sl = slice(p * TPC, (p + 1) * TPC)
        maps.append(dict(xd=x3[c], oTd=_blockT(o2T[b][:, sl]), **f2w))
    res = run(nc, maps)
    x4 = [np.asarray(res.results[c]["x2o"]) for c in range(8)]

    nc = _get("A3", build_A3)
    maps = []
    for c in range(8):
        b, p = c // 2, c % 2
        xb = np.concatenate([x4[2 * b], x4[2 * b + 1]], axis=0)
        maps.append(dict(xb=AC(xb), wq=hslice(I["L3_attn_Wq"], p),
                         wk=hslice(I["L3_attn_Wk"], p),
                         wv=hslice(I["L3_attn_Wv"], p)))
    res = run(nc, maps)
    o3T = np.zeros((4, D, S), np.float32)
    for c in range(8):
        b, p = c // 2, c % 2
        o3T[b][p * 512:(p + 1) * 512] = np.asarray(res.results[c]["oT"])

    nc = _get("F3", build_F3)
    f3w = dict(
        wod=AC(I["L3_attn_Wo"]),
        wrd=AC(I["L3_ffn_Wr"]),
        brd=AC(I["L3_ffn_br"].reshape(1, 8)),
        w1d=AC(I["L3_ffn_W1"].reshape(8 * 1024, 2048)),
        b1d=AC(I["L3_ffn_b1"].reshape(8, 16, 128).transpose(2, 0, 1)
               .reshape(128, 128)),
        w2d=AC(I["L3_ffn_W2"].reshape(8 * 2048, 1024)),
        b2d=AC(I["L3_ffn_b2"]))
    maps = []
    for c in range(8):
        b, p = c // 2, c % 2
        sl = slice(p * TPC, (p + 1) * TPC)
        maps.append(dict(xd=x4[c], oTd=_blockT(o3T[b][:, sl]), **f3w))
    res = run(nc, maps)
    out = np.zeros((4, S, D), np.float32)
    for c in range(8):
        b, p = c // 2, c % 2
        out[b][p * TPC:(p + 1) * TPC] = np.asarray(res.results[c]["x2o"])
    return out
